# version 12
# baseline (speedup 1.0000x reference)
import numpy as np
from collections import defaultdict
from contextlib import ExitStack

import concourse.bass as bass
import concourse.mybir as mybir
from concourse.bass_utils import run_bass_kernel_spmd

F32 = mybir.dt.float32
BF16 = mybir.dt.bfloat16
AF = mybir.ActivationFunctionType
OP = mybir.AluOpType

B, S, D = 4, 8192, 1024
INNER = 2048
NPT = INNER // 128
NDC = D // 128
T = 4096
TILE = 512
NT = T // TILE
NCH = TILE // 128
EPS = 1e-8
LNEPS = 1e-5
GEPS = 0.5 + 1e-8


def build_nc(n_tiles=NT, dbg=False):
    nt = n_tiles
    t_tok = nt * TILE
    nc = bass.Bass("TRN2", target_bir_lowering=False,
                   detect_race_conditions=False)

    EI = dict(kind="ExternalInput")
    XS = nc.dram_tensor("XS", [t_tok, D], F32, **EI).ap()
    XH = nc.dram_tensor("XH", [3, D], F32, **EI).ap()
    WUT = nc.dram_tensor("WUT", [D, 2 * INNER], F32, **EI).ap()
    WDT = nc.dram_tensor("WDT", [INNER, D], F32, **EI).ap()
    BDF_D = nc.dram_tensor("BDF", [128, NPT * 128], F32, **EI).ap()
    BDI_D = nc.dram_tensor("BDI", [128, NPT * 128], F32, **EI).ap()
    BDH_D = nc.dram_tensor("BDH", [128, NPT * 128], F32, **EI).ap()
    CWD = nc.dram_tensor("CWD", [128, 4 * NPT], F32, **EI).ap()
    BUPM_D = nc.dram_tensor("BUPM", [128, NPT], F32, **EI).ap()
    BUPZ_D = nc.dram_tensor("BUPZ", [128, NPT], F32, **EI).ap()
    CB_D = nc.dram_tensor("CBD", [128, NPT], F32, **EI).ap()
    SKIP_D = nc.dram_tensor("SKIPD", [128, NPT], F32, **EI).ap()
    HALOM_D = nc.dram_tensor("HALOM", [128, 1], F32, **EI).ap()
    PAIRM_D = nc.dram_tensor("PAIRM", [128, 16], F32, **EI).ap()
    EYE_D = nc.dram_tensor("EYE", [128, 128], F32, **EI).ap()
    BDOWN_D = nc.dram_tensor("BDOWND", [1, D], F32, **EI).ap()
    BR_D = nc.dram_tensor("BRD", [1, 512], F32, **EI).ap()
    BI_D = nc.dram_tensor("BID", [1, 512], F32, **EI).ap()
    G1R_D = nc.dram_tensor("G1R", [1, 4096], F32, **EI).ap()
    G2R_D = nc.dram_tensor("G2R", [1, 1024], F32, **EI).ap()
    G1I_D = nc.dram_tensor("G1I", [1, 4096], F32, **EI).ap()
    G2I_D = nc.dram_tensor("G2I", [1, 1024], F32, **EI).ap()
    CONS_D = nc.dram_tensor("CONS", [128, 4], F32, **EI).ap()
    BUPMR_D = nc.dram_tensor("BUPMR", [1, INNER], F32, **EI).ap()
    CBR_D = nc.dram_tensor("CBR", [1, INNER], F32, **EI).ap()

    OUT = nc.dram_tensor("OUT", [t_tok, D], F32, kind="ExternalOutput").ap()

    XN = nc.dram_tensor("XNs", [t_tok, D], BF16).ap()
    XN2 = nc.dram_tensor("XN2s", [t_tok, D], BF16).ap()
    AG = nc.dram_tensor("AGs", [NPT, 128, t_tok], BF16).ap()
    ODR = nc.dram_tensor("ODRs", [1024, 256], BF16).ap()
    ODI = nc.dram_tensor("ODIs", [1024, 256], BF16).ap()
    CCIN = nc.dram_tensor("CCINs", [128, 32], F32).ap()
    CCOUT = nc.dram_tensor("CCOUTs", [8 * 128, 32], F32).ap()

    st = ExitStack()
    st1 = ExitStack()

    def sb(stk, name, shape, dt):
        return stk.enter_context(nc.sbuf_tensor(name, shape, dt))

    def psm(stk, name, shape, dt=F32):
        return stk.enter_context(nc.psum_tensor(name, shape, dt))

    WUT_SB = sb(st, "wut", [128, NDC * INNER], BF16)
    EYE = sb(st, "eye", [128, 128], F32)
    EYEB = sb(st, "eyeb", [128, 128], BF16)
    BUPM = sb(st, "bupm", [128, NPT], F32)
    BUPZ = sb(st, "bupz", [128, NPT], F32)
    CBS = sb(st, "cbs", [128, NPT], F32)
    SKIPS = sb(st, "skips", [128, NPT], F32)
    HALOM = sb(st, "halom", [128, 1], F32)
    PAIRM = sb(st, "pairm", [128, 16], F32)
    ONESB = sb(st, "onesb", [1, 128], BF16)
    ONES512 = sb(st, "ones512", [1, 512], BF16)
    BUPMR = sb(st, "bupmr", [1, INNER], BF16)
    CBR = sb(st, "cbr", [1, INNER], BF16)
    BDOWNB = sb(st, "bdownb", [1, D], BF16)
    BRB = sb(st, "brb", [1, 512], BF16)
    BIB = sb(st, "bib", [1, 512], BF16)
    RSTATE = sb(st, "rstate", [128, NPT], F32)
    PFSUM = sb(st, "pfsum", [128, NPT], F32)
    LNFACC = sb(st, "lnfacc", [128, NPT], F32)
    CCB = sb(st, "ccb", [128, 32], F32)
    GATH = sb(st, "gath", [128, 256], F32)
    SEL1 = sb(st, "sel1", [128, 32], F32)
    SEL2 = sb(st, "sel2", [128, 32], F32)
    HL = sb(st, "hl", [128, NPT], F32)
    TMP16 = sb(st, "tmp16", [128, NPT], F32)
    STATB = sb(st, "statb", [128, 12], F32)
    MV = sb(st, "mv", [128, 2], F32)
    SD1 = sb(st, "sd1", [128, 1], F32)
    RSTD = sb(st, "rstd", [128, 1], F32)
    NMR = sb(st, "nmr", [128, 1], F32)
    CONS = sb(st, "cons", [128, 4], F32)
    SCRQ = sb(st, "scrq", [128, 1024], F32)
    SMB = sb(st, "smb", [128, 1], F32)
    SSQ = sb(st, "ssq", [128, 1], F32)
    MEANB = sb(st, "meanb", [128, 1], F32)
    MSQ = sb(st, "msq", [128, 1], F32)
    VARB = sb(st, "varb", [128, 1], F32)
    JUNKV = sb(st, "junkv", [128, 16], F32)
    JUNKA = sb(st, "junka", [128, 16], F32)
    XNT = [sb(st, "xnt0", [128, NDC * TILE], BF16),
           sb(st, "xnt1", [128, NDC * TILE], BF16)]

    BDF_SB = sb(st1, "bdf", [128, NPT * 128], BF16)
    BDI_SB = sb(st1, "bdi", [128, NPT * 128], BF16)
    BDH_SB = sb(st1, "bdh", [128, NPT * 128], BF16)
    DIAG = sb(st1, "diag", [128, 4 * NPT * 128], BF16)
    CWS = sb(st1, "cws", [128, 4 * NPT], F32)
    BUPMK = sb(st1, "bupmk", [128, NPT], F32)
    XMIN = sb(st1, "xmin", [128, NPT * 520], BF16)
    XB = [sb(st1, "xb0", [128, 2 * D], F32), sb(st1, "xb1", [128, 2 * D], F32)]
    XNB = [sb(st1, "xnb0", [128, D], BF16), sb(st1, "xnb1", [128, D], BF16)]
    AGT = sb(st1, "agt", [128, NPT * TILE], BF16)
    FIH = [sb(st1, "fih0", [128, 3 * TILE], F32),
           sb(st1, "fih1", [128, 3 * TILE], F32)]
    UB = [sb(st1, "ub0", [128, TILE], F32), sb(st1, "ub1", [128, TILE], F32)]
    WB = [sb(st1, "wb0", [128, TILE], F32), sb(st1, "wb1", [128, TILE], F32)]
    LNH = sb(st1, "lnh", [128, TILE], F32)
    GB = [sb(st1, "gb0", [128, TILE], F32), sb(st1, "gb1", [128, TILE], F32)]
    GHB = [sb(st1, "ghb0", [128, TILE], F32), sb(st1, "ghb1", [128, TILE], F32)]
    FB = [sb(st1, "fb0", [128, TILE], F32), sb(st1, "fb1", [128, TILE], F32)]
    TB = sb(st1, "tb", [128, TILE], F32)
    RSOUT = sb(st1, "rsout", [128, TILE], F32)

    P_MM = psm(st1, "pmm", [128, 2 * TILE])
    P_CV = psm(st1, "pcv", [128, 2 * TILE])


    G1RS = sb(st1, "g1rs", [1, 4096], BF16)
    G2RS = sb(st1, "g2rs", [1, 1024], BF16)
    G1IS = sb(st1, "g1is", [1, 4096], BF16)
    G2IS = sb(st1, "g2is", [1, 1024], BF16)
    OSB = sb(st1, "osb", [128, 256], BF16)
    XHST = sb(st1, "xhst", [3, 12], F32)
    XHMV = sb(st1, "xhmv", [3, 2], F32)
    XHSD = sb(st1, "xhsd", [3, 1], F32)
    XHRS = sb(st1, "xhrs", [3, 1], F32)
    XHNM = sb(st1, "xhnm", [3, 1], F32)
    XHT = sb(st1, "xht", [128, NDC * 3], BF16)
    XHS = XB[0][0:3, 0:D]
    XHN = XB[0][0:3, D:2 * D]
    P_FI = psm(st1, "pfi1", [128, 2 * TILE])
    P_H = psm(st1, "ph1", [128, 2 * TILE])

    sem_names = ["SX", "SXNT", "SXNW", "SAG", "SW", "SMI", "SCC", "SACTLD",
                 "SXN2W", "SXN2T", "SOUT", "SPE", "SACT", "SVEC", "SGP"]
    SEM = {n: st.enter_context(nc.semaphore(n)) for n in sem_names}

    cnt = defaultdict(int)
    mark = {}

    def inc(s, n, tag=None):
        if s in ("SVEC", "SACT") and last_nops:
            n = last_nops.pop()
        cnt[s] += n
        if tag is not None:
            mark[(s, tag)] = cnt[s]
        return cnt[s]

    def at(s, tag):
        return mark[(s, tag)]

    plans = {"ph1": [], "ph2": []}
    cur = ["ph1"]

    _WRAP = ("activation", "tensor_scalar", "tensor_tensor",
             "scalar_tensor_tensor", "tensor_tensor_scan", "tensor_copy",
             "tensor_reduce", "reciprocal", "memset", "memzero", "select",
             "copy", "add", "mul", "sign", "sqrt", "square", "reduce_max",
             "reduce_sum", "bn_stats", "bn_aggr")

    class _Counter:
        def __init__(self):
            self.n = 0

        def __getattr__(self, name):
            if name in _WRAP:
                def f(*a, **k):
                    self.n += 1
                    return self
                return f
            def g(*a, **k):
                return self
            return g

        def then_inc(self, *a, **k):
            return self

    class _Proxy:
        def __init__(self, e, sem, base):
            self._e = e
            self._sem = sem
            self._n = base

        def __getattr__(self, name):
            e = self._e
            if name in _WRAP:
                def f(*a, **k):
                    e.wait_ge(self._sem, self._n)
                    r = getattr(e, name)(*a, **k)
                    r.then_inc(self._sem, 1)
                    self._n += 1
                    return r
                return f
            return getattr(e, name)

    _selfsem = {"vector": "SVEC", "scalar": "SACT"}
    last_nops = []

    def em(eng, waits, fn, tinc=None):
        w = [(SEM[s], v) for (s, v) in waits]
        if eng in _selfsem:
            c = _Counter()
            fn(c)
            last_nops.append(c.n)
            ssem = SEM[_selfsem[eng]]
            base = cnt[_selfsem[eng]]

            if tinc is not None and tinc[0] == _selfsem[eng]:
                tinc = None
            ti = (SEM[tinc[0]], tinc[1]) if tinc else None

            def run(e, w=w, fn=fn, ssem=ssem, base=base, nops=c.n, ti=ti):
                for sh, v in w:
                    e.wait_ge(sh, v)
                p = _Proxy(e, ssem, base)
                r = fn(p)
                if ti is not None:
                    r.then_inc(ti[0], ti[1])
                assert p._n == base + nops, (
                    f"op count mismatch: {p._n} != {base}+{nops}")
            plans[cur[0]].append((eng, run))
            return

        ti = (SEM[tinc[0]], tinc[1]) if tinc else None

        def run(e, w=w, fn=fn, ti=ti):
            for sh, v in w:
                e.wait_ge(sh, v)
            r = fn(e)
            if ti is not None:
                r.then_inc(ti[0], ti[1])
        plans[cur[0]].append((eng, run))

    def _inc_orig(s, n, tag=None):
        cnt[s] += n
        if tag is not None:
            mark[(s, tag)] = cnt[s]
        return cnt[s]

    def dma(eng, waits, out_ap, in_ap, s, tag=None, transpose=False):
        inc(s, 16, tag)
        em(eng, waits,
           (lambda e, o=out_ap, i=in_ap, tr=transpose:
            e.dma_start(out=o, in_=i, transpose=tr)),
           tinc=(s, 16))

    if dbg:
        DXHT = nc.dram_tensor("DXHT", [128, NDC * 3], BF16,
                              kind="ExternalOutput").ap()
        DXHN = nc.dram_tensor("DXHN", [3, D], F32, kind="ExternalOutput").ap()
        DXHRS = nc.dram_tensor("DXHRS", [3, 2], F32,
                               kind="ExternalOutput").ap()
        DXMIN = nc.dram_tensor("DXMIN", [128, NPT * 520], BF16,
                               kind="ExternalOutput").ap()


    RD = 1.0 / float(D)

    def stats_ops(e, src_ap, np_):
        e.tensor_scalar(out=SCRQ[0:np_, :], in0=src_ap, scalar1=0.0,
                        scalar2=0.0, op0=OP.add, op1=OP.add,
                        accum_out=SMB[0:np_, 0:1])
        e.scalar_tensor_tensor(out=SCRQ[0:np_, :], in0=src_ap, scalar=1.0,
                               in1=src_ap, op0=OP.mult, op1=OP.mult,
                               accum_out=SSQ[0:np_, 0:1])
        e.tensor_scalar(out=MEANB[0:np_, 0:1], in0=SMB[0:np_, 0:1],
                        scalar1=RD, scalar2=None, op0=OP.mult)
        e.tensor_tensor(out=MSQ[0:np_, 0:1], in0=MEANB[0:np_, 0:1],
                        in1=MEANB[0:np_, 0:1], op=OP.mult)
        e.scalar_tensor_tensor(out=VARB[0:np_, 0:1],
                               in0=SSQ[0:np_, 0:1], scalar=RD,
                               in1=MSQ[0:np_, 0:1], op0=OP.mult,
                               op1=OP.subtract)
        return e.tensor_tensor(out=JUNKV[0:np_, 0:1], in0=VARB[0:np_, 0:1],
                               in1=VARB[0:np_, 0:1], op=OP.add)
    for dc in range(NDC):
        dma("gpsimd", [], WUT_SB[:, dc * INNER:(dc + 1) * INNER],
            WUT[dc * 128:(dc + 1) * 128, 0:INNER], "SW")
    for bd_d, bd_s in ((BDF_D, BDF_SB), (BDI_D, BDI_SB), (BDH_D, BDH_SB)):
        dma("gpsimd", [], bd_s[:, :], bd_d, "SW")
    dma("gpsimd", [], CWS[:, :], CWD, "SW")
    dma("gpsimd", [], EYE[:, :], EYE_D, "SW")
    dma("gpsimd", [], BUPM[:, :], BUPM_D, "SW")
    dma("gpsimd", [], BUPZ[:, :], BUPZ_D, "SW")
    dma("gpsimd", [], CBS[:, :], CB_D, "SW")
    dma("gpsimd", [], SKIPS[:, :], SKIP_D, "SW")
    dma("gpsimd", [], HALOM[:, :], HALOM_D, "SW")
    dma("gpsimd", [], PAIRM[:, :], PAIRM_D, "SW")
    dma("gpsimd", [], BDOWNB[:, :], BDOWN_D, "SW")
    dma("gpsimd", [], BRB[:, :], BR_D, "SW")
    dma("gpsimd", [], BIB[:, :], BI_D, "SW")
    dma("gpsimd", [], G1RS[:, :], G1R_D, "SW")
    dma("gpsimd", [], G2RS[:, :], G2R_D, "SW")
    dma("gpsimd", [], G1IS[:, :], G1I_D, "SW")
    dma("gpsimd", [], G2IS[:, :], G2I_D, "SW")
    dma("gpsimd", [], CONS[:, :], CONS_D, "SW")
    dma("gpsimd", [], BUPMR[:, :], BUPMR_D, "SW")
    dma("gpsimd", [], CBR[:, :], CBR_D, "SW")
    w_all = cnt["SW"]

    dma("sync", [], XHS, XH, "SX", "xhalo")

    def v_init(e):
        e.memset(RSTATE[:, :], 0.0)
        e.memset(PFSUM[:, :], 0.0)
        e.memset(ONESB[:, :], 1.0)
        e.memset(ONES512[:, :], 1.0)
        return e.tensor_copy(EYEB[:, :], EYE[:, :])
    em("vector", [("SW", w_all)], v_init, tinc=("SVEC", 1))
    inc("SVEC", 1, "vinit")

    def v_diag(e):
        r = None
        for pt in range(NPT):
            for k in range(4):
                j = pt * 4 + k
                r = e.tensor_scalar(out=DIAG[:, j * 128:(j + 1) * 128],
                                    in0=EYE[:, :], scalar1=CWS[:, j:j + 1],
                                    scalar2=None, op0=OP.mult)
        return r
    em("vector", [], v_diag, tinc=("SVEC", 1))
    inc("SVEC", 1, "vdiag")

    def v_bupmk(e):
        return e.tensor_scalar(out=BUPMK[:, :], in0=BUPM[:, :],
                               scalar1=HALOM[:, 0:1], scalar2=None, op0=OP.mult)
    em("vector", [], v_bupmk, tinc=("SVEC", 1))
    inc("SVEC", 1, "vbupmk")

    def v_halostats(e):
        return stats_ops(e, XHS, 3)
    em("vector", [("SX", at("SX", "xhalo"))], v_halostats, tinc=("SVEC", 1))
    inc("SVEC", 1, "vhstat")

    em("scalar", [("SVEC", at("SVEC", "vhstat")), ("SW", w_all)],
       lambda e: (e.activation(out=XHSD[0:3, 0:1], in_=VARB[0:3, 0:1],
                               func=AF.Ln, bias=CONS[0:3, 0:1]),
                  e.activation(out=XHRS[0:3, 0:1], in_=XHSD[0:3, 0:1],
                               func=AF.Exp, scale=-0.5),
                  e.activation(out=JUNKA[0:3, 0:1], in_=XHRS[0:3, 0:1],
                               func=AF.Copy))[2],
       tinc=("SACT", 1))
    inc("SACT", 1, "ahsqrt")

    def v_hrs(e):
        e.scalar_tensor_tensor(out=XHNM[0:3, 0:1], in0=MEANB[0:3, 0:1],
                               scalar=-1.0, in1=XHRS[0:3, 0:1],
                               op0=OP.mult, op1=OP.mult)
        return e.tensor_tensor(out=JUNKV[0:3, 1:2], in0=XHRS[0:3, 0:1],
                               in1=XHNM[0:3, 0:1], op=OP.add)
    em("vector", [("SACT", at("SACT", "ahsqrt"))], v_hrs, tinc=("SVEC", 1))
    inc("SVEC", 1, "vhrs")

    em("scalar", [("SVEC", at("SVEC", "vhrs"))],
       lambda e: e.activation(out=XHN, in_=XHS, func=AF.Identity,
                              scale=XHRS[0:3, 0:1], bias=XHNM[0:3, 0:1]),
       tinc=("SACT", 1))
    inc("SACT", 1, "ahnorm")

    for dc in range(NDC):
        em("tensor",
           [("SACT", at("SACT", "ahnorm")), ("SVEC", at("SVEC", "vinit"))] +
           ([("SACT", at("SACT", ("ahxt", dc - 1)))] if dc > 0 else []),
           (lambda e, dc=dc: e.transpose(out=P_CV[:, 0:3],
                                         in_=XHN[:, dc * 128:(dc + 1) * 128],
                                         identity=EYE[0:3, 0:3])),
           tinc=("SPE", 1))
        inc("SPE", 1, ("hxt", dc))
        em("scalar", [("SPE", at("SPE", ("hxt", dc)))],
           (lambda e, dc=dc: e.activation(out=XHT[:, dc * 3:(dc + 1) * 3],
                                          in_=P_CV[:, 0:3], func=AF.Copy)),
           tinc=("SACT", 1))
        inc("SACT", 1, ("ahxt", dc))

    for ep in range(NPT):
        def pe_halo(e, ep=ep):
            r = None
            for dc in range(NDC):
                r = e.matmul(out=P_MM[:, (ep % 2) * TILE:(ep % 2) * TILE + 3],
                             lhsT=WUT_SB[:, dc * INNER + ep * 128:
                                         dc * INNER + (ep + 1) * 128],
                             rhs=XHT[:, dc * 3:(dc + 1) * 3],
                             start=(dc == 0), stop=(dc == NDC - 1))
            return r
        waits = [("SACT", at("SACT", ("ahxt", NDC - 1))), ("SW", w_all)]
        if ep >= 2:
            waits.append(("SACT", at("SACT", ("ahev", ep - 2))))
        em("tensor", waits, pe_halo, tinc=("SPE", 1))
        inc("SPE", 1, ("hmm", ep))
        em("scalar", [("SPE", at("SPE", ("hmm", ep))),
                      ("SVEC", at("SVEC", "vbupmk"))],
           (lambda e, ep=ep: e.activation(out=XMIN[:, ep * 520: ep * 520 + 3],
                                          in_=P_MM[:, (ep % 2) * TILE:
                                                   (ep % 2) * TILE + 3],
                                          func=AF.Identity,
                                          scale=HALOM[:, 0:1],
                                          bias=BUPMK[:, ep:ep + 1])),
           tinc=("SACT", 1))
        inc("SACT", 1, ("ahev", ep))

    if dbg:
        dma("scalar", [("SACT", at("SACT", ("ahev", NPT - 1)))],
            DXHT, XHT[:, :], "SMI")
        dma("scalar", [("SACT", at("SACT", "ahnorm"))], DXHN, XHN, "SMI")
        dma("scalar", [("SVEC", at("SVEC", "vhrs"))], DXHRS[0:3, 0:2],
            XHMV[0:3, 0:2], "SMI")

    prev_tag = None
    for oi, (g1, g2, od) in enumerate(((G1RS, G2RS, ODR), (G1IS, G2IS, ODI))):
        for q in range(8):
            def pe_outer(e, g1=g1, g2=g2, q=q):
                r = None
                for c in range(4):
                    r = e.matmul(out=P_CV[:, 0:256],
                                 lhsT=g1[0:1, c * 1024 + q * 128:
                                         c * 1024 + (q + 1) * 128],
                                 rhs=g2[0:1, c * 256:(c + 1) * 256],
                                 start=(c == 0), stop=(c == 3))
                return r
            waits = [("SW", w_all), ("SACT", at("SACT", ("ahev", NPT - 1)))]
            if prev_tag is not None:
                waits.append(("SACT", at("SACT", ("wbev", prev_tag))))
            em("tensor", waits, pe_outer, tinc=("SPE", 1))
            inc("SPE", 1, ("wbmm", (oi, q)))
            waits = [("SPE", at("SPE", ("wbmm", (oi, q))))]
            if prev_tag is not None:
                waits.append(("SMI", at("SMI", ("wbod", prev_tag))))
            em("scalar", waits,
               (lambda e: (e.activation(out=OSB[:, :], in_=P_CV[:, 0:256],
                                        func=AF.Copy),
                           e.activation(out=JUNKA[:, 0:1], in_=OSB[:, 255:256],
                                        func=AF.Copy))[1]),
               tinc=("SACT", 1))
            inc("SACT", 1, ("wbev", (oi, q)))
            dma("scalar", [("SACT", at("SACT", ("wbev", (oi, q))))],
                od[q * 128:(q + 1) * 128, :], OSB[:, :],
                "SMI", ("wbod", (oi, q)))
            prev_tag = (oi, q)
    wb_last = prev_tag


    xs4 = XS.rearrange("(t c p) d -> t c p d", c=NCH, p=128)
    xn4 = XN.rearrange("(t c p) d -> t c p d", c=NCH, p=128)

    def emit_xload(t, hf):
        waits = []
        if t == 0 and hf == 0:
            waits += [("SPE", at("SPE", ("wbmm", wb_last))),
                      ("SMI", at("SMI", ("wbod", wb_last))),
                      ("SACT", at("SACT", ("ahev", NPT - 1)))]
        if t >= 1:
            waits.append(("SACT", at("SACT", ("aln", t - 1, 2 * hf + 1))))
        dma("sync", waits,
            XB[hf][:, :].rearrange("p (c d) -> p c d", c=2),
            xs4[t, 2 * hf:2 * hf + 2].rearrange("c p d -> p c d"),
            "SX", ("x", t, hf))

    def emit_ln(t, c):
        hf, ci = c // 2, c % 2
        def v_stats(e, hf=hf, ci=ci):
            return stats_ops(e, XB[hf][:, ci * D:(ci + 1) * D], 128)
        w = [("SX", at("SX", ("x", t, hf)))]
        if (t, c) != (0, 0):
            pc = (t, c - 1) if c > 0 else (t - 1, NCH - 1)
            w.append(("SACT", at("SACT", ("aln", pc[0], pc[1]))))
        em("vector", w, v_stats, tinc=("SVEC", 1))
        inc("SVEC", 1, ("vstat", t, c))

        em("scalar", [("SVEC", at("SVEC", ("vstat", t, c)))],
           (lambda e: (e.activation(out=SD1[:, 0:1], in_=VARB[:, 0:1],
                                    func=AF.Ln, bias=CONS[:, 0:1]),
                       e.activation(out=RSTD[:, 0:1], in_=SD1[:, 0:1],
                                    func=AF.Exp, scale=-0.5),
                       e.activation(out=JUNKA[:, 0:1], in_=RSTD[:, 0:1],
                                    func=AF.Copy))[2]),
           tinc=("SACT", 1))
        inc("SACT", 1, ("asq", t, c))

        def v_rs(e):
            e.scalar_tensor_tensor(out=NMR[:, 0:1], in0=MEANB[:, 0:1],
                                   scalar=-1.0, in1=RSTD[:, 0:1],
                                   op0=OP.mult, op1=OP.mult)
            return e.tensor_tensor(out=JUNKV[:, 1:2], in0=RSTD[:, 0:1],
                                   in1=NMR[:, 0:1], op=OP.add)
        em("vector", [("SACT", at("SACT", ("asq", t, c)))], v_rs,
           tinc=("SVEC", 1))
        inc("SVEC", 1, ("vrs", t, c))

        w = [("SVEC", at("SVEC", ("vrs", t, c)))]
        if (t, c) >= (0, 2):
            pc = (t, c - 2) if c >= 2 else (t - 1, NCH - 2 + c)
            w.append(("SXNW", at("SXNW", ("xnw", pc[0], pc[1]))))
        em("scalar", w,
           (lambda e, c=c, hf=hf, ci=ci: (
               e.activation(out=XNB[c % 2][:, :],
                            in_=XB[hf][:, ci * D:(ci + 1) * D],
                            func=AF.Identity, scale=RSTD[:, 0:1],
                            bias=NMR[:, 0:1]),
               e.activation(out=JUNKA[:, 0:1], in_=XNB[c % 2][:, D - 1:D],
                            func=AF.Copy))[1]),
           tinc=("SACT", 1))
        inc("SACT", 1, ("aln", t, c))
        dma("scalar", [("SACT", at("SACT", ("aln", t, c)))],
            xn4[t, c], XNB[c % 2][:, :], "SXNW", ("xnw", t, c))

    def emit_xnt(t):
        bi = t % 2
        for dc in range(NDC):
            w = [("SXNW", at("SXNW", ("xnw", t, NCH - 1)))]
            if t >= 2:
                w.append(("SPE", at("SPE", ("mm", t - 2, NPT - 1))))
            dma("sync", w, XNT[bi][:, dc * TILE:(dc + 1) * TILE],
                XN[t * TILE:(t + 1) * TILE, dc * 128:(dc + 1) * 128],
                "SXNT", ("xnt", t, dc), transpose=True)

        def v_settle(e, bi=bi):
            r = None
            for dc in range(NDC):
                r = e.tensor_copy(JUNKA[:, 2:3],
                                  XNT[bi][:, (dc + 1) * TILE - 1:
                                          (dc + 1) * TILE])
            return r
        em("vector", [("SXNT", at("SXNT", ("xnt", t, NDC - 1)))], v_settle,
           tinc=("SVEC", 1))
        inc("SVEC", 1, ("vxs", t))

    XMINW = XMIN[:, :].rearrange("p (w c) -> p w c", c=520)

    def emit_mmaev(t, ep):
        bi = t % 2
        def pe_mm(e, ep=ep, bi=bi):
            sl = P_MM[:, (ep % 2) * TILE:(ep % 2 + 1) * TILE]
            for dc in range(NDC):
                e.matmul(out=sl,
                         lhsT=WUT_SB[:, dc * INNER + ep * 128:
                                     dc * INNER + (ep + 1) * 128],
                         rhs=XNT[bi][:, dc * TILE:(dc + 1) * TILE],
                         start=(dc == 0), stop=False)
            return e.matmul(out=sl,
                            lhsT=BUPMR[0:1, ep * 128:(ep + 1) * 128],
                            rhs=ONES512[0:1, :], start=False, stop=True)
        w = [("SVEC", at("SVEC", ("vxs", t)))]
        if t == 0 and ep < 2:
            w.append(("SACT", at("SACT", ("ahev", NPT - 1))))
        if (t, ep) >= (0, 2):
            pe_, ep_ = (t, ep - 2) if ep >= 2 else (t - 1, NPT - 2 + ep)
            w.append(("SACT", at("SACT", ("aev", pe_, ep_ - ep_ % 2))))
        em("tensor", w, pe_mm, tinc=("SPE", 1))
        inc("SPE", 1, ("mm", t, ep))

        if ep % 2 == 1:
            w = [("SPE", at("SPE", ("mm", t, ep)))]
            if t > 0:
                w.append(("SVEC", at("SVEC", ("vhc", t - 1, ep - 1))))
                w.append(("SVEC", at("SVEC", ("vhc", t - 1, ep))))
            em("scalar", w,
               (lambda e, ep=ep: (e.activation(
                   out=XMINW[:, ep - 1:ep + 1, 3:515],
                   in_=P_MM[:, :], func=AF.Identity),
                   e.activation(out=JUNKA[:, 0:1],
                                in_=XMIN[:, ep * 520 + 514: ep * 520 + 515],
                                func=AF.Copy))[1]),
               tinc=("SACT", 1))
            inc("SACT", 1, ("aev", t, ep - 1))
            inc("SACT", 0, ("aev", t, ep))

    def emit_cvsilu(t, pt):
        def pe_cv(e, pt=pt):
            sl = P_CV[:, (pt % 2) * TILE:(pt % 2 + 1) * TILE]
            for k in range(4):
                j = pt * 4 + k
                e.matmul(out=sl,
                         lhsT=DIAG[:, j * 128:(j + 1) * 128],
                         rhs=XMIN[:, pt * 520 + k: pt * 520 + k + 512],
                         start=(k == 0), stop=False)
            return e.matmul(out=sl, lhsT=CBR[0:1, pt * 128:(pt + 1) * 128],
                            rhs=ONES512[0:1, :], start=False, stop=True)
        w = [("SACT", at("SACT", ("aev", t, pt - pt % 2))),
             ("SVEC", at("SVEC", "vdiag"))]
        if (t, pt) >= (0, 2):
            pe_, pt_ = (t, pt - 2) if pt >= 2 else (t - 1, NPT - 2 + pt)
            w.append(("SACT", at("SACT", ("asilu", pe_, pt_ - pt_ % 2))))
        em("tensor", w, pe_cv, tinc=("SPE", 1))
        inc("SPE", 1, ("cv", t, pt))

        if pt % 2 == 1:
            w = [("SPE", at("SPE", ("cv", t, pt)))]
            if t > 0:
                w.append(("SPE", at("SPE", ("gt", t - 1, pt - 1))))
                w.append(("SPE", at("SPE", ("gt", t - 1, pt))))
                w.append(("SAG", at("SAG", ("ag", t - 1))))
            em("scalar", w,
               (lambda e, pt=pt: (e.activation(
                   out=AGT[:, (pt - 1) * TILE:(pt + 1) * TILE],
                   in_=P_CV[:, :], func=AF.Silu),
                   e.activation(out=JUNKA[:, 0:1],
                                in_=AGT[:, (pt + 1) * TILE - 1:(pt + 1) * TILE],
                                func=AF.Copy))[1]),
               tinc=("SACT", 1))
            inc("SACT", 1, ("asilu", t, pt - 1))
            inc("SACT", 0, ("asilu", t, pt))

    def emit_vhc(t, pt):
        def v_hc(e, pt=pt):
            return e.tensor_copy(XMIN[:, pt * 520: pt * 520 + 3],
                                 XMIN[:, pt * 520 + 512: pt * 520 + 515])
        em("vector", [("SPE", at("SPE", ("cv", t, pt)))], v_hc,
           tinc=("SVEC", 1))
        inc("SVEC", 1, ("vhc", t, pt))

    def emit_spill(t):
        dma("scalar", [("SACT", at("SACT", ("asilu", t, NPT - 1)))],
            AG[:, :, t * TILE:(t + 1) * TILE].rearrange("q p s -> p q s"),
            AGT[:, :].rearrange("p (q s) -> p q s", q=NPT), "SAG", ("ag", t))

    def prevslot(t, pt, d):
        k = t * NPT + pt - d
        return (k // NPT, k % NPT) if k >= 0 else None

    def emit_gt(t, pt):
        pb = pt % 2
        def pe_gt(e, pt=pt, pb=pb):
            e.matmul(out=P_FI[:, 0:512],
                     lhsT=BDF_SB[:, pt * 128:(pt + 1) * 128],
                     rhs=AGT[:, pt * TILE:(pt + 1) * TILE],
                     start=True, stop=True)
            e.matmul(out=P_FI[:, 512:1024],
                     lhsT=BDI_SB[:, pt * 128:(pt + 1) * 128],
                     rhs=AGT[:, pt * TILE:(pt + 1) * TILE],
                     start=True, stop=True)
            return e.matmul(out=P_H[:, pb * 512:(pb + 1) * 512],
                            lhsT=BDH_SB[:, pt * 128:(pt + 1) * 128],
                            rhs=AGT[:, pt * TILE:(pt + 1) * TILE],
                            start=True, stop=True)
        w = [("SACT", at("SACT", ("asilu", t, pt - pt % 2)))]
        p1 = prevslot(t, pt, 1)
        p2 = prevslot(t, pt, 2)
        if p1:
            w.append(("SACT", at("SACT", ("a1",) + p1)))
        if p2:
            w.append(("SVEC", at("SVEC", ("v1",) + p2)))
        em("tensor", w, pe_gt, tinc=("SPE", 1))
        inc("SPE", 1, ("gt", t, pt))

    def emit_a1(t, pt):
        pb = pt % 2
        def a_1(e, pb=pb):
            e.activation(out=FIH[pb][:, 0:1024], in_=P_FI[:, :], func=AF.Exp,
                         scale=-1.0)
            e.activation(out=FIH[pb][:, 1024:1536],
                         in_=P_H[:, pb * 512:(pb + 1) * 512], func=AF.Exp,
                         scale=-1.0)
            e.activation(out=LNH[:, :], in_=FIH[pb][:, 1024:1536],
                         func=AF.Ln, bias=CONS[:, 1:2])
            e.activation(out=GB[pb][:, :], in_=LNH[:, :], func=AF.Exp,
                         scale=-1.0)
            return e.activation(out=JUNKA[:, 0:2],
                                in_=FIH[pb][:, 1022:1024], func=AF.Copy)
        w = [("SPE", at("SPE", ("gt", t, pt)))]
        p2 = prevslot(t, pt, 2)
        if p2:
            w.append(("SVEC", at("SVEC", ("v2",) + p2)))
        em("scalar", w, a_1, tinc=("SACT", 1))
        inc("SACT", 1, ("a1", t, pt))

    def emit_a3(t, pt):
        def a_3(e, pt=pt):
            return e.activation(out=LNH[:, :], in_=FB[pt % 2][:, :], func=AF.Ln,
                                accum_out=LNFACC[:, pt:pt + 1])
        em("scalar", [("SVEC", at("SVEC", ("v2", t, pt)))], a_3,
           tinc=("SACT", 1))
        inc("SACT", 1, ("a3", t, pt))

    def emit_a2(t, pt):
        pb = pt % 2
        def a_2(e, pb=pb):
            e.activation(out=LNH[:, :], in_=UB[pb][:, :], func=AF.Ln,
                         bias=CONS[:, 3:4])
            e.activation(out=WB[pb][:, :], in_=LNH[:, :], func=AF.Exp,
                         scale=-1.0)
            return e.activation(out=JUNKA[:, 0:1], in_=WB[pb][:, 511:512],
                                func=AF.Copy)
        em("scalar", [("SVEC", at("SVEC", ("v1", t, pt)))], a_2,
           tinc=("SACT", 1))
        inc("SACT", 1, ("a2", t, pt))

    def emit_v1(t, pt):
        pb = pt % 2
        p3 = prevslot(t, pt, 3)
        def v_1(e, pb=pb, p3=p3):
            if p3:
                pp = p3[1]
                e.tensor_tensor(out=PFSUM[:, pp:pp + 1],
                                in0=PFSUM[:, pp:pp + 1],
                                in1=LNFACC[:, pp:pp + 1], op=OP.add)
            e.tensor_tensor(out=UB[pb][:, :], in0=FIH[pb][:, 0:512],
                            in1=FIH[pb][:, 512:1024], op=OP.add)
            e.tensor_tensor(out=JUNKV[:, 3:4], in0=UB[pb][:, 511:512],
                            in1=UB[pb][:, 511:512], op=OP.add)
            return e.scalar_tensor_tensor(out=GHB[pb][:, :],
                                          in0=P_H[:, pb * 512:(pb + 1) * 512],
                                          scalar=GEPS, in1=GB[pb][:, :],
                                          op0=OP.add, op1=OP.max)
        em("vector", [("SACT", at("SACT", ("a1", t, pt)))], v_1,
           tinc=("SVEC", 1))
        inc("SVEC", 1, ("v1", t, pt))

    def emit_v2(t, pt):
        pb = pt % 2
        def v_2(e, pt=pt, pb=pb):
            e.scalar_tensor_tensor(out=FB[pb][:, :], in0=FIH[pb][:, 512:1024],
                                   scalar=1.0, in1=WB[pb][:, :],
                                   op0=OP.add, op1=OP.mult)
            e.scalar_tensor_tensor(out=TB[:, :], in0=FIH[pb][:, 0:512],
                                   scalar=1.0, in1=WB[pb][:, :],
                                   op0=OP.add, op1=OP.mult)
            e.tensor_tensor(out=TB[:, :], in0=TB[:, :], in1=GHB[pb][:, :],
                            op=OP.mult)
            e.tensor_tensor_scan(out=RSOUT[:, :], data0=FB[pb][:, :],
                                 data1=TB[:, :],
                                 initial=RSTATE[:, pt:pt + 1],
                                 op0=OP.mult, op1=OP.add)
            e.tensor_copy(RSTATE[:, pt:pt + 1], RSOUT[:, 511:512])
            return e.tensor_tensor(out=JUNKV[:, 4:5], in0=FB[pb][:, 511:512],
                                   in1=FB[pb][:, 511:512], op=OP.add)
        em("vector", [("SACT", at("SACT", ("a2", t, pt)))], v_2,
           tinc=("SVEC", 1))
        inc("SVEC", 1, ("v2", t, pt))

    def emit_chain_slot(t, pt):
        emit_gt(t, pt)
        p1 = prevslot(t, pt, 1)
        if p1:
            emit_a2(*p1)
        emit_a1(t, pt)
        p2 = prevslot(t, pt, 2)
        if p2:
            emit_a3(*p2)
        emit_v1(t, pt)
        if p1:
            emit_v2(*p1)

    def emit_prep(t, slot):
        if slot == 0:
            emit_spill_t = t - 1
            if emit_spill_t >= 0:
                emit_spill(emit_spill_t)
            emit_xload(t, 0)
            emit_ln(t, 0)
        elif slot == 1:
            emit_ln(t, 1)
        elif slot == 2:
            emit_xload(t, 1)
            emit_ln(t, 2)
        elif slot == 3:
            emit_ln(t, 3)
        elif slot == 4:
            emit_xnt(t)
        elif 5 <= slot <= 12:
            for ep in (2 * (slot - 5), 2 * (slot - 5) + 1):
                emit_mmaev(t, ep)
        if 11 <= slot <= 15:
            n = (3, 3, 3, 3, 4)[slot - 11]
            base = sum((3, 3, 3, 3, 4)[:slot - 11])
            for pt2 in range(base, base + n):
                emit_cvsilu(t, pt2)

    for slot in range(16):
        emit_prep(0, slot)

    for t in range(nt):
        for pt in range(NPT):
            emit_gt(t, pt)
            if t < nt - 1:
                if pt < 4:
                    for pt2 in range(4 * pt, 4 * pt + 4):
                        emit_vhc(t, pt2)
                emit_prep(t + 1, pt)
            p1 = prevslot(t, pt, 1)
            if p1:
                emit_a2(*p1)
            emit_a1(t, pt)
            p2 = prevslot(t, pt, 2)
            if p2:
                emit_a3(*p2)
            emit_v1(t, pt)
            if p1:
                emit_v2(*p1)
        if t == nt - 1:
            emit_spill(nt - 1)
            L = (nt - 1, NPT - 1)
            emit_a3(*prevslot(nt - 1, NPT - 1, 1))
            emit_a2(*L)
            emit_v2(*L)
            emit_a3(*L)

    def v_ccb(e):
        for pp in (NPT - 3, NPT - 2, NPT - 1):
            e.tensor_tensor(out=PFSUM[:, pp:pp + 1], in0=PFSUM[:, pp:pp + 1],
                            in1=LNFACC[:, pp:pp + 1], op=OP.add)
        e.tensor_copy(CCB[:, 0:16], RSTATE[:, :])
        return e.tensor_tensor(out=JUNKV[:, 0:16], in0=CCB[:, 0:16],
                               in1=CCB[:, 0:16], op=OP.add)
    em("vector", [("SVEC", at("SVEC", ("v2", nt - 1, NPT - 1))),
                  ("SACT", at("SACT", ("a3", nt - 1, NPT - 1))),
                  ("SACT", at("SACT", ("a3", nt - 1, NPT - 2))),
                  ("SACT", at("SACT", ("a3", nt - 1, NPT - 3)))], v_ccb,
       tinc=("SVEC", 1))
    inc("SVEC", 1, "vccb")

    em("scalar", [("SVEC", at("SVEC", "vccb"))],
       (lambda e: (e.activation(out=CCB[:, 16:32], in_=PFSUM[:, :], func=AF.Exp,
                                bias=CONS[:, 2:3]),
                   e.activation(out=JUNKA[:, 0:16], in_=CCB[:, 16:32],
                                func=AF.Copy))[1]),
       tinc=("SACT", 1))
    inc("SACT", 1, "apf")

    dma("gpsimd", [("SVEC", at("SVEC", "vccb")), ("SACT", at("SACT", "apf"))],
        CCIN, CCB[:, :], "SGP", "ccin")

    def g_cc(e):
        return e.collective_compute(
            "AllGather", OP.bypass, replica_groups=[list(range(8))],
            ins=[CCIN.opt()], outs=[CCOUT.opt()])
    em("gpsimd", [("SGP", at("SGP", "ccin"))], g_cc, tinc=("SCC", 1))
    inc("SCC", 1, "cc")

    dma("gpsimd", [("SCC", at("SCC", "cc"))],
        GATH[:, :].rearrange("p (r c) -> p r c", r=8),
        CCOUT.rearrange("(r p) c -> p r c", p=128), "SGP", "gath")

    def v_hlast(e):
        e.tensor_scalar(out=SEL1[:, :], in0=GATH[:, 0:32],
                        scalar1=PAIRM[:, 0:1], scalar2=None, op0=OP.mult)
        e.tensor_scalar(out=SEL2[:, :], in0=GATH[:, 0:32],
                        scalar1=PAIRM[:, 8:9], scalar2=None, op0=OP.mult)
        for r in range(1, 8):
            e.scalar_tensor_tensor(out=SEL1[:, :],
                                   in0=GATH[:, 32 * r:32 * (r + 1)],
                                   scalar=PAIRM[:, r:r + 1], in1=SEL1[:, :],
                                   op0=OP.mult, op1=OP.add)
            e.scalar_tensor_tensor(out=SEL2[:, :],
                                   in0=GATH[:, 32 * r:32 * (r + 1)],
                                   scalar=PAIRM[:, 8 + r:9 + r], in1=SEL2[:, :],
                                   op0=OP.mult, op1=OP.add)
        e.tensor_tensor(out=TMP16[:, :], in0=SEL2[:, 16:32], in1=SEL1[:, 0:16],
                        op=OP.mult)
        e.tensor_tensor(out=HL[:, :], in0=SEL2[:, 0:16], in1=TMP16[:, :],
                        op=OP.add)
        return e.tensor_tensor(out=JUNKV[:, 0:16], in0=HL[:, :],
                               in1=HL[:, :], op=OP.add)
    em("vector", [("SGP", at("SGP", "gath"))], v_hlast, tinc=("SVEC", 1))
    inc("SVEC", 1, "vhl")

    sag_full = cnt["SAG"]
    st1.close()

    cur[0] = "ph2"

    WDT_SB = sb(st, "wdt", [128, 16 * D], BF16)
    WRT = sb(st, "wrt", [128, 4 * 512], BF16)
    WIT = sb(st, "wit", [128, 4 * 512], BF16)
    WINEG = sb(st, "wineg", [128, 4 * 512], BF16)
    UCH = [sb(st, "uch0", [128, TILE], BF16), sb(st, "uch1", [128, TILE], BF16)]
    ACTT = [sb(st, "actt0", [128, TILE], BF16), sb(st, "actt1", [128, TILE], BF16)]
    T2B = sb(st, "t2b", [128, TILE], BF16)
    YT = sb(st, "yt", [128, NPT * TILE], BF16)
    XBC = [sb(st, "xbc0", [128, D], F32), sb(st, "xbc1", [128, D], F32)]
    R1 = sb(st, "r1", [128, NCH * D], F32)
    XN2B = sb(st, "xn2b", [128, NCH * D], BF16)
    XN2T = [sb(st, "xn2t0", [128, NDC * TILE], BF16),
            sb(st, "xn2t1", [128, NDC * TILE], BF16)]
    OUTC = [sb(st, "outc0", [128, D], F32), sb(st, "outc1", [128, D], F32)]

    P2_Z = [psm(st, "pz0", [128, TILE]), psm(st, "pz1", [128, TILE])]
    P2_AT = [psm(st, "pat0", [128, D]), psm(st, "pat1", [128, D])]
    P2_FR = psm(st, "pfr", [128, 512])
    P2_FI = psm(st, "pfi", [128, 512])

    for dc in range(NDC):
        dma("gpsimd", [], WUT_SB[:, dc * INNER:(dc + 1) * INNER],
            WUT[dc * 128:(dc + 1) * 128, INNER:2 * INNER], "SW")
    inc("SW", 0, "wz")
    for ec in range(16):
        dma("gpsimd", [], WDT_SB[:, ec * D:(ec + 1) * D],
            WDT[ec * 128:(ec + 1) * 128, :], "SW")
    inc("SW", 0, "wdt")
    for w_sb, od in ((WRT, ODR), (WIT, ODI)):
        od4 = od.rearrange("(d b) (g e) -> d b g e", b=32, e=16)
        for d in range(32):
            pp, r0 = d // 8, (d % 8) * 16
            dma("gpsimd", [("SMI", at("SMI", ("wbod", wb_last)))],
                w_sb[r0:r0 + 16, pp * 512:(pp + 1) * 512]
                .rearrange("g (b e) -> g b e", b=32),
                od4[d].rearrange("b g e -> g b e"), "SW")
    inc("SW", 0, "wffn")

    def v_wineg(e):
        return e.tensor_scalar(out=WINEG[:, :], in0=WIT[:, :], scalar1=-1.0,
                               scalar2=None, op0=OP.mult)
    em("vector", [("SW", at("SW", "wffn"))], v_wineg, tinc=("SVEC", 1))
    inc("SVEC", 1, "vwineg")

    xn2_4 = XN2.rearrange("(t c p) d -> t c p d", c=NCH, p=128)

    for t in range(nt):
        bi = t % 2
        for dc in range(NDC):
            w = []
            if t >= 2:
                w.append(("SPE", at("SPE", ("z", t - 2, NPT - 1))))
            dma("sync", w, XNT[bi][:, dc * TILE:(dc + 1) * TILE],
                XN[t * TILE:(t + 1) * TILE, dc * 128:(dc + 1) * 128],
                "SXNT", ("xnt2", t, dc), transpose=True)

        def v_settle2(e, bi=bi):
            r = None
            for dc in range(NDC):
                r = e.tensor_copy(JUNKA[:, 2:3],
                                  XNT[bi][:, (dc + 1) * TILE - 1:
                                          (dc + 1) * TILE])
            return r
        em("vector", [("SXNT", at("SXNT", ("xnt2", t, NDC - 1)))], v_settle2,
           tinc=("SVEC", 1))
        inc("SVEC", 1, ("vxs2", t))

        def emit_actld(t, pt):
            w = []
            if t == 0 and pt < 2:
                w.append(("SAG", sag_full))
            if (t, pt) >= (0, 2):
                pe_, pt_ = (t, pt - 2) if pt >= 2 else (t - 1, NPT - 2 + pt)
                w.append(("SACT", at("SACT", ("at2", pe_, pt_))))
            dma("scalar", w, ACTT[pt % 2][:, :],
                AG[pt, :, t * TILE:(t + 1) * TILE], "SACTLD", ("actld", t, pt))

        emit_actld(t, 0)
        emit_actld(t, 1)

        for pt in range(NPT):
            def pe_z(e, pt=pt, bi=bi):
                r = None
                for dc in range(NDC):
                    r = e.matmul(out=P2_Z[pt % 2][:, :],
                                 lhsT=WUT_SB[:, dc * INNER + pt * 128:
                                             dc * INNER + (pt + 1) * 128],
                                 rhs=XNT[bi][:, dc * TILE:(dc + 1) * TILE],
                                 start=(dc == 0), stop=(dc == NDC - 1))
                return r
            w = [("SVEC", at("SVEC", ("vxs2", t))), ("SW", at("SW", "wz"))]
            if (t, pt) >= (0, 2):
                pe_, pt_ = (t, pt - 2) if pt >= 2 else (t - 1, NPT - 2 + pt)
                w.append(("SACT", at("SACT", ("au", pe_, pt_))))
            em("tensor", w, pe_z, tinc=("SPE", 1))
            inc("SPE", 1, ("z", t, pt))

            w = [("SPE", at("SPE", ("z", t, pt)))]
            if (t, pt) >= (0, 2):
                pe_, pt_ = (t, pt - 2) if pt >= 2 else (t - 1, NPT - 2 + pt)
                w.append(("SVEC", at("SVEC", ("vy", pe_, pt_))))
            em("scalar", w,
               (lambda e, pt=pt: (e.activation(out=UCH[pt % 2][:, :],
                                               in_=P2_Z[pt % 2][:, :],
                                               func=AF.Silu,
                                               bias=BUPZ[:, pt:pt + 1]),
                                  e.activation(out=JUNKA[:, 0:1],
                                               in_=UCH[pt % 2][:, 511:512],
                                               func=AF.Copy))[1]),
               tinc=("SACT", 1))
            inc("SACT", 1, ("au", t, pt))

            w = [("SACTLD", at("SACTLD", ("actld", t, pt))),
                 ("SVEC", at("SVEC", "vhl"))]
            if (t, pt) >= (0, 1):
                pe_, pt_ = (t, pt - 1) if pt >= 1 else (t - 1, NPT - 1)
                w.append(("SVEC", at("SVEC", ("vy", pe_, pt_))))
            em("scalar", w,
               (lambda e, pt=pt: (e.activation(out=T2B[:, :],
                                               in_=ACTT[pt % 2][:, :],
                                               func=AF.Identity,
                                               scale=SKIPS[:, pt:pt + 1],
                                               bias=HL[:, pt:pt + 1]),
                                  e.activation(out=JUNKA[:, 0:1],
                                               in_=T2B[:, 511:512],
                                               func=AF.Copy))[1]),
               tinc=("SACT", 1))
            inc("SACT", 1, ("at2", t, pt))
            if pt + 2 < NPT:
                emit_actld(t, pt + 2)

            def v_y(e, pt=pt):
                e.tensor_tensor(out=YT[:, pt * TILE:(pt + 1) * TILE],
                                in0=T2B[:, :], in1=UCH[pt % 2][:, :],
                                op=OP.mult)
                return e.tensor_tensor(out=JUNKV[:, 2:3],
                                       in0=YT[:, (pt + 1) * TILE - 1:
                                              (pt + 1) * TILE],
                                       in1=YT[:, (pt + 1) * TILE - 1:
                                              (pt + 1) * TILE], op=OP.add)
            w = [("SACT", at("SACT", ("at2", t, pt))),
                 ("SACT", at("SACT", ("au", t, pt)))]
            if t >= 1:
                w.append(("SPE", at("SPE", ("dn", t - 1, NCH - 1))))
            em("vector", w, v_y, tinc=("SVEC", 1))
            inc("SVEC", 1, ("vy", t, pt))

        def emit_x2(t, c):
            w = []
            if (t, c) >= (0, 2):
                pe_, pc_ = (t, c - 2) if c >= 2 else (t - 1, NCH - 2 + c)
                w.append(("SVEC", at("SVEC", ("vr1", pe_, pc_))))
            dma("sync", w, XBC[c % 2][:, :], xs4[t, c], "SX", ("x2", t, c))

        emit_x2(t, 0)
        emit_x2(t, 1)

        for c in range(NCH):
            def pe_dn(e, c=c):
                r = None
                for ec in range(16):
                    for n in range(2):
                        r = e.matmul(out=P2_AT[c % 2][:, n * 512:(n + 1) * 512],
                                     lhsT=YT[:, ec * TILE + c * 128:
                                             ec * TILE + (c + 1) * 128],
                                     rhs=WDT_SB[:, ec * D + n * 512:
                                                ec * D + (n + 1) * 512],
                                     start=(ec == 0), stop=False)
                for n in range(2):
                    r = e.matmul(out=P2_AT[c % 2][:, n * 512:(n + 1) * 512],
                                 lhsT=ONESB[0:1, :],
                                 rhs=BDOWNB[0:1, n * 512:(n + 1) * 512],
                                 start=False, stop=True)
                return r
            w = [("SVEC", at("SVEC", ("vy", t, NPT - 1))),
                 ("SW", at("SW", "wdt"))]
            if (t, c) >= (0, 2):
                pe_, pc_ = (t, c - 2) if c >= 2 else (t - 1, NCH - 2 + c)
                w.append(("SVEC", at("SVEC", ("vr1", pe_, pc_))))
            em("tensor", w, pe_dn, tinc=("SPE", 1))
            inc("SPE", 1, ("dn", t, c))

            def v_r1(e, c=c):
                return e.tensor_tensor(out=R1[:, c * D:(c + 1) * D],
                                       in0=XBC[c % 2][:, :],
                                       in1=P2_AT[c % 2][:, :], op=OP.add)
            em("vector", [("SPE", at("SPE", ("dn", t, c))),
                          ("SX", at("SX", ("x2", t, c)))], v_r1,
               tinc=("SVEC", 1))
            inc("SVEC", 1, ("vr1", t, c))
            if c + 2 < NCH:
                emit_x2(t, c + 2)

            def v_st2(e, c=c):
                return stats_ops(e, R1[:, c * D:(c + 1) * D], 128)
            w = [("SVEC", at("SVEC", ("vr1", t, c)))]
            if (t, c) != (0, 0):
                pc = (t, c - 1) if c > 0 else (t - 1, NCH - 1)
                w.append(("SACT", at("SACT", ("aln2", pc[0], pc[1]))))
            em("vector", w, v_st2, tinc=("SVEC", 1))
            inc("SVEC", 1, ("vst2", t, c))

            em("scalar", [("SVEC", at("SVEC", ("vst2", t, c)))],
               (lambda e: (e.activation(out=SD1[:, 0:1], in_=VARB[:, 0:1],
                                        func=AF.Ln, bias=CONS[:, 0:1]),
                           e.activation(out=RSTD[:, 0:1], in_=SD1[:, 0:1],
                                        func=AF.Exp, scale=-0.5),
                           e.activation(out=JUNKA[:, 0:1], in_=RSTD[:, 0:1],
                                        func=AF.Copy))[2]),
               tinc=("SACT", 1))
            inc("SACT", 1, ("asq2", t, c))

            def v_rs2(e):
                e.scalar_tensor_tensor(out=NMR[:, 0:1], in0=MEANB[:, 0:1],
                                       scalar=-1.0, in1=RSTD[:, 0:1],
                                       op0=OP.mult, op1=OP.mult)
                return e.tensor_tensor(out=JUNKV[:, 1:2], in0=RSTD[:, 0:1],
                                       in1=NMR[:, 0:1], op=OP.add)
            em("vector", [("SACT", at("SACT", ("asq2", t, c)))], v_rs2,
               tinc=("SVEC", 1))
            inc("SVEC", 1, ("vrs2", t, c))

            w = [("SVEC", at("SVEC", ("vrs2", t, c)))]
            if c == 0 and t > 0:
                w.append(("SXN2W", at("SXN2W", ("xn2w", t - 1))))
            em("scalar", w,
               (lambda e, c=c: (e.activation(out=XN2B[:, c * D:(c + 1) * D],
                                             in_=R1[:, c * D:(c + 1) * D],
                                             func=AF.Identity,
                                             scale=RSTD[:, 0:1],
                                             bias=NMR[:, 0:1]),
                                e.activation(out=JUNKA[:, 0:1],
                                             in_=XN2B[:, (c + 1) * D - 1:
                                                     (c + 1) * D],
                                             func=AF.Copy))[1]),
               tinc=("SACT", 1))
            inc("SACT", 1, ("aln2", t, c))

        dma("scalar", [("SACT", at("SACT", ("aln2", t, NCH - 1)))],
            xn2_4[t].rearrange("c p d -> p c d"),
            XN2B[:, :].rearrange("p (c d) -> p c d", c=NCH), "SXN2W", ("xn2w", t))
        for dc in range(NDC):
            w = [("SXN2W", at("SXN2W", ("xn2w", t)))]
            if t >= 2:
                w.append(("SPE", at("SPE", ("ffn", t - 2, NCH - 1))))
            dma("sync", w, XN2T[bi][:, dc * TILE:(dc + 1) * TILE],
                XN2[t * TILE:(t + 1) * TILE, dc * 128:(dc + 1) * 128],
                "SXN2T", ("xn2t", t, dc), transpose=True)

        def v_settle3(e, bi=bi):
            r = None
            for dc in range(NDC):
                r = e.tensor_copy(JUNKA[:, 2:3],
                                  XN2T[bi][:, (dc + 1) * TILE - 1:
                                           (dc + 1) * TILE])
            return r
        em("vector", [("SXN2T", at("SXN2T", ("xn2t", t, NDC - 1)))], v_settle3,
           tinc=("SVEC", 1))
        inc("SVEC", 1, ("vxs3", t))

        for c in range(NCH):
            def pe_ffn(e, c=c, bi=bi):
                for dc in range(4):
                    e.matmul(out=P2_FR[:, :],
                             lhsT=XN2T[bi][:, dc * TILE + c * 128:
                                           dc * TILE + (c + 1) * 128],
                             rhs=WRT[:, dc * 512:(dc + 1) * 512],
                             start=(dc == 0), stop=False)
                for dc in range(4):
                    e.matmul(out=P2_FR[:, :],
                             lhsT=XN2T[bi][:, (4 + dc) * TILE + c * 128:
                                           (4 + dc) * TILE + (c + 1) * 128],
                             rhs=WINEG[:, dc * 512:(dc + 1) * 512],
                             start=False, stop=False)
                e.matmul(out=P2_FR[:, :], lhsT=ONESB[0:1, :], rhs=BRB[0:1, :],
                         start=False, stop=True)
                for dc in range(4):
                    e.matmul(out=P2_FI[:, :],
                             lhsT=XN2T[bi][:, dc * TILE + c * 128:
                                           dc * TILE + (c + 1) * 128],
                             rhs=WIT[:, dc * 512:(dc + 1) * 512],
                             start=(dc == 0), stop=False)
                for dc in range(4):
                    e.matmul(out=P2_FI[:, :],
                             lhsT=XN2T[bi][:, (4 + dc) * TILE + c * 128:
                                           (4 + dc) * TILE + (c + 1) * 128],
                             rhs=WRT[:, dc * 512:(dc + 1) * 512],
                             start=False, stop=False)
                return e.matmul(out=P2_FI[:, :], lhsT=ONESB[0:1, :],
                                rhs=BIB[0:1, :], start=False, stop=True)
            w = [("SVEC", at("SVEC", ("vxs3", t))),
                 ("SW", at("SW", "wffn")), ("SVEC", at("SVEC", "vwineg"))]
            if (t, c) >= (0, 1):
                pe_, pc_ = (t, c - 1) if c >= 1 else (t - 1, NCH - 1)
                w.append(("SVEC", at("SVEC", ("vfin", pe_, pc_))))
            em("tensor", w, pe_ffn, tinc=("SPE", 1))
            inc("SPE", 1, ("ffn", t, c))

            def v_fin(e, c=c):
                e.tensor_tensor(out=OUTC[c % 2][:, 0:512],
                                in0=R1[:, c * D: c * D + 512],
                                in1=P2_FR[:, :], op=OP.add)
                e.tensor_tensor(out=OUTC[c % 2][:, 512:1024],
                                in0=R1[:, c * D + 512: c * D + 1024],
                                in1=P2_FI[:, :], op=OP.add)
                return e.tensor_tensor(out=JUNKV[:, 5:6],
                                       in0=OUTC[c % 2][:, 1023:1024],
                                       in1=OUTC[c % 2][:, 1023:1024],
                                       op=OP.add)
            w = [("SPE", at("SPE", ("ffn", t, c)))]
            if (t, c) >= (0, 2):
                pe_, pc_ = (t, c - 2) if c >= 2 else (t - 1, NCH - 2 + c)
                w.append(("SOUT", at("SOUT", ("out", pe_, pc_))))
            em("vector", w, v_fin, tinc=("SVEC", 1))
            inc("SVEC", 1, ("vfin", t, c))

            dma("scalar", [("SVEC", at("SVEC", ("vfin", t, c)))],
                OUT[(t * NCH + c) * 128:(t * NCH + c + 1) * 128, :],
                OUTC[c % 2][:, :], "SOUT", ("out", t, c))

    if dbg:
        DXN = nc.dram_tensor("DXN", [t_tok, D], BF16, kind="ExternalOutput").ap()
        DAG = nc.dram_tensor("DAG", [NPT, 128, t_tok], BF16,
                             kind="ExternalOutput").ap()
        DXN2 = nc.dram_tensor("DXN2", [t_tok, D], BF16,
                              kind="ExternalOutput").ap()
        DCCB = nc.dram_tensor("DCCB", [128, 32], F32, kind="ExternalOutput").ap()
        DGATH = nc.dram_tensor("DGATH", [128, 256], F32,
                               kind="ExternalOutput").ap()
        DHL = nc.dram_tensor("DHL", [128, NPT], F32, kind="ExternalOutput").ap()
        DWRT = nc.dram_tensor("DWRT", [128, 4 * 512], BF16,
                              kind="ExternalOutput").ap()
        fo = cnt["SOUT"]
        dma("gpsimd", [("SOUT", fo)], DXN, XN, "SGP")
        dma("gpsimd", [], DAG, AG, "SGP")
        dma("gpsimd", [], DXN2, XN2, "SGP")
        dma("gpsimd", [], DCCB, CCB[:, :], "SGP")
        dma("gpsimd", [], DGATH, GATH[:, :], "SGP")
        dma("gpsimd", [], DHL, HL[:, :], "SGP")
        dma("gpsimd", [], DWRT, WRT[:, :], "SGP")
        em("gpsimd", [("SGP", cnt["SGP"])], lambda e: e.nop())

    final_out = cnt["SOUT"]
    em("scalar", [("SOUT", final_out)], lambda e: e.nop())

    for name in ("ph1", "ph2"):
        pl = plans[name]
        with nc.Block(name) as blk:
            @blk.sync
            def _(e, pl=pl):
                for eng, fn in pl:
                    if eng == "sync":
                        fn(e)

            @blk.gpsimd
            def _(e, pl=pl):
                for eng, fn in pl:
                    if eng == "gpsimd":
                        fn(e)

            @blk.tensor
            def _(e, pl=pl):
                for eng, fn in pl:
                    if eng == "tensor":
                        fn(e)

            @blk.scalar
            def _(e, pl=pl):
                for eng, fn in pl:
                    if eng == "scalar":
                        fn(e)

            @blk.vector
            def _(e, pl=pl):
                for eng, fn in pl:
                    if eng == "vector":
                        fn(e)

    st.close()
    return nc


def _host_inputs(inputs, core):
    b, h = core // 2, core % 2
    x = inputs["L0_attn_x"]
    w_up = inputs["L0_attn_w_up"]
    conv_w = inputs["L0_attn_conv_w"][:, 0, :]
    start = h * T
    xs = np.ascontiguousarray(x[b, start:start + T])
    if h == 0:
        xh = np.zeros((3, D), np.float32)
        halom = np.zeros((128, 1), np.float32)
    else:
        xh = np.ascontiguousarray(x[b, start - 3:start])
        halom = np.ones((128, 1), np.float32)
    pairm = np.zeros((128, 16), np.float32)
    pairm[:, 2 * b] = 1.0
    pairm[:, 8 + 2 * b + 1] = 1.0

    def pcol(v):
        return np.ascontiguousarray(v.reshape(NPT, 128).T)

    bd = {}
    for nm, w in (("BDF", inputs["L0_attn_w_f"]), ("BDI", inputs["L0_attn_w_i"]),
                  ("BDH", inputs["L0_attn_w_h"])):
        m = np.zeros((NPT, 128, 128), np.float32)
        for g in range(128):
            tt, j = g // 8, g % 8
            m[tt, 16 * j:16 * (j + 1), 16 * j:16 * (j + 1)] = w[g].T
        bd[nm] = np.ascontiguousarray(m.transpose(1, 0, 2).reshape(128, NPT * 128))

    g1r = inputs["L0_ffn_G1_R"][0]
    g2r = inputs["L0_ffn_G2_R"][:, :, 0, :]
    g1i = inputs["L0_ffn_G1_I"][0]
    g2i = inputs["L0_ffn_G2_I"][:, :, 0, :]

    def g1pack(g1):
        return np.ascontiguousarray(g1.transpose(1, 2, 0).reshape(1, 4096))

    def g2pack(g2):
        return np.ascontiguousarray(g2.transpose(0, 2, 1).reshape(1, 1024))

    cwm = np.ascontiguousarray(
        conv_w.reshape(NPT, 128, 4).transpose(1, 0, 2).reshape(128, 4 * NPT))

    return {
        "XS": xs, "XH": xh,
        "WUT": np.ascontiguousarray(w_up.T),
        "WDT": np.ascontiguousarray(inputs["L0_attn_w_down"].T),
        "BDF": bd["BDF"], "BDI": bd["BDI"], "BDH": bd["BDH"],
        "CWD": cwm,
        "BUPM": pcol(inputs["L0_attn_b_up"][:INNER]),
        "BUPMR": np.ascontiguousarray(
            inputs["L0_attn_b_up"][:INNER].reshape(1, INNER)),
        "CBR": np.ascontiguousarray(
            inputs["L0_attn_conv_b"].reshape(1, INNER)),
        "BUPZ": pcol(inputs["L0_attn_b_up"][INNER:]),
        "CBD": pcol(inputs["L0_attn_conv_b"]),
        "SKIPD": pcol(inputs["L0_attn_skip"]),
        "HALOM": halom, "PAIRM": pairm,
        "EYE": np.eye(128, dtype=np.float32),
        "BDOWND": np.ascontiguousarray(inputs["L0_attn_b_down"].reshape(1, D)),
        "BRD": np.ascontiguousarray(inputs["L0_ffn_bias_real"].reshape(1, 512)),
        "BID": np.ascontiguousarray(inputs["L0_ffn_bias_imag"].reshape(1, 512)),
        "G1R": g1pack(g1r), "G2R": g2pack(g2r),
        "G1I": g1pack(g1i), "G2I": g2pack(g2i),
        "CONS": np.tile(np.array([[LNEPS, 1.0, T * EPS, 2.0]], np.float32),
                        (128, 1)),
    }


def kernel(**inputs):
    inputs = {k: np.ascontiguousarray(np.asarray(v)) for k, v in inputs.items()}
    nc = build_nc()
    in_maps = [_host_inputs(inputs, k) for k in range(8)]
    res = run_bass_kernel_spmd(nc, in_maps, core_ids=list(range(8)))
    out = np.empty((B, S, D), np.float32)
    for k in range(8):
        b, h = k // 2, k % 2
        out[b, h * T:(h + 1) * T] = res.results[k]["OUT"]
    return out




# version 13
# speedup vs baseline: 1.1221x; 1.1221x over previous
import numpy as np
from collections import defaultdict
from contextlib import ExitStack

import concourse.bass as bass
import concourse.mybir as mybir
from concourse.bass_utils import run_bass_kernel_spmd

F32 = mybir.dt.float32
BF16 = mybir.dt.bfloat16
AF = mybir.ActivationFunctionType
OP = mybir.AluOpType

B, S, D = 4, 8192, 1024
INNER = 2048
NPT = INNER // 128
NDC = D // 128
T = 4096
TILE = 512
NT = T // TILE
NCH = TILE // 128
EPS = 1e-8
LNEPS = 1e-5
GEPS = 0.5 + 1e-8


def build_nc(n_tiles=NT, dbg=False):
    nt = n_tiles
    t_tok = nt * TILE
    nc = bass.Bass("TRN2", target_bir_lowering=False,
                   detect_race_conditions=False)

    EI = dict(kind="ExternalInput")
    XS = nc.dram_tensor("XS", [t_tok, D], F32, **EI).ap()
    XH = nc.dram_tensor("XH", [3, D], F32, **EI).ap()
    WUT = nc.dram_tensor("WUT", [D, 2 * INNER], F32, **EI).ap()
    WDT = nc.dram_tensor("WDT", [INNER, D], F32, **EI).ap()
    BDF_D = nc.dram_tensor("BDF", [128, NPT * 128], F32, **EI).ap()
    BDI_D = nc.dram_tensor("BDI", [128, NPT * 128], F32, **EI).ap()
    BDH_D = nc.dram_tensor("BDH", [128, NPT * 128], F32, **EI).ap()
    CWD = nc.dram_tensor("CWD", [128, 4 * NPT], F32, **EI).ap()
    BUPM_D = nc.dram_tensor("BUPM", [128, NPT], F32, **EI).ap()
    BUPZ_D = nc.dram_tensor("BUPZ", [128, NPT], F32, **EI).ap()
    CB_D = nc.dram_tensor("CBD", [128, NPT], F32, **EI).ap()
    SKIP_D = nc.dram_tensor("SKIPD", [128, NPT], F32, **EI).ap()
    HALOM_D = nc.dram_tensor("HALOM", [128, 1], F32, **EI).ap()
    PAIRM_D = nc.dram_tensor("PAIRM", [128, 16], F32, **EI).ap()
    EYE_D = nc.dram_tensor("EYE", [128, 128], F32, **EI).ap()
    BDOWN_D = nc.dram_tensor("BDOWND", [1, D], F32, **EI).ap()
    BR_D = nc.dram_tensor("BRD", [1, 512], F32, **EI).ap()
    BI_D = nc.dram_tensor("BID", [1, 512], F32, **EI).ap()
    G1R_D = nc.dram_tensor("G1R", [1, 4096], F32, **EI).ap()
    G2R_D = nc.dram_tensor("G2R", [1, 1024], F32, **EI).ap()
    G1I_D = nc.dram_tensor("G1I", [1, 4096], F32, **EI).ap()
    G2I_D = nc.dram_tensor("G2I", [1, 1024], F32, **EI).ap()
    CONS_D = nc.dram_tensor("CONS", [128, 4], F32, **EI).ap()

    OUT = nc.dram_tensor("OUT", [t_tok, D], F32, kind="ExternalOutput").ap()

    XN = nc.dram_tensor("XNs", [t_tok, D], BF16).ap()
    XN2 = nc.dram_tensor("XN2s", [t_tok, D], BF16).ap()
    AG = nc.dram_tensor("AGs", [NPT, 128, t_tok], BF16).ap()
    ODR = nc.dram_tensor("ODRs", [1024, 256], BF16).ap()
    ODI = nc.dram_tensor("ODIs", [1024, 256], BF16).ap()
    CCIN = nc.dram_tensor("CCINs", [128, 32], F32).ap()
    CCOUT = nc.dram_tensor("CCOUTs", [8 * 128, 32], F32).ap()

    st = ExitStack()
    st1 = ExitStack()

    def sb(stk, name, shape, dt):
        return stk.enter_context(nc.sbuf_tensor(name, shape, dt))

    def psm(stk, name, shape, dt=F32):
        return stk.enter_context(nc.psum_tensor(name, shape, dt))

    WUT_SB = sb(st, "wut", [128, NDC * INNER], BF16)
    EYE = sb(st, "eye", [128, 128], F32)
    EYEB = sb(st, "eyeb", [128, 128], BF16)
    BUPM = sb(st, "bupm", [128, NPT], F32)
    BUPZ = sb(st, "bupz", [128, NPT], F32)
    CBS = sb(st, "cbs", [128, NPT], F32)
    SKIPS = sb(st, "skips", [128, NPT], F32)
    HALOM = sb(st, "halom", [128, 1], F32)
    PAIRM = sb(st, "pairm", [128, 16], F32)
    ONESB = sb(st, "onesb", [1, 128], BF16)
    BDOWNB = sb(st, "bdownb", [1, D], BF16)
    BRB = sb(st, "brb", [1, 512], BF16)
    BIB = sb(st, "bib", [1, 512], BF16)
    RSTATE = sb(st, "rstate", [128, NPT], F32)
    PFSUM = sb(st, "pfsum", [128, NPT], F32)
    LNFACC = sb(st, "lnfacc", [128, NPT], F32)
    CCB = sb(st, "ccb", [128, 32], F32)
    GATH = sb(st, "gath", [128, 256], F32)
    SEL1 = sb(st, "sel1", [128, 32], F32)
    SEL2 = sb(st, "sel2", [128, 32], F32)
    HL = sb(st, "hl", [128, NPT], F32)
    TMP16 = sb(st, "tmp16", [128, NPT], F32)
    STATB = sb(st, "statb", [128, 12], F32)
    MV = sb(st, "mv", [128, 2], F32)
    SD1 = sb(st, "sd1", [128, 1], F32)
    RSTD = sb(st, "rstd", [128, 1], F32)
    NMR = sb(st, "nmr", [128, 1], F32)
    CONS = sb(st, "cons", [128, 4], F32)
    SCRQ = sb(st, "scrq", [128, 1024], F32)
    SMB = sb(st, "smb", [128, 1], F32)
    SSQ = sb(st, "ssq", [128, 1], F32)
    MEANB = sb(st, "meanb", [128, 1], F32)
    MSQ = sb(st, "msq", [128, 1], F32)
    VARB = sb(st, "varb", [128, 1], F32)
    JUNKV = sb(st, "junkv", [128, 16], F32)
    JUNKA = sb(st, "junka", [128, 16], F32)
    XNT = [sb(st, "xnt0", [128, NDC * TILE], BF16),
           sb(st, "xnt1", [128, NDC * TILE], BF16)]

    BDF_SB = sb(st1, "bdf", [128, NPT * 128], BF16)
    BDI_SB = sb(st1, "bdi", [128, NPT * 128], BF16)
    BDH_SB = sb(st1, "bdh", [128, NPT * 128], BF16)
    DIAG = sb(st1, "diag", [128, 4 * NPT * 128], BF16)
    CWS = sb(st1, "cws", [128, 4 * NPT], F32)
    BUPMK = sb(st1, "bupmk", [128, NPT], F32)
    XMIN = sb(st1, "xmin", [128, NPT * 520], BF16)
    XB = [sb(st1, "xb0", [128, 2 * D], F32), sb(st1, "xb1", [128, 2 * D], F32)]
    XNB = [sb(st1, "xnb0", [128, D], BF16), sb(st1, "xnb1", [128, D], BF16)]
    AGT = sb(st1, "agt", [128, NPT * TILE], BF16)
    FIH = [sb(st1, "fih0", [128, 3 * TILE], F32),
           sb(st1, "fih1", [128, 3 * TILE], F32)]
    UB = [sb(st1, "ub0", [128, TILE], F32), sb(st1, "ub1", [128, TILE], F32)]
    WB = [sb(st1, "wb0", [128, TILE], F32), sb(st1, "wb1", [128, TILE], F32)]
    LNH = sb(st1, "lnh", [128, TILE], F32)
    GB = sb(st1, "gb", [128, TILE], F32)
    GHB = [sb(st1, "ghb0", [128, TILE], F32), sb(st1, "ghb1", [128, TILE], F32)]
    FB = [sb(st1, "fb0", [128, TILE], F32), sb(st1, "fb1", [128, TILE], F32)]
    TB = sb(st1, "tb", [128, TILE], F32)
    GGB = sb(st1, "ggb", [128, TILE], F32)
    RSOUT = sb(st1, "rsout", [128, TILE], F32)

    P_MM = [psm(st1, "pmm0", [128, TILE]), psm(st1, "pmm1", [128, TILE])]
    P_CV = [psm(st1, "pcv0", [128, TILE]), psm(st1, "pcv1", [128, TILE])]

    G1RS = sb(st1, "g1rs", [1, 4096], BF16)
    G2RS = sb(st1, "g2rs", [1, 1024], BF16)
    G1IS = sb(st1, "g1is", [1, 4096], BF16)
    G2IS = sb(st1, "g2is", [1, 1024], BF16)
    OSB = sb(st1, "osb", [128, 256], BF16)
    XHST = sb(st1, "xhst", [3, 12], F32)
    XHMV = sb(st1, "xhmv", [3, 2], F32)
    XHSD = sb(st1, "xhsd", [3, 1], F32)
    XHRS = sb(st1, "xhrs", [3, 1], F32)
    XHNM = sb(st1, "xhnm", [3, 1], F32)
    XHT = sb(st1, "xht", [128, NDC * 3], BF16)
    XHS = XB[0][0:3, 0:D]
    XHN = XB[0][0:3, D:2 * D]
    P_AUX = psm(st1, "paux", [128, 256])
    P_G = psm(st1, "pg", [128, 3 * TILE])

    sem_names = ["SX", "SXNT", "SXNW", "SAG", "SW", "SMI", "SCC", "SACTLD",
                 "SXN2W", "SXN2T", "SOUT", "SPE", "SACT", "SVEC", "SGP"]
    SEM = {n: st.enter_context(nc.semaphore(n)) for n in sem_names}

    cnt = defaultdict(int)
    mark = {}

    def inc(s, n, tag=None):
        if s in ("SVEC", "SACT") and last_nops:
            n = last_nops.pop()
        cnt[s] += n
        if tag is not None:
            mark[(s, tag)] = cnt[s]
        return cnt[s]

    def at(s, tag):
        return mark[(s, tag)]

    plans = {"ph1": [], "ph2": []}
    cur = ["ph1"]

    _WRAP = ("activation", "tensor_scalar", "tensor_tensor",
             "scalar_tensor_tensor", "tensor_tensor_scan", "tensor_copy",
             "tensor_reduce", "reciprocal", "memset", "memzero", "select",
             "copy", "add", "mul", "sign", "sqrt", "square", "reduce_max",
             "reduce_sum", "bn_stats", "bn_aggr")

    class _Counter:
        def __init__(self):
            self.n = 0

        def __getattr__(self, name):
            if name in _WRAP:
                def f(*a, **k):
                    self.n += 1
                    return self
                return f
            def g(*a, **k):
                return self
            return g

        def then_inc(self, *a, **k):
            return self

    class _Proxy:
        def __init__(self, e, sem, base):
            self._e = e
            self._sem = sem
            self._n = base

        def __getattr__(self, name):
            e = self._e
            if name in _WRAP:
                def f(*a, **k):
                    e.wait_ge(self._sem, self._n)
                    r = getattr(e, name)(*a, **k)
                    r.then_inc(self._sem, 1)
                    self._n += 1
                    return r
                return f
            return getattr(e, name)

    _selfsem = {"vector": "SVEC", "scalar": "SACT"}
    last_nops = []

    def em(eng, waits, fn, tinc=None):
        w = [(SEM[s], v) for (s, v) in waits]
        if eng in _selfsem:
            c = _Counter()
            fn(c)
            last_nops.append(c.n)
            ssem = SEM[_selfsem[eng]]
            base = cnt[_selfsem[eng]]

            if tinc is not None and tinc[0] == _selfsem[eng]:
                tinc = None
            ti = (SEM[tinc[0]], tinc[1]) if tinc else None

            def run(e, w=w, fn=fn, ssem=ssem, base=base, nops=c.n, ti=ti):
                for sh, v in w:
                    e.wait_ge(sh, v)
                p = _Proxy(e, ssem, base)
                r = fn(p)
                if ti is not None:
                    r.then_inc(ti[0], ti[1])
                assert p._n == base + nops, (
                    f"op count mismatch: {p._n} != {base}+{nops}")
            plans[cur[0]].append((eng, run))
            return

        ti = (SEM[tinc[0]], tinc[1]) if tinc else None

        def run(e, w=w, fn=fn, ti=ti):
            for sh, v in w:
                e.wait_ge(sh, v)
            r = fn(e)
            if ti is not None:
                r.then_inc(ti[0], ti[1])
        plans[cur[0]].append((eng, run))

    def _inc_orig(s, n, tag=None):
        cnt[s] += n
        if tag is not None:
            mark[(s, tag)] = cnt[s]
        return cnt[s]

    def dma(eng, waits, out_ap, in_ap, s, tag=None, transpose=False):
        inc(s, 16, tag)
        em(eng, waits,
           (lambda e, o=out_ap, i=in_ap, tr=transpose:
            e.dma_start(out=o, in_=i, transpose=tr)),
           tinc=(s, 16))

    if dbg:
        DXHT = nc.dram_tensor("DXHT", [128, NDC * 3], BF16,
                              kind="ExternalOutput").ap()
        DXHN = nc.dram_tensor("DXHN", [3, D], F32, kind="ExternalOutput").ap()
        DXHRS = nc.dram_tensor("DXHRS", [3, 2], F32,
                               kind="ExternalOutput").ap()
        DXMIN = nc.dram_tensor("DXMIN", [128, NPT * 520], BF16,
                               kind="ExternalOutput").ap()


    RD = 1.0 / float(D)

    def stats_ops(e, src_ap, np_):
        e.tensor_scalar(out=SCRQ[0:np_, :], in0=src_ap, scalar1=0.0,
                        scalar2=0.0, op0=OP.add, op1=OP.add,
                        accum_out=SMB[0:np_, 0:1])
        e.scalar_tensor_tensor(out=SCRQ[0:np_, :], in0=src_ap, scalar=1.0,
                               in1=src_ap, op0=OP.mult, op1=OP.mult,
                               accum_out=SSQ[0:np_, 0:1])
        e.tensor_scalar(out=MEANB[0:np_, 0:1], in0=SMB[0:np_, 0:1],
                        scalar1=RD, scalar2=None, op0=OP.mult)
        e.tensor_tensor(out=MSQ[0:np_, 0:1], in0=MEANB[0:np_, 0:1],
                        in1=MEANB[0:np_, 0:1], op=OP.mult)
        e.scalar_tensor_tensor(out=VARB[0:np_, 0:1],
                               in0=SSQ[0:np_, 0:1], scalar=RD,
                               in1=MSQ[0:np_, 0:1], op0=OP.mult,
                               op1=OP.subtract)
        return e.tensor_tensor(out=JUNKV[0:np_, 0:1], in0=VARB[0:np_, 0:1],
                               in1=VARB[0:np_, 0:1], op=OP.add)
    for dc in range(NDC):
        dma("gpsimd", [], WUT_SB[:, dc * INNER:(dc + 1) * INNER],
            WUT[dc * 128:(dc + 1) * 128, 0:INNER], "SW")
    for bd_d, bd_s in ((BDF_D, BDF_SB), (BDI_D, BDI_SB), (BDH_D, BDH_SB)):
        dma("gpsimd", [], bd_s[:, :], bd_d, "SW")
    dma("gpsimd", [], CWS[:, :], CWD, "SW")
    dma("gpsimd", [], EYE[:, :], EYE_D, "SW")
    dma("gpsimd", [], BUPM[:, :], BUPM_D, "SW")
    dma("gpsimd", [], BUPZ[:, :], BUPZ_D, "SW")
    dma("gpsimd", [], CBS[:, :], CB_D, "SW")
    dma("gpsimd", [], SKIPS[:, :], SKIP_D, "SW")
    dma("gpsimd", [], HALOM[:, :], HALOM_D, "SW")
    dma("gpsimd", [], PAIRM[:, :], PAIRM_D, "SW")
    dma("gpsimd", [], BDOWNB[:, :], BDOWN_D, "SW")
    dma("gpsimd", [], BRB[:, :], BR_D, "SW")
    dma("gpsimd", [], BIB[:, :], BI_D, "SW")
    dma("gpsimd", [], G1RS[:, :], G1R_D, "SW")
    dma("gpsimd", [], G2RS[:, :], G2R_D, "SW")
    dma("gpsimd", [], G1IS[:, :], G1I_D, "SW")
    dma("gpsimd", [], G2IS[:, :], G2I_D, "SW")
    dma("gpsimd", [], CONS[:, :], CONS_D, "SW")
    w_all = cnt["SW"]

    dma("sync", [], XHS, XH, "SX", "xhalo")

    def v_init(e):
        e.memset(RSTATE[:, :], 0.0)
        e.memset(PFSUM[:, :], 0.0)
        e.memset(ONESB[:, :], 1.0)
        return e.tensor_copy(EYEB[:, :], EYE[:, :])
    em("vector", [("SW", w_all)], v_init, tinc=("SVEC", 1))
    inc("SVEC", 1, "vinit")

    def v_diag(e):
        r = None
        for pt in range(NPT):
            for k in range(4):
                j = pt * 4 + k
                r = e.tensor_scalar(out=DIAG[:, j * 128:(j + 1) * 128],
                                    in0=EYE[:, :], scalar1=CWS[:, j:j + 1],
                                    scalar2=None, op0=OP.mult)
        return r
    em("vector", [], v_diag, tinc=("SVEC", 1))
    inc("SVEC", 1, "vdiag")

    def v_bupmk(e):
        return e.tensor_scalar(out=BUPMK[:, :], in0=BUPM[:, :],
                               scalar1=HALOM[:, 0:1], scalar2=None, op0=OP.mult)
    em("vector", [], v_bupmk, tinc=("SVEC", 1))
    inc("SVEC", 1, "vbupmk")

    def v_halostats(e):
        return stats_ops(e, XHS, 3)
    em("vector", [("SX", at("SX", "xhalo"))], v_halostats, tinc=("SVEC", 1))
    inc("SVEC", 1, "vhstat")

    em("scalar", [("SVEC", at("SVEC", "vhstat")), ("SW", w_all)],
       lambda e: (e.activation(out=XHSD[0:3, 0:1], in_=VARB[0:3, 0:1],
                               func=AF.Ln, bias=CONS[0:3, 0:1]),
                  e.activation(out=XHRS[0:3, 0:1], in_=XHSD[0:3, 0:1],
                               func=AF.Exp, scale=-0.5),
                  e.activation(out=JUNKA[0:3, 0:1], in_=XHRS[0:3, 0:1],
                               func=AF.Copy))[2],
       tinc=("SACT", 1))
    inc("SACT", 1, "ahsqrt")

    def v_hrs(e):
        e.scalar_tensor_tensor(out=XHNM[0:3, 0:1], in0=MEANB[0:3, 0:1],
                               scalar=-1.0, in1=XHRS[0:3, 0:1],
                               op0=OP.mult, op1=OP.mult)
        return e.tensor_tensor(out=JUNKV[0:3, 1:2], in0=XHRS[0:3, 0:1],
                               in1=XHNM[0:3, 0:1], op=OP.add)
    em("vector", [("SACT", at("SACT", "ahsqrt"))], v_hrs, tinc=("SVEC", 1))
    inc("SVEC", 1, "vhrs")

    em("scalar", [("SVEC", at("SVEC", "vhrs"))],
       lambda e: e.activation(out=XHN, in_=XHS, func=AF.Identity,
                              scale=XHRS[0:3, 0:1], bias=XHNM[0:3, 0:1]),
       tinc=("SACT", 1))
    inc("SACT", 1, "ahnorm")

    for dc in range(NDC):
        em("tensor",
           [("SACT", at("SACT", "ahnorm")), ("SVEC", at("SVEC", "vinit"))] +
           ([("SACT", at("SACT", ("ahxt", dc - 1)))] if dc > 0 else []),
           (lambda e, dc=dc: e.transpose(out=P_AUX[:, 0:3],
                                         in_=XHN[:, dc * 128:(dc + 1) * 128],
                                         identity=EYE[0:3, 0:3])),
           tinc=("SPE", 1))
        inc("SPE", 1, ("hxt", dc))
        em("scalar", [("SPE", at("SPE", ("hxt", dc)))],
           (lambda e, dc=dc: e.activation(out=XHT[:, dc * 3:(dc + 1) * 3],
                                          in_=P_AUX[:, 0:3], func=AF.Copy)),
           tinc=("SACT", 1))
        inc("SACT", 1, ("ahxt", dc))

    for ep in range(NPT):
        def pe_halo(e, ep=ep):
            r = None
            for dc in range(NDC):
                r = e.matmul(out=P_MM[ep % 2][:, 0:3],
                             lhsT=WUT_SB[:, dc * INNER + ep * 128:
                                         dc * INNER + (ep + 1) * 128],
                             rhs=XHT[:, dc * 3:(dc + 1) * 3],
                             start=(dc == 0), stop=(dc == NDC - 1))
            return r
        waits = [("SACT", at("SACT", ("ahxt", NDC - 1))), ("SW", w_all)]
        if ep >= 2:
            waits.append(("SACT", at("SACT", ("ahev", ep - 2))))
        em("tensor", waits, pe_halo, tinc=("SPE", 1))
        inc("SPE", 1, ("hmm", ep))
        em("scalar", [("SPE", at("SPE", ("hmm", ep))),
                      ("SVEC", at("SVEC", "vbupmk"))],
           (lambda e, ep=ep: e.activation(out=XMIN[:, ep * 520: ep * 520 + 3],
                                          in_=P_MM[ep % 2][:, 0:3], func=AF.Identity,
                                          scale=HALOM[:, 0:1],
                                          bias=BUPMK[:, ep:ep + 1])),
           tinc=("SACT", 1))
        inc("SACT", 1, ("ahev", ep))

    if dbg:
        dma("scalar", [("SACT", at("SACT", ("ahev", NPT - 1)))],
            DXHT, XHT[:, :], "SMI")
        dma("scalar", [("SACT", at("SACT", "ahnorm"))], DXHN, XHN, "SMI")
        dma("scalar", [("SVEC", at("SVEC", "vhrs"))], DXHRS[0:3, 0:2],
            XHMV[0:3, 0:2], "SMI")

    prev_tag = None
    for oi, (g1, g2, od) in enumerate(((G1RS, G2RS, ODR), (G1IS, G2IS, ODI))):
        for q in range(8):
            def pe_outer(e, g1=g1, g2=g2, q=q):
                r = None
                for c in range(4):
                    r = e.matmul(out=P_AUX[:, 0:256],
                                 lhsT=g1[0:1, c * 1024 + q * 128:
                                         c * 1024 + (q + 1) * 128],
                                 rhs=g2[0:1, c * 256:(c + 1) * 256],
                                 start=(c == 0), stop=(c == 3))
                return r
            waits = [("SW", w_all), ("SACT", at("SACT", ("ahev", NPT - 1)))]
            if prev_tag is not None:
                waits.append(("SACT", at("SACT", ("wbev", prev_tag))))
            em("tensor", waits, pe_outer, tinc=("SPE", 1))
            inc("SPE", 1, ("wbmm", (oi, q)))
            waits = [("SPE", at("SPE", ("wbmm", (oi, q))))]
            if prev_tag is not None:
                waits.append(("SMI", at("SMI", ("wbod", prev_tag))))
            em("scalar", waits,
               (lambda e: (e.activation(out=OSB[:, :], in_=P_AUX[:, 0:256],
                                        func=AF.Copy),
                           e.activation(out=JUNKA[:, 0:1], in_=OSB[:, 255:256],
                                        func=AF.Copy))[1]),
               tinc=("SACT", 1))
            inc("SACT", 1, ("wbev", (oi, q)))
            dma("scalar", [("SACT", at("SACT", ("wbev", (oi, q))))],
                od[q * 128:(q + 1) * 128, :], OSB[:, :],
                "SMI", ("wbod", (oi, q)))
            prev_tag = (oi, q)
    wb_last = prev_tag


    xs4 = XS.rearrange("(t c p) d -> t c p d", c=NCH, p=128)
    xn4 = XN.rearrange("(t c p) d -> t c p d", c=NCH, p=128)

    def emit_xload(t, hf):
        waits = []
        if t == 0 and hf == 0:
            waits += [("SPE", at("SPE", ("wbmm", wb_last))),
                      ("SMI", at("SMI", ("wbod", wb_last))),
                      ("SACT", at("SACT", ("ahev", NPT - 1)))]
        if t >= 1:
            waits.append(("SACT", at("SACT", ("aln", t - 1, 2 * hf + 1))))
        dma("sync", waits,
            XB[hf][:, :].rearrange("p (c d) -> p c d", c=2),
            xs4[t, 2 * hf:2 * hf + 2].rearrange("c p d -> p c d"),
            "SX", ("x", t, hf))

    def emit_ln(t, c):
        hf, ci = c // 2, c % 2
        def v_stats(e, hf=hf, ci=ci):
            return stats_ops(e, XB[hf][:, ci * D:(ci + 1) * D], 128)
        w = [("SX", at("SX", ("x", t, hf)))]
        if (t, c) != (0, 0):
            pc = (t, c - 1) if c > 0 else (t - 1, NCH - 1)
            w.append(("SACT", at("SACT", ("aln", pc[0], pc[1]))))
        em("vector", w, v_stats, tinc=("SVEC", 1))
        inc("SVEC", 1, ("vstat", t, c))

        em("scalar", [("SVEC", at("SVEC", ("vstat", t, c)))],
           (lambda e: (e.activation(out=SD1[:, 0:1], in_=VARB[:, 0:1],
                                    func=AF.Ln, bias=CONS[:, 0:1]),
                       e.activation(out=RSTD[:, 0:1], in_=SD1[:, 0:1],
                                    func=AF.Exp, scale=-0.5),
                       e.activation(out=JUNKA[:, 0:1], in_=RSTD[:, 0:1],
                                    func=AF.Copy))[2]),
           tinc=("SACT", 1))
        inc("SACT", 1, ("asq", t, c))

        def v_rs(e):
            e.scalar_tensor_tensor(out=NMR[:, 0:1], in0=MEANB[:, 0:1],
                                   scalar=-1.0, in1=RSTD[:, 0:1],
                                   op0=OP.mult, op1=OP.mult)
            return e.tensor_tensor(out=JUNKV[:, 1:2], in0=RSTD[:, 0:1],
                                   in1=NMR[:, 0:1], op=OP.add)
        em("vector", [("SACT", at("SACT", ("asq", t, c)))], v_rs,
           tinc=("SVEC", 1))
        inc("SVEC", 1, ("vrs", t, c))

        w = [("SVEC", at("SVEC", ("vrs", t, c)))]
        if (t, c) >= (0, 2):
            pc = (t, c - 2) if c >= 2 else (t - 1, NCH - 2 + c)
            w.append(("SXNW", at("SXNW", ("xnw", pc[0], pc[1]))))
        em("scalar", w,
           (lambda e, c=c, hf=hf, ci=ci: (
               e.activation(out=XNB[c % 2][:, :],
                            in_=XB[hf][:, ci * D:(ci + 1) * D],
                            func=AF.Identity, scale=RSTD[:, 0:1],
                            bias=NMR[:, 0:1]),
               e.activation(out=JUNKA[:, 0:1], in_=XNB[c % 2][:, D - 1:D],
                            func=AF.Copy))[1]),
           tinc=("SACT", 1))
        inc("SACT", 1, ("aln", t, c))
        dma("scalar", [("SACT", at("SACT", ("aln", t, c)))],
            xn4[t, c], XNB[c % 2][:, :], "SXNW", ("xnw", t, c))

    def emit_xnt(t):
        bi = t % 2
        for dc in range(NDC):
            w = [("SXNW", at("SXNW", ("xnw", t, NCH - 1)))]
            if t >= 2:
                w.append(("SPE", at("SPE", ("mm", t - 2, NPT - 1))))
            dma("sync", w, XNT[bi][:, dc * TILE:(dc + 1) * TILE],
                XN[t * TILE:(t + 1) * TILE, dc * 128:(dc + 1) * 128],
                "SXNT", ("xnt", t, dc), transpose=True)

        def v_settle(e, bi=bi):
            r = None
            for dc in range(NDC):
                r = e.tensor_copy(JUNKA[:, 2:3],
                                  XNT[bi][:, (dc + 1) * TILE - 1:
                                          (dc + 1) * TILE])
            return r
        em("vector", [("SXNT", at("SXNT", ("xnt", t, NDC - 1)))], v_settle,
           tinc=("SVEC", 1))
        inc("SVEC", 1, ("vxs", t))

    def emit_mmaev(t, ep):
        bi = t % 2
        def pe_mm(e, ep=ep, bi=bi):
            r = None
            for dc in range(NDC):
                r = e.matmul(out=P_MM[ep % 2][:, :],
                             lhsT=WUT_SB[:, dc * INNER + ep * 128:
                                         dc * INNER + (ep + 1) * 128],
                             rhs=XNT[bi][:, dc * TILE:(dc + 1) * TILE],
                             start=(dc == 0), stop=(dc == NDC - 1))
            return r
        w = [("SVEC", at("SVEC", ("vxs", t)))]
        if t == 0 and ep < 2:
            w.append(("SACT", at("SACT", ("ahev", NPT - 1))))
        if (t, ep) >= (0, 2):
            pe_, ep_ = (t, ep - 2) if ep >= 2 else (t - 1, NPT - 2 + ep)
            w.append(("SACT", at("SACT", ("aev", pe_, ep_))))
        em("tensor", w, pe_mm, tinc=("SPE", 1))
        inc("SPE", 1, ("mm", t, ep))

        w = [("SPE", at("SPE", ("mm", t, ep)))]
        if t > 0:
            w.append(("SVEC", at("SVEC", ("vhc", t - 1, ep))))
        em("scalar", w,
           (lambda e, ep=ep: (e.activation(
               out=XMIN[:, ep * 520 + 3: ep * 520 + 515],
               in_=P_MM[ep % 2][:, :], func=AF.Identity,
               bias=BUPM[:, ep:ep + 1]),
               e.activation(out=JUNKA[:, 0:1],
                            in_=XMIN[:, ep * 520 + 514: ep * 520 + 515],
                            func=AF.Copy))[1]),
           tinc=("SACT", 1))
        inc("SACT", 1, ("aev", t, ep))

    def emit_cvsilu(t, pt):
        def pe_cv(e, pt=pt):
            r = None
            for k in range(4):
                j = pt * 4 + k
                r = e.matmul(out=P_CV[pt % 2][:, :],
                             lhsT=DIAG[:, j * 128:(j + 1) * 128],
                             rhs=XMIN[:, pt * 520 + k: pt * 520 + k + 512],
                             start=(k == 0), stop=(k == 3))
            return r
        w = [("SACT", at("SACT", ("aev", t, pt))),
             ("SVEC", at("SVEC", "vdiag"))]
        if (t, pt) >= (0, 2):
            pe_, pt_ = (t, pt - 2) if pt >= 2 else (t - 1, NPT - 2 + pt)
            w.append(("SACT", at("SACT", ("asilu", pe_, pt_))))
        em("tensor", w, pe_cv, tinc=("SPE", 1))
        inc("SPE", 1, ("cv", t, pt))

        w = [("SPE", at("SPE", ("cv", t, pt)))]
        if t > 0:
            w.append(("SPE", at("SPE", ("gt", t - 1, pt))))
            w.append(("SAG", at("SAG", ("ag", t - 1))))
        em("scalar", w,
           (lambda e, pt=pt: (e.activation(
               out=AGT[:, pt * TILE:(pt + 1) * TILE],
               in_=P_CV[pt % 2][:, :], func=AF.Silu,
               bias=CBS[:, pt:pt + 1]),
               e.activation(out=JUNKA[:, 0:1],
                            in_=AGT[:, (pt + 1) * TILE - 1:(pt + 1) * TILE],
                            func=AF.Copy))[1]),
           tinc=("SACT", 1))
        inc("SACT", 1, ("asilu", t, pt))

    def emit_vhc(t, pt):
        def v_hc(e, pt=pt):
            return e.tensor_copy(XMIN[:, pt * 520: pt * 520 + 3],
                                 XMIN[:, pt * 520 + 512: pt * 520 + 515])
        em("vector", [("SPE", at("SPE", ("cv", t, pt)))], v_hc,
           tinc=("SVEC", 1))
        inc("SVEC", 1, ("vhc", t, pt))

    def emit_spill(t):
        dma("scalar", [("SACT", at("SACT", ("asilu", t, NPT - 1)))],
            AG[:, :, t * TILE:(t + 1) * TILE].rearrange("q p s -> p q s"),
            AGT[:, :].rearrange("p (q s) -> p q s", q=NPT), "SAG", ("ag", t))

    def prevslot(t, pt, d):
        k = t * NPT + pt - d
        return (k // NPT, k % NPT) if k >= 0 else None

    def emit_gt(t, pt):
        def pe_gt(e, pt=pt):
            e.matmul(out=P_G[:, 0:512],
                     lhsT=BDF_SB[:, pt * 128:(pt + 1) * 128],
                     rhs=AGT[:, pt * TILE:(pt + 1) * TILE],
                     start=True, stop=True)
            e.matmul(out=P_G[:, 512:1024],
                     lhsT=BDI_SB[:, pt * 128:(pt + 1) * 128],
                     rhs=AGT[:, pt * TILE:(pt + 1) * TILE],
                     start=True, stop=True)
            return e.matmul(out=P_G[:, 1024:1536],
                            lhsT=BDH_SB[:, pt * 128:(pt + 1) * 128],
                            rhs=AGT[:, pt * TILE:(pt + 1) * TILE],
                            start=True, stop=True)
        w = [("SACT", at("SACT", ("asilu", t, pt)))]
        p1 = prevslot(t, pt, 1)
        if p1:
            w.append(("SACT", at("SACT", ("a1",) + p1)))
            w.append(("SVEC", at("SVEC", ("v1",) + p1)))
        em("tensor", w, pe_gt, tinc=("SPE", 1))
        inc("SPE", 1, ("gt", t, pt))

    def emit_a1(t, pt):
        pb = pt % 2
        def a_1(e, pb=pb):
            e.activation(out=FIH[pb][:, :], in_=P_G[:, :], func=AF.Exp,
                         scale=-1.0)
            e.activation(out=LNH[:, :], in_=FIH[pb][:, 1024:1536],
                         func=AF.Ln, bias=CONS[:, 1:2])
            e.activation(out=GB[:, :], in_=LNH[:, :], func=AF.Exp,
                         scale=-1.0)
            return e.activation(out=JUNKA[:, 0:2],
                                in_=FIH[pb][:, 1022:1024], func=AF.Copy)
        w = [("SPE", at("SPE", ("gt", t, pt)))]
        p2 = prevslot(t, pt, 2)
        if p2:
            w.append(("SVEC", at("SVEC", ("v2",) + p2)))
        em("scalar", w, a_1, tinc=("SACT", 1))
        inc("SACT", 1, ("a1", t, pt))

    def emit_a3(t, pt):
        def a_3(e, pt=pt):
            return e.activation(out=LNH[:, :], in_=FB[pt % 2][:, :], func=AF.Ln,
                                accum_out=LNFACC[:, pt:pt + 1])
        em("scalar", [("SVEC", at("SVEC", ("v2", t, pt)))], a_3,
           tinc=("SACT", 1))
        inc("SACT", 1, ("a3", t, pt))

    def emit_a2(t, pt):
        pb = pt % 2
        def a_2(e, pb=pb):
            e.activation(out=LNH[:, :], in_=UB[pb][:, :], func=AF.Ln,
                         bias=CONS[:, 3:4])
            e.activation(out=WB[pb][:, :], in_=LNH[:, :], func=AF.Exp,
                         scale=-1.0)
            return e.activation(out=JUNKA[:, 0:1], in_=WB[pb][:, 511:512],
                                func=AF.Copy)
        em("scalar", [("SVEC", at("SVEC", ("v1", t, pt)))], a_2,
           tinc=("SACT", 1))
        inc("SACT", 1, ("a2", t, pt))

    def emit_v1(t, pt):
        pb = pt % 2
        p3 = prevslot(t, pt, 3)
        def v_1(e, pb=pb, p3=p3):
            if p3:
                pp = p3[1]
                e.tensor_tensor(out=PFSUM[:, pp:pp + 1],
                                in0=PFSUM[:, pp:pp + 1],
                                in1=LNFACC[:, pp:pp + 1], op=OP.add)
            e.tensor_tensor(out=UB[pb][:, :], in0=FIH[pb][:, 0:512],
                            in1=FIH[pb][:, 512:1024], op=OP.add)
            e.tensor_tensor(out=JUNKV[:, 3:4], in0=UB[pb][:, 511:512],
                            in1=UB[pb][:, 511:512], op=OP.add)
            return e.scalar_tensor_tensor(out=GHB[pb][:, :],
                                          in0=P_G[:, 1024:1536],
                                          scalar=GEPS, in1=GB[:, :],
                                          op0=OP.add, op1=OP.max)
        em("vector", [("SACT", at("SACT", ("a1", t, pt)))], v_1,
           tinc=("SVEC", 1))
        inc("SVEC", 1, ("v1", t, pt))

    def emit_v2(t, pt):
        pb = pt % 2
        def v_2(e, pt=pt, pb=pb):
            e.scalar_tensor_tensor(out=FB[pb][:, :], in0=FIH[pb][:, 512:1024],
                                   scalar=1.0, in1=WB[pb][:, :],
                                   op0=OP.add, op1=OP.mult)
            e.scalar_tensor_tensor(out=TB[:, :], in0=FIH[pb][:, 0:512],
                                   scalar=1.0, in1=WB[pb][:, :],
                                   op0=OP.add, op1=OP.mult)
            e.tensor_tensor(out=GGB[:, :], in0=TB[:, :], in1=GHB[pb][:, :],
                            op=OP.mult)
            e.tensor_tensor_scan(out=RSOUT[:, :], data0=FB[pb][:, :],
                                 data1=GGB[:, :],
                                 initial=RSTATE[:, pt:pt + 1],
                                 op0=OP.mult, op1=OP.add)
            e.tensor_copy(RSTATE[:, pt:pt + 1], RSOUT[:, 511:512])
            return e.tensor_tensor(out=JUNKV[:, 4:5], in0=FB[pb][:, 511:512],
                                   in1=FB[pb][:, 511:512], op=OP.add)
        em("vector", [("SACT", at("SACT", ("a2", t, pt)))], v_2,
           tinc=("SVEC", 1))
        inc("SVEC", 1, ("v2", t, pt))

    def emit_chain_slot(t, pt):
        emit_gt(t, pt)
        p1 = prevslot(t, pt, 1)
        if p1:
            emit_a2(*p1)
        emit_a1(t, pt)
        p2 = prevslot(t, pt, 2)
        if p2:
            emit_a3(*p2)
        emit_v1(t, pt)
        if p1:
            emit_v2(*p1)

    def emit_prep(t, slot):
        if slot == 0:
            emit_spill_t = t - 1
            if emit_spill_t >= 0:
                emit_spill(emit_spill_t)
            emit_xload(t, 0)
            emit_ln(t, 0)
        elif slot == 1:
            emit_ln(t, 1)
        elif slot == 2:
            emit_xload(t, 1)
            emit_ln(t, 2)
        elif slot == 3:
            emit_ln(t, 3)
        elif slot == 4:
            emit_xnt(t)
        elif 5 <= slot <= 12:
            for ep in (2 * (slot - 5), 2 * (slot - 5) + 1):
                emit_mmaev(t, ep)
        if 8 <= slot <= 15:
            for pt2 in (2 * (slot - 8), 2 * (slot - 8) + 1):
                emit_cvsilu(t, pt2)

    for slot in range(16):
        emit_prep(0, slot)

    for t in range(nt):
        for pt in range(NPT):
            emit_chain_slot(t, pt)
            if t < nt - 1:
                if pt < 4:
                    for pt2 in range(4 * pt, 4 * pt + 4):
                        emit_vhc(t, pt2)
                emit_prep(t + 1, pt)
        if t == nt - 1:
            emit_spill(nt - 1)
            L = (nt - 1, NPT - 1)
            emit_a3(*prevslot(nt - 1, NPT - 1, 1))
            emit_a2(*L)
            emit_v2(*L)
            emit_a3(*L)

    def v_ccb(e):
        for pp in (NPT - 3, NPT - 2, NPT - 1):
            e.tensor_tensor(out=PFSUM[:, pp:pp + 1], in0=PFSUM[:, pp:pp + 1],
                            in1=LNFACC[:, pp:pp + 1], op=OP.add)
        e.tensor_copy(CCB[:, 0:16], RSTATE[:, :])
        return e.tensor_tensor(out=JUNKV[:, 0:16], in0=CCB[:, 0:16],
                               in1=CCB[:, 0:16], op=OP.add)
    em("vector", [("SVEC", at("SVEC", ("v2", nt - 1, NPT - 1))),
                  ("SACT", at("SACT", ("a3", nt - 1, NPT - 1))),
                  ("SACT", at("SACT", ("a3", nt - 1, NPT - 2))),
                  ("SACT", at("SACT", ("a3", nt - 1, NPT - 3)))], v_ccb,
       tinc=("SVEC", 1))
    inc("SVEC", 1, "vccb")

    em("scalar", [("SVEC", at("SVEC", "vccb"))],
       (lambda e: (e.activation(out=CCB[:, 16:32], in_=PFSUM[:, :], func=AF.Exp,
                                bias=CONS[:, 2:3]),
                   e.activation(out=JUNKA[:, 0:16], in_=CCB[:, 16:32],
                                func=AF.Copy))[1]),
       tinc=("SACT", 1))
    inc("SACT", 1, "apf")

    dma("gpsimd", [("SVEC", at("SVEC", "vccb")), ("SACT", at("SACT", "apf"))],
        CCIN, CCB[:, :], "SGP", "ccin")

    def g_cc(e):
        return e.collective_compute(
            "AllGather", OP.bypass, replica_groups=[list(range(8))],
            ins=[CCIN.opt()], outs=[CCOUT.opt()])
    em("gpsimd", [("SGP", at("SGP", "ccin"))], g_cc, tinc=("SCC", 1))
    inc("SCC", 1, "cc")

    dma("gpsimd", [("SCC", at("SCC", "cc"))],
        GATH[:, :].rearrange("p (r c) -> p r c", r=8),
        CCOUT.rearrange("(r p) c -> p r c", p=128), "SGP", "gath")

    def v_hlast(e):
        e.tensor_scalar(out=SEL1[:, :], in0=GATH[:, 0:32],
                        scalar1=PAIRM[:, 0:1], scalar2=None, op0=OP.mult)
        e.tensor_scalar(out=SEL2[:, :], in0=GATH[:, 0:32],
                        scalar1=PAIRM[:, 8:9], scalar2=None, op0=OP.mult)
        for r in range(1, 8):
            e.scalar_tensor_tensor(out=SEL1[:, :],
                                   in0=GATH[:, 32 * r:32 * (r + 1)],
                                   scalar=PAIRM[:, r:r + 1], in1=SEL1[:, :],
                                   op0=OP.mult, op1=OP.add)
            e.scalar_tensor_tensor(out=SEL2[:, :],
                                   in0=GATH[:, 32 * r:32 * (r + 1)],
                                   scalar=PAIRM[:, 8 + r:9 + r], in1=SEL2[:, :],
                                   op0=OP.mult, op1=OP.add)
        e.tensor_tensor(out=TMP16[:, :], in0=SEL2[:, 16:32], in1=SEL1[:, 0:16],
                        op=OP.mult)
        e.tensor_tensor(out=HL[:, :], in0=SEL2[:, 0:16], in1=TMP16[:, :],
                        op=OP.add)
        return e.tensor_tensor(out=JUNKV[:, 0:16], in0=HL[:, :],
                               in1=HL[:, :], op=OP.add)
    em("vector", [("SGP", at("SGP", "gath"))], v_hlast, tinc=("SVEC", 1))
    inc("SVEC", 1, "vhl")

    sag_full = cnt["SAG"]
    st1.close()

    cur[0] = "ph2"

    WDT_SB = sb(st, "wdt", [128, 16 * D], BF16)
    WRT = sb(st, "wrt", [128, 4 * 512], BF16)
    WIT = sb(st, "wit", [128, 4 * 512], BF16)
    WINEG = sb(st, "wineg", [128, 4 * 512], BF16)
    UCH = [sb(st, "uch0", [128, TILE], BF16), sb(st, "uch1", [128, TILE], BF16)]
    ACTT = [sb(st, "actt0", [128, TILE], BF16), sb(st, "actt1", [128, TILE], BF16)]
    T2B = sb(st, "t2b", [128, TILE], BF16)
    YT = sb(st, "yt", [128, NPT * TILE], BF16)
    XBC = [sb(st, "xbc0", [128, D], F32), sb(st, "xbc1", [128, D], F32)]
    R1B = [sb(st, "r10", [128, NCH * D], F32),
           sb(st, "r11", [128, NCH * D], F32)]
    XN2B = sb(st, "xn2b", [128, NCH * D], BF16)
    XN2T = [sb(st, "xn2t0", [128, NDC * TILE], BF16),
            sb(st, "xn2t1", [128, NDC * TILE], BF16)]
    OUTC = [sb(st, "outc0", [128, D], F32), sb(st, "outc1", [128, D], F32)]

    P2_Z = [psm(st, "pz0", [128, TILE]), psm(st, "pz1", [128, TILE])]
    P2_AT = [psm(st, "pat0", [128, D]), psm(st, "pat1", [128, D])]
    P2_FR = psm(st, "pfr", [128, 512])
    P2_FI = psm(st, "pfi", [128, 512])

    for dc in range(NDC):
        dma("gpsimd", [], WUT_SB[:, dc * INNER:(dc + 1) * INNER],
            WUT[dc * 128:(dc + 1) * 128, INNER:2 * INNER], "SW")
    inc("SW", 0, "wz")
    for ec in range(16):
        dma("gpsimd", [], WDT_SB[:, ec * D:(ec + 1) * D],
            WDT[ec * 128:(ec + 1) * 128, :], "SW")
    inc("SW", 0, "wdt")
    for w_sb, od in ((WRT, ODR), (WIT, ODI)):
        od4 = od.rearrange("(d b) (g e) -> d b g e", b=32, e=16)
        for d in range(32):
            pp, r0 = d // 8, (d % 8) * 16
            dma("gpsimd", [("SMI", at("SMI", ("wbod", wb_last)))],
                w_sb[r0:r0 + 16, pp * 512:(pp + 1) * 512]
                .rearrange("g (b e) -> g b e", b=32),
                od4[d].rearrange("b g e -> g b e"), "SW")
    inc("SW", 0, "wffn")

    def v_wineg(e):
        return e.tensor_scalar(out=WINEG[:, :], in0=WIT[:, :], scalar1=-1.0,
                               scalar2=None, op0=OP.mult)
    em("vector", [("SW", at("SW", "wffn"))], v_wineg, tinc=("SVEC", 1))
    inc("SVEC", 1, "vwineg")

    xn2_4 = XN2.rearrange("(t c p) d -> t c p d", c=NCH, p=128)

    def emit_ffn_block(t):
        bi = t % 2
        def v_settle3(e, bi=bi):
            r = None
            for dc in range(NDC):
                r = e.tensor_copy(JUNKA[:, 2:3],
                                  XN2T[bi][:, (dc + 1) * TILE - 1:
                                           (dc + 1) * TILE])
            return r
        em("vector", [("SXN2T", at("SXN2T", ("xn2t", t, NDC - 1)))], v_settle3,
           tinc=("SVEC", 1))
        inc("SVEC", 1, ("vxs3", t))

        for c in range(NCH):
            def pe_ffn(e, c=c, bi=bi):
                for dc in range(4):
                    e.matmul(out=P2_FR[:, :],
                             lhsT=XN2T[bi][:, dc * TILE + c * 128:
                                           dc * TILE + (c + 1) * 128],
                             rhs=WRT[:, dc * 512:(dc + 1) * 512],
                             start=(dc == 0), stop=False)
                for dc in range(4):
                    e.matmul(out=P2_FR[:, :],
                             lhsT=XN2T[bi][:, (4 + dc) * TILE + c * 128:
                                           (4 + dc) * TILE + (c + 1) * 128],
                             rhs=WINEG[:, dc * 512:(dc + 1) * 512],
                             start=False, stop=False)
                e.matmul(out=P2_FR[:, :], lhsT=ONESB[0:1, :], rhs=BRB[0:1, :],
                         start=False, stop=True)
                for dc in range(4):
                    e.matmul(out=P2_FI[:, :],
                             lhsT=XN2T[bi][:, dc * TILE + c * 128:
                                           dc * TILE + (c + 1) * 128],
                             rhs=WIT[:, dc * 512:(dc + 1) * 512],
                             start=(dc == 0), stop=False)
                for dc in range(4):
                    e.matmul(out=P2_FI[:, :],
                             lhsT=XN2T[bi][:, (4 + dc) * TILE + c * 128:
                                           (4 + dc) * TILE + (c + 1) * 128],
                             rhs=WRT[:, dc * 512:(dc + 1) * 512],
                             start=False, stop=False)
                return e.matmul(out=P2_FI[:, :], lhsT=ONESB[0:1, :],
                                rhs=BIB[0:1, :], start=False, stop=True)
            w = [("SVEC", at("SVEC", ("vxs3", t))),
                 ("SW", at("SW", "wffn")), ("SVEC", at("SVEC", "vwineg"))]
            if (t, c) >= (0, 1):
                pe_, pc_ = (t, c - 1) if c >= 1 else (t - 1, NCH - 1)
                w.append(("SVEC", at("SVEC", ("vfin", pe_, pc_))))
            em("tensor", w, pe_ffn, tinc=("SPE", 1))
            inc("SPE", 1, ("ffn", t, c))

            def v_fin(e, c=c, bi=bi):
                e.tensor_tensor(out=OUTC[c % 2][:, 0:512],
                                in0=R1B[bi][:, c * D: c * D + 512],
                                in1=P2_FR[:, :], op=OP.add)
                e.tensor_tensor(out=OUTC[c % 2][:, 512:1024],
                                in0=R1B[bi][:, c * D + 512: c * D + 1024],
                                in1=P2_FI[:, :], op=OP.add)
                return e.tensor_tensor(out=JUNKV[:, 5:6],
                                       in0=OUTC[c % 2][:, 1023:1024],
                                       in1=OUTC[c % 2][:, 1023:1024],
                                       op=OP.add)
            w = [("SPE", at("SPE", ("ffn", t, c)))]
            if (t, c) >= (0, 2):
                pe_, pc_ = (t, c - 2) if c >= 2 else (t - 1, NCH - 2 + c)
                w.append(("SOUT", at("SOUT", ("out", pe_, pc_))))
            em("vector", w, v_fin, tinc=("SVEC", 1))
            inc("SVEC", 1, ("vfin", t, c))

            dma("scalar", [("SVEC", at("SVEC", ("vfin", t, c)))],
                OUT[(t * NCH + c) * 128:(t * NCH + c + 1) * 128, :],
                OUTC[c % 2][:, :], "SOUT", ("out", t, c))



    for t in range(nt):
        bi = t % 2
        for dc in range(NDC):
            w = []
            if t >= 2:
                w.append(("SPE", at("SPE", ("z", t - 2, NPT - 1))))
            dma("sync", w, XNT[bi][:, dc * TILE:(dc + 1) * TILE],
                XN[t * TILE:(t + 1) * TILE, dc * 128:(dc + 1) * 128],
                "SXNT", ("xnt2", t, dc), transpose=True)

        def v_settle2(e, bi=bi):
            r = None
            for dc in range(NDC):
                r = e.tensor_copy(JUNKA[:, 2:3],
                                  XNT[bi][:, (dc + 1) * TILE - 1:
                                          (dc + 1) * TILE])
            return r
        em("vector", [("SXNT", at("SXNT", ("xnt2", t, NDC - 1)))], v_settle2,
           tinc=("SVEC", 1))
        inc("SVEC", 1, ("vxs2", t))

        def emit_actld(t, pt):
            w = []
            if t == 0 and pt < 2:
                w.append(("SAG", sag_full))
            if (t, pt) >= (0, 2):
                pe_, pt_ = (t, pt - 2) if pt >= 2 else (t - 1, NPT - 2 + pt)
                w.append(("SACT", at("SACT", ("at2", pe_, pt_))))
            dma("scalar", w, ACTT[pt % 2][:, :],
                AG[pt, :, t * TILE:(t + 1) * TILE], "SACTLD", ("actld", t, pt))

        emit_actld(t, 0)
        emit_actld(t, 1)

        for pt in range(NPT):
            def pe_z(e, pt=pt, bi=bi):
                r = None
                for dc in range(NDC):
                    r = e.matmul(out=P2_Z[pt % 2][:, :],
                                 lhsT=WUT_SB[:, dc * INNER + pt * 128:
                                             dc * INNER + (pt + 1) * 128],
                                 rhs=XNT[bi][:, dc * TILE:(dc + 1) * TILE],
                                 start=(dc == 0), stop=(dc == NDC - 1))
                return r
            w = [("SVEC", at("SVEC", ("vxs2", t))), ("SW", at("SW", "wz"))]
            if (t, pt) >= (0, 2):
                pe_, pt_ = (t, pt - 2) if pt >= 2 else (t - 1, NPT - 2 + pt)
                w.append(("SACT", at("SACT", ("au", pe_, pt_))))
            em("tensor", w, pe_z, tinc=("SPE", 1))
            inc("SPE", 1, ("z", t, pt))

            w = [("SPE", at("SPE", ("z", t, pt)))]
            if (t, pt) >= (0, 2):
                pe_, pt_ = (t, pt - 2) if pt >= 2 else (t - 1, NPT - 2 + pt)
                w.append(("SVEC", at("SVEC", ("vy", pe_, pt_))))
            em("scalar", w,
               (lambda e, pt=pt: (e.activation(out=UCH[pt % 2][:, :],
                                               in_=P2_Z[pt % 2][:, :],
                                               func=AF.Silu,
                                               bias=BUPZ[:, pt:pt + 1]),
                                  e.activation(out=JUNKA[:, 0:1],
                                               in_=UCH[pt % 2][:, 511:512],
                                               func=AF.Copy))[1]),
               tinc=("SACT", 1))
            inc("SACT", 1, ("au", t, pt))

            w = [("SACTLD", at("SACTLD", ("actld", t, pt))),
                 ("SVEC", at("SVEC", "vhl"))]
            if (t, pt) >= (0, 1):
                pe_, pt_ = (t, pt - 1) if pt >= 1 else (t - 1, NPT - 1)
                w.append(("SVEC", at("SVEC", ("vy", pe_, pt_))))
            em("scalar", w,
               (lambda e, pt=pt: (e.activation(out=T2B[:, :],
                                               in_=ACTT[pt % 2][:, :],
                                               func=AF.Identity,
                                               scale=SKIPS[:, pt:pt + 1],
                                               bias=HL[:, pt:pt + 1]),
                                  e.activation(out=JUNKA[:, 0:1],
                                               in_=T2B[:, 511:512],
                                               func=AF.Copy))[1]),
               tinc=("SACT", 1))
            inc("SACT", 1, ("at2", t, pt))
            if pt + 2 < NPT:
                emit_actld(t, pt + 2)

            def v_y(e, pt=pt):
                e.tensor_tensor(out=YT[:, pt * TILE:(pt + 1) * TILE],
                                in0=T2B[:, :], in1=UCH[pt % 2][:, :],
                                op=OP.mult)
                return e.tensor_tensor(out=JUNKV[:, 2:3],
                                       in0=YT[:, (pt + 1) * TILE - 1:
                                              (pt + 1) * TILE],
                                       in1=YT[:, (pt + 1) * TILE - 1:
                                              (pt + 1) * TILE], op=OP.add)
            w = [("SACT", at("SACT", ("at2", t, pt))),
                 ("SACT", at("SACT", ("au", t, pt)))]
            if t >= 1:
                w.append(("SPE", at("SPE", ("dn", t - 1, NCH - 1))))
            em("vector", w, v_y, tinc=("SVEC", 1))
            inc("SVEC", 1, ("vy", t, pt))

        if t >= 1:
            emit_ffn_block(t - 1)

        def emit_x2(t, c):
            w = []
            if (t, c) >= (0, 2):
                pe_, pc_ = (t, c - 2) if c >= 2 else (t - 1, NCH - 2 + c)
                w.append(("SVEC", at("SVEC", ("vr1", pe_, pc_))))
            dma("sync", w, XBC[c % 2][:, :], xs4[t, c], "SX", ("x2", t, c))

        emit_x2(t, 0)
        emit_x2(t, 1)

        for c in range(NCH):
            def pe_dn(e, c=c):
                r = None
                for ec in range(16):
                    for n in range(2):
                        r = e.matmul(out=P2_AT[c % 2][:, n * 512:(n + 1) * 512],
                                     lhsT=YT[:, ec * TILE + c * 128:
                                             ec * TILE + (c + 1) * 128],
                                     rhs=WDT_SB[:, ec * D + n * 512:
                                                ec * D + (n + 1) * 512],
                                     start=(ec == 0), stop=False)
                for n in range(2):
                    r = e.matmul(out=P2_AT[c % 2][:, n * 512:(n + 1) * 512],
                                 lhsT=ONESB[0:1, :],
                                 rhs=BDOWNB[0:1, n * 512:(n + 1) * 512],
                                 start=False, stop=True)
                return r
            w = [("SVEC", at("SVEC", ("vy", t, NPT - 1))),
                 ("SW", at("SW", "wdt"))]
            if (t, c) >= (0, 2):
                pe_, pc_ = (t, c - 2) if c >= 2 else (t - 1, NCH - 2 + c)
                w.append(("SVEC", at("SVEC", ("vr1", pe_, pc_))))
            em("tensor", w, pe_dn, tinc=("SPE", 1))
            inc("SPE", 1, ("dn", t, c))

            def v_r1(e, c=c, bi=bi):
                return e.tensor_tensor(out=R1B[bi][:, c * D:(c + 1) * D],
                                       in0=XBC[c % 2][:, :],
                                       in1=P2_AT[c % 2][:, :], op=OP.add)
            em("vector", [("SPE", at("SPE", ("dn", t, c))),
                          ("SX", at("SX", ("x2", t, c)))], v_r1,
               tinc=("SVEC", 1))
            inc("SVEC", 1, ("vr1", t, c))
            if c + 2 < NCH:
                emit_x2(t, c + 2)

            def v_st2(e, c=c, bi=bi):
                return stats_ops(e, R1B[bi][:, c * D:(c + 1) * D], 128)
            w = [("SVEC", at("SVEC", ("vr1", t, c)))]
            if (t, c) != (0, 0):
                pc = (t, c - 1) if c > 0 else (t - 1, NCH - 1)
                w.append(("SACT", at("SACT", ("aln2", pc[0], pc[1]))))
            em("vector", w, v_st2, tinc=("SVEC", 1))
            inc("SVEC", 1, ("vst2", t, c))

            em("scalar", [("SVEC", at("SVEC", ("vst2", t, c)))],
               (lambda e: (e.activation(out=SD1[:, 0:1], in_=VARB[:, 0:1],
                                        func=AF.Ln, bias=CONS[:, 0:1]),
                           e.activation(out=RSTD[:, 0:1], in_=SD1[:, 0:1],
                                        func=AF.Exp, scale=-0.5),
                           e.activation(out=JUNKA[:, 0:1], in_=RSTD[:, 0:1],
                                        func=AF.Copy))[2]),
               tinc=("SACT", 1))
            inc("SACT", 1, ("asq2", t, c))

            def v_rs2(e):
                e.scalar_tensor_tensor(out=NMR[:, 0:1], in0=MEANB[:, 0:1],
                                       scalar=-1.0, in1=RSTD[:, 0:1],
                                       op0=OP.mult, op1=OP.mult)
                return e.tensor_tensor(out=JUNKV[:, 1:2], in0=RSTD[:, 0:1],
                                       in1=NMR[:, 0:1], op=OP.add)
            em("vector", [("SACT", at("SACT", ("asq2", t, c)))], v_rs2,
               tinc=("SVEC", 1))
            inc("SVEC", 1, ("vrs2", t, c))

            w = [("SVEC", at("SVEC", ("vrs2", t, c)))]
            if c == 0 and t > 0:
                w.append(("SXN2W", at("SXN2W", ("xn2w", t - 1))))
            em("scalar", w,
               (lambda e, c=c, bi=bi: (e.activation(out=XN2B[:, c * D:(c + 1) * D],
                                             in_=R1B[bi][:, c * D:(c + 1) * D],
                                             func=AF.Identity,
                                             scale=RSTD[:, 0:1],
                                             bias=NMR[:, 0:1]),
                                e.activation(out=JUNKA[:, 0:1],
                                             in_=XN2B[:, (c + 1) * D - 1:
                                                     (c + 1) * D],
                                             func=AF.Copy))[1]),
               tinc=("SACT", 1))
            inc("SACT", 1, ("aln2", t, c))

        dma("scalar", [("SACT", at("SACT", ("aln2", t, NCH - 1)))],
            xn2_4[t].rearrange("c p d -> p c d"),
            XN2B[:, :].rearrange("p (c d) -> p c d", c=NCH), "SXN2W", ("xn2w", t))
        for dc in range(NDC):
            w = [("SXN2W", at("SXN2W", ("xn2w", t)))]
            if t >= 2:
                w.append(("SPE", at("SPE", ("ffn", t - 2, NCH - 1))))
            dma("sync", w, XN2T[bi][:, dc * TILE:(dc + 1) * TILE],
                XN2[t * TILE:(t + 1) * TILE, dc * 128:(dc + 1) * 128],
                "SXN2T", ("xn2t", t, dc), transpose=True)

    emit_ffn_block(nt - 1)

    if dbg:
        DXN = nc.dram_tensor("DXN", [t_tok, D], BF16, kind="ExternalOutput").ap()
        DAG = nc.dram_tensor("DAG", [NPT, 128, t_tok], BF16,
                             kind="ExternalOutput").ap()
        DXN2 = nc.dram_tensor("DXN2", [t_tok, D], BF16,
                              kind="ExternalOutput").ap()
        DCCB = nc.dram_tensor("DCCB", [128, 32], F32, kind="ExternalOutput").ap()
        DGATH = nc.dram_tensor("DGATH", [128, 256], F32,
                               kind="ExternalOutput").ap()
        DHL = nc.dram_tensor("DHL", [128, NPT], F32, kind="ExternalOutput").ap()
        DWRT = nc.dram_tensor("DWRT", [128, 4 * 512], BF16,
                              kind="ExternalOutput").ap()
        fo = cnt["SOUT"]
        dma("gpsimd", [("SOUT", fo)], DXN, XN, "SGP")
        dma("gpsimd", [], DAG, AG, "SGP")
        dma("gpsimd", [], DXN2, XN2, "SGP")
        dma("gpsimd", [], DCCB, CCB[:, :], "SGP")
        dma("gpsimd", [], DGATH, GATH[:, :], "SGP")
        dma("gpsimd", [], DHL, HL[:, :], "SGP")
        dma("gpsimd", [], DWRT, WRT[:, :], "SGP")
        em("gpsimd", [("SGP", cnt["SGP"])], lambda e: e.nop())

    final_out = cnt["SOUT"]
    em("scalar", [("SOUT", final_out)], lambda e: e.nop())

    for name in ("ph1", "ph2"):
        pl = plans[name]
        with nc.Block(name) as blk:
            @blk.sync
            def _(e, pl=pl):
                for eng, fn in pl:
                    if eng == "sync":
                        fn(e)

            @blk.gpsimd
            def _(e, pl=pl):
                for eng, fn in pl:
                    if eng == "gpsimd":
                        fn(e)

            @blk.tensor
            def _(e, pl=pl):
                for eng, fn in pl:
                    if eng == "tensor":
                        fn(e)

            @blk.scalar
            def _(e, pl=pl):
                for eng, fn in pl:
                    if eng == "scalar":
                        fn(e)

            @blk.vector
            def _(e, pl=pl):
                for eng, fn in pl:
                    if eng == "vector":
                        fn(e)

    st.close()
    return nc


def _host_inputs(inputs, core):
    b, h = core // 2, core % 2
    x = inputs["L0_attn_x"]
    w_up = inputs["L0_attn_w_up"]
    conv_w = inputs["L0_attn_conv_w"][:, 0, :]
    start = h * T
    xs = np.ascontiguousarray(x[b, start:start + T])
    if h == 0:
        xh = np.zeros((3, D), np.float32)
        halom = np.zeros((128, 1), np.float32)
    else:
        xh = np.ascontiguousarray(x[b, start - 3:start])
        halom = np.ones((128, 1), np.float32)
    pairm = np.zeros((128, 16), np.float32)
    pairm[:, 2 * b] = 1.0
    pairm[:, 8 + 2 * b + 1] = 1.0

    def pcol(v):
        return np.ascontiguousarray(v.reshape(NPT, 128).T)

    bd = {}
    for nm, w in (("BDF", inputs["L0_attn_w_f"]), ("BDI", inputs["L0_attn_w_i"]),
                  ("BDH", inputs["L0_attn_w_h"])):
        m = np.zeros((NPT, 128, 128), np.float32)
        for g in range(128):
            tt, j = g // 8, g % 8
            m[tt, 16 * j:16 * (j + 1), 16 * j:16 * (j + 1)] = w[g].T
        bd[nm] = np.ascontiguousarray(m.transpose(1, 0, 2).reshape(128, NPT * 128))

    g1r = inputs["L0_ffn_G1_R"][0]
    g2r = inputs["L0_ffn_G2_R"][:, :, 0, :]
    g1i = inputs["L0_ffn_G1_I"][0]
    g2i = inputs["L0_ffn_G2_I"][:, :, 0, :]

    def g1pack(g1):
        return np.ascontiguousarray(g1.transpose(1, 2, 0).reshape(1, 4096))

    def g2pack(g2):
        return np.ascontiguousarray(g2.transpose(0, 2, 1).reshape(1, 1024))

    cwm = np.ascontiguousarray(
        conv_w.reshape(NPT, 128, 4).transpose(1, 0, 2).reshape(128, 4 * NPT))

    return {
        "XS": xs, "XH": xh,
        "WUT": np.ascontiguousarray(w_up.T),
        "WDT": np.ascontiguousarray(inputs["L0_attn_w_down"].T),
        "BDF": bd["BDF"], "BDI": bd["BDI"], "BDH": bd["BDH"],
        "CWD": cwm,
        "BUPM": pcol(inputs["L0_attn_b_up"][:INNER]),
        "BUPZ": pcol(inputs["L0_attn_b_up"][INNER:]),
        "CBD": pcol(inputs["L0_attn_conv_b"]),
        "SKIPD": pcol(inputs["L0_attn_skip"]),
        "HALOM": halom, "PAIRM": pairm,
        "EYE": np.eye(128, dtype=np.float32),
        "BDOWND": np.ascontiguousarray(inputs["L0_attn_b_down"].reshape(1, D)),
        "BRD": np.ascontiguousarray(inputs["L0_ffn_bias_real"].reshape(1, 512)),
        "BID": np.ascontiguousarray(inputs["L0_ffn_bias_imag"].reshape(1, 512)),
        "G1R": g1pack(g1r), "G2R": g2pack(g2r),
        "G1I": g1pack(g1i), "G2I": g2pack(g2i),
        "CONS": np.tile(np.array([[LNEPS, 1.0, T * EPS, 2.0]], np.float32),
                        (128, 1)),
    }


def kernel(**inputs):
    inputs = {k: np.ascontiguousarray(np.asarray(v)) for k, v in inputs.items()}
    nc = build_nc()
    in_maps = [_host_inputs(inputs, k) for k in range(8)]
    res = run_bass_kernel_spmd(nc, in_maps, core_ids=list(range(8)))
    out = np.empty((B, S, D), np.float32)
    for k in range(8):
        b, h = k // 2, k % 2
        out[b, h * T:(h + 1) * T] = res.results[k]["OUT"]
    return out




# version 14 (final)
# speedup vs baseline: 1.1402x; 1.0161x over previous
import numpy as np
from collections import defaultdict
from contextlib import ExitStack

import concourse.bass as bass
import concourse.mybir as mybir
from concourse.bass_utils import run_bass_kernel_spmd

F32 = mybir.dt.float32
BF16 = mybir.dt.bfloat16
AF = mybir.ActivationFunctionType
OP = mybir.AluOpType

B, S, D = 4, 8192, 1024
INNER = 2048
NPT = INNER // 128
NDC = D // 128
T = 4096
TILE = 512
NT = T // TILE
NCH = TILE // 128
EPS = 1e-8
LNEPS = 1e-5
GEPS = 0.5 + 1e-8


def build_nc(n_tiles=NT, dbg=False):
    nt = n_tiles
    t_tok = nt * TILE
    nc = bass.Bass("TRN2", target_bir_lowering=False,
                   detect_race_conditions=False)

    EI = dict(kind="ExternalInput")
    XS = nc.dram_tensor("XS", [t_tok, D], F32, **EI).ap()
    XH = nc.dram_tensor("XH", [3, D], F32, **EI).ap()
    WUT = nc.dram_tensor("WUT", [D, 2 * INNER], F32, **EI).ap()
    WDT = nc.dram_tensor("WDT", [INNER, D], F32, **EI).ap()
    BDF_D = nc.dram_tensor("BDF", [128, NPT * 128], F32, **EI).ap()
    BDI_D = nc.dram_tensor("BDI", [128, NPT * 128], F32, **EI).ap()
    BDH_D = nc.dram_tensor("BDH", [128, NPT * 128], F32, **EI).ap()
    CWD = nc.dram_tensor("CWD", [128, 4 * NPT], F32, **EI).ap()
    BUPM_D = nc.dram_tensor("BUPM", [128, NPT], F32, **EI).ap()
    BUPZ_D = nc.dram_tensor("BUPZ", [128, NPT], F32, **EI).ap()
    CB_D = nc.dram_tensor("CBD", [128, NPT], F32, **EI).ap()
    SKIP_D = nc.dram_tensor("SKIPD", [128, NPT], F32, **EI).ap()
    HALOM_D = nc.dram_tensor("HALOM", [128, 1], F32, **EI).ap()
    PAIRM_D = nc.dram_tensor("PAIRM", [128, 16], F32, **EI).ap()
    EYE_D = nc.dram_tensor("EYE", [128, 128], F32, **EI).ap()
    BDOWN_D = nc.dram_tensor("BDOWND", [1, D], F32, **EI).ap()
    BR_D = nc.dram_tensor("BRD", [1, 512], F32, **EI).ap()
    BI_D = nc.dram_tensor("BID", [1, 512], F32, **EI).ap()
    G1R_D = nc.dram_tensor("G1R", [1, 4096], F32, **EI).ap()
    G2R_D = nc.dram_tensor("G2R", [1, 1024], F32, **EI).ap()
    G1I_D = nc.dram_tensor("G1I", [1, 4096], F32, **EI).ap()
    G2I_D = nc.dram_tensor("G2I", [1, 1024], F32, **EI).ap()
    CONS_D = nc.dram_tensor("CONS", [128, 4], F32, **EI).ap()

    OUT = nc.dram_tensor("OUT", [t_tok, D], F32, kind="ExternalOutput").ap()

    XN = nc.dram_tensor("XNs", [t_tok, D], BF16).ap()
    XN2 = nc.dram_tensor("XN2s", [t_tok, D], BF16).ap()
    AG = nc.dram_tensor("AGs", [NPT, 128, t_tok], BF16).ap()
    ODR = nc.dram_tensor("ODRs", [1024, 256], BF16).ap()
    ODI = nc.dram_tensor("ODIs", [1024, 256], BF16).ap()
    CCIN = nc.dram_tensor("CCINs", [128, 32], F32).ap()
    CCOUT = nc.dram_tensor("CCOUTs", [8 * 128, 32], F32).ap()

    st = ExitStack()
    st1 = ExitStack()

    def sb(stk, name, shape, dt):
        return stk.enter_context(nc.sbuf_tensor(name, shape, dt))

    def psm(stk, name, shape, dt=F32):
        return stk.enter_context(nc.psum_tensor(name, shape, dt))

    WUT_SB = sb(st, "wut", [128, NDC * INNER], BF16)
    EYE = sb(st, "eye", [128, 128], F32)
    EYEB = sb(st, "eyeb", [128, 128], BF16)
    BUPM = sb(st, "bupm", [128, NPT], F32)
    BUPZ = sb(st, "bupz", [128, NPT], F32)
    CBS = sb(st, "cbs", [128, NPT], F32)
    SKIPS = sb(st, "skips", [128, NPT], F32)
    HALOM = sb(st, "halom", [128, 1], F32)
    PAIRM = sb(st, "pairm", [128, 16], F32)
    ONESB = sb(st, "onesb", [1, 128], BF16)
    BDOWNB = sb(st, "bdownb", [1, D], BF16)
    BRB = sb(st, "brb", [1, 512], BF16)
    BIB = sb(st, "bib", [1, 512], BF16)
    RSTATE = sb(st, "rstate", [128, NPT], F32)
    PFSUM = sb(st, "pfsum", [128, NPT], F32)
    LNFACC = sb(st, "lnfacc", [128, NPT], F32)
    CCB = sb(st, "ccb", [128, 32], F32)
    GATH = sb(st, "gath", [128, 256], F32)
    SEL1 = sb(st, "sel1", [128, 32], F32)
    SEL2 = sb(st, "sel2", [128, 32], F32)
    HL = sb(st, "hl", [128, NPT], F32)
    TMP16 = sb(st, "tmp16", [128, NPT], F32)
    STATB = sb(st, "statb", [128, 12], F32)
    MV = sb(st, "mv", [128, 2], F32)
    SD1 = sb(st, "sd1", [128, 1], F32)
    RSTD = sb(st, "rstd", [128, 1], F32)
    NMR = sb(st, "nmr", [128, 1], F32)
    CONS = sb(st, "cons", [128, 4], F32)
    SCRQ = sb(st, "scrq", [128, 1024], F32)
    SMB = sb(st, "smb", [128, 1], F32)
    SSQ = sb(st, "ssq", [128, 1], F32)
    MEANB = sb(st, "meanb", [128, 1], F32)
    MSQ = sb(st, "msq", [128, 1], F32)
    VARB = sb(st, "varb", [128, 1], F32)
    JUNKV = sb(st, "junkv", [128, 16], F32)
    JUNKA = sb(st, "junka", [128, 16], F32)
    XNT = [sb(st, "xnt0", [128, NDC * TILE], BF16),
           sb(st, "xnt1", [128, NDC * TILE], BF16)]

    BDF_SB = sb(st1, "bdf", [128, NPT * 128], BF16)
    BDI_SB = sb(st1, "bdi", [128, NPT * 128], BF16)
    BDH_SB = sb(st1, "bdh", [128, NPT * 128], BF16)
    DIAG = sb(st1, "diag", [128, 4 * NPT * 128], BF16)
    CWS = sb(st1, "cws", [128, 4 * NPT], F32)
    BUPMK = sb(st1, "bupmk", [128, NPT], F32)
    XMIN = sb(st1, "xmin", [128, NPT * 520], BF16)
    XB = [sb(st1, "xb0", [128, 2 * D], F32), sb(st1, "xb1", [128, 2 * D], F32)]
    XNB = [sb(st1, "xnb0", [128, D], BF16), sb(st1, "xnb1", [128, D], BF16)]
    AGT = sb(st1, "agt", [128, NPT * TILE], BF16)
    FIH = [sb(st1, "fih0", [128, 3 * TILE], F32),
           sb(st1, "fih1", [128, 3 * TILE], F32)]
    UB = [sb(st1, "ub0", [128, TILE], F32), sb(st1, "ub1", [128, TILE], F32)]
    WB = [sb(st1, "wb0", [128, TILE], F32), sb(st1, "wb1", [128, TILE], F32)]
    LNH = sb(st1, "lnh", [128, TILE], F32)
    GB = sb(st1, "gb", [128, TILE], F32)
    GHB = [sb(st1, "ghb0", [128, TILE], F32), sb(st1, "ghb1", [128, TILE], F32)]
    FB = [sb(st1, "fb0", [128, TILE], F32), sb(st1, "fb1", [128, TILE], F32)]
    TB = sb(st1, "tb", [128, TILE], F32)
    GGB = sb(st1, "ggb", [128, TILE], F32)
    RSOUT = sb(st1, "rsout", [128, TILE], F32)

    P_MM = [psm(st1, "pmm0", [128, TILE]), psm(st1, "pmm1", [128, TILE])]
    P_CV = [psm(st1, "pcv0", [128, TILE]), psm(st1, "pcv1", [128, TILE])]

    G1RS = sb(st1, "g1rs", [1, 4096], BF16)
    G2RS = sb(st1, "g2rs", [1, 1024], BF16)
    G1IS = sb(st1, "g1is", [1, 4096], BF16)
    G2IS = sb(st1, "g2is", [1, 1024], BF16)
    OSB = sb(st1, "osb", [128, 256], BF16)
    XHST = sb(st1, "xhst", [3, 12], F32)
    XHMV = sb(st1, "xhmv", [3, 2], F32)
    XHSD = sb(st1, "xhsd", [3, 1], F32)
    XHRS = sb(st1, "xhrs", [3, 1], F32)
    XHNM = sb(st1, "xhnm", [3, 1], F32)
    XHT = sb(st1, "xht", [128, NDC * 3], BF16)
    XHS = XB[0][0:3, 0:D]
    XHN = XB[0][0:3, D:2 * D]
    P_AUX = psm(st1, "paux", [128, 256])
    P_G = psm(st1, "pg", [128, 3 * TILE])

    sem_names = ["SX", "SXNT", "SXNW", "SAG", "SW", "SMI", "SCC", "SACTLD",
                 "SXN2W", "SXN2T", "SOUT", "SPE", "SACT", "SVEC", "SGP"]
    SEM = {n: st.enter_context(nc.semaphore(n)) for n in sem_names}

    cnt = defaultdict(int)
    mark = {}

    def inc(s, n, tag=None):
        if s in ("SVEC", "SACT") and last_nops:
            n = last_nops.pop()
        cnt[s] += n
        if tag is not None:
            mark[(s, tag)] = cnt[s]
        return cnt[s]

    def at(s, tag):
        return mark[(s, tag)]

    plans = {"ph1": [], "ph2": []}
    cur = ["ph1"]

    _WRAP = ("activation", "tensor_scalar", "tensor_tensor",
             "scalar_tensor_tensor", "tensor_tensor_scan", "tensor_copy",
             "tensor_reduce", "reciprocal", "memset", "memzero", "select",
             "copy", "add", "mul", "sign", "sqrt", "square", "reduce_max",
             "reduce_sum", "bn_stats", "bn_aggr")

    class _Counter:
        def __init__(self):
            self.n = 0

        def __getattr__(self, name):
            if name in _WRAP:
                def f(*a, **k):
                    self.n += 1
                    return self
                return f
            def g(*a, **k):
                return self
            return g

        def then_inc(self, *a, **k):
            return self

    GROUP_SEMS = False

    class _Proxy:
        def __init__(self, e, sem, base, nops):
            self._e = e
            self._sem = sem
            self._n = base
            self._base = base
            self._nops = nops

        def __getattr__(self, name):
            e = self._e
            if name in _WRAP:
                def f(*a, **k):
                    if not GROUP_SEMS:
                        e.wait_ge(self._sem, self._n)
                    elif self._n == self._base:
                        e.wait_ge(self._sem, self._base)
                    r = getattr(e, name)(*a, **k)
                    if not GROUP_SEMS or self._n == self._base + self._nops - 1:
                        r.then_inc(self._sem,
                                   1 if not GROUP_SEMS else self._nops)
                    self._n += 1
                    return r
                return f
            return getattr(e, name)

    _selfsem = {"vector": "SVEC", "scalar": "SACT"}
    last_nops = []

    def em(eng, waits, fn, tinc=None):
        w = [(SEM[s], v) for (s, v) in waits]
        if eng in _selfsem:
            c = _Counter()
            fn(c)
            last_nops.append(c.n)
            ssem = SEM[_selfsem[eng]]
            base = cnt[_selfsem[eng]]

            if tinc is not None and tinc[0] == _selfsem[eng]:
                tinc = None
            ti = (SEM[tinc[0]], tinc[1]) if tinc else None

            def run(e, w=w, fn=fn, ssem=ssem, base=base, nops=c.n, ti=ti):
                for sh, v in w:
                    e.wait_ge(sh, v)
                p = _Proxy(e, ssem, base, nops)
                r = fn(p)
                if ti is not None:
                    r.then_inc(ti[0], ti[1])
                assert p._n == base + nops, (
                    f"op count mismatch: {p._n} != {base}+{nops}")
            plans[cur[0]].append((eng, run))
            return

        ti = (SEM[tinc[0]], tinc[1]) if tinc else None

        def run(e, w=w, fn=fn, ti=ti):
            for sh, v in w:
                e.wait_ge(sh, v)
            r = fn(e)
            if ti is not None:
                r.then_inc(ti[0], ti[1])
        plans[cur[0]].append((eng, run))

    def _inc_orig(s, n, tag=None):
        cnt[s] += n
        if tag is not None:
            mark[(s, tag)] = cnt[s]
        return cnt[s]

    def dma(eng, waits, out_ap, in_ap, s, tag=None, transpose=False):
        inc(s, 16, tag)
        em(eng, waits,
           (lambda e, o=out_ap, i=in_ap, tr=transpose:
            e.dma_start(out=o, in_=i, transpose=tr)),
           tinc=(s, 16))

    if dbg:
        DXHT = nc.dram_tensor("DXHT", [128, NDC * 3], BF16,
                              kind="ExternalOutput").ap()
        DXHN = nc.dram_tensor("DXHN", [3, D], F32, kind="ExternalOutput").ap()
        DXHRS = nc.dram_tensor("DXHRS", [3, 2], F32,
                               kind="ExternalOutput").ap()
        DXMIN = nc.dram_tensor("DXMIN", [128, NPT * 520], BF16,
                               kind="ExternalOutput").ap()


    RD = 1.0 / float(D)

    def stats_ops(e, src_ap, np_):
        e.tensor_scalar(out=SCRQ[0:np_, :], in0=src_ap, scalar1=0.0,
                        scalar2=0.0, op0=OP.add, op1=OP.add,
                        accum_out=SMB[0:np_, 0:1])
        e.scalar_tensor_tensor(out=SCRQ[0:np_, :], in0=src_ap, scalar=1.0,
                               in1=src_ap, op0=OP.mult, op1=OP.mult,
                               accum_out=SSQ[0:np_, 0:1])
        e.tensor_scalar(out=MEANB[0:np_, 0:1], in0=SMB[0:np_, 0:1],
                        scalar1=RD, scalar2=None, op0=OP.mult)
        e.tensor_tensor(out=MSQ[0:np_, 0:1], in0=MEANB[0:np_, 0:1],
                        in1=MEANB[0:np_, 0:1], op=OP.mult)
        e.scalar_tensor_tensor(out=VARB[0:np_, 0:1],
                               in0=SSQ[0:np_, 0:1], scalar=RD,
                               in1=MSQ[0:np_, 0:1], op0=OP.mult,
                               op1=OP.subtract)
        return e.tensor_tensor(out=JUNKV[0:np_, 0:1], in0=VARB[0:np_, 0:1],
                               in1=VARB[0:np_, 0:1], op=OP.add)
    for dc in range(NDC):
        dma("gpsimd", [], WUT_SB[:, dc * INNER:(dc + 1) * INNER],
            WUT[dc * 128:(dc + 1) * 128, 0:INNER], "SW")
    for bd_d, bd_s in ((BDF_D, BDF_SB), (BDI_D, BDI_SB), (BDH_D, BDH_SB)):
        dma("gpsimd", [], bd_s[:, :], bd_d, "SW")
    dma("gpsimd", [], CWS[:, :], CWD, "SW")
    dma("gpsimd", [], EYE[:, :], EYE_D, "SW")
    dma("gpsimd", [], BUPM[:, :], BUPM_D, "SW")
    dma("gpsimd", [], BUPZ[:, :], BUPZ_D, "SW")
    dma("gpsimd", [], CBS[:, :], CB_D, "SW")
    dma("gpsimd", [], SKIPS[:, :], SKIP_D, "SW")
    dma("gpsimd", [], HALOM[:, :], HALOM_D, "SW")
    dma("gpsimd", [], PAIRM[:, :], PAIRM_D, "SW")
    dma("gpsimd", [], BDOWNB[:, :], BDOWN_D, "SW")
    dma("gpsimd", [], BRB[:, :], BR_D, "SW")
    dma("gpsimd", [], BIB[:, :], BI_D, "SW")
    dma("gpsimd", [], G1RS[:, :], G1R_D, "SW")
    dma("gpsimd", [], G2RS[:, :], G2R_D, "SW")
    dma("gpsimd", [], G1IS[:, :], G1I_D, "SW")
    dma("gpsimd", [], G2IS[:, :], G2I_D, "SW")
    dma("gpsimd", [], CONS[:, :], CONS_D, "SW")
    w_all = cnt["SW"]

    dma("sync", [], XHS, XH, "SX", "xhalo")

    def v_init(e):
        e.memset(RSTATE[:, :], 0.0)
        e.memset(PFSUM[:, :], 0.0)
        e.memset(ONESB[:, :], 1.0)
        return e.tensor_copy(EYEB[:, :], EYE[:, :])
    em("vector", [("SW", w_all)], v_init, tinc=("SVEC", 1))
    inc("SVEC", 1, "vinit")

    def v_diag(e):
        r = None
        for pt in range(NPT):
            for k in range(4):
                j = pt * 4 + k
                r = e.tensor_scalar(out=DIAG[:, j * 128:(j + 1) * 128],
                                    in0=EYE[:, :], scalar1=CWS[:, j:j + 1],
                                    scalar2=None, op0=OP.mult)
        return r
    em("vector", [], v_diag, tinc=("SVEC", 1))
    inc("SVEC", 1, "vdiag")

    def v_bupmk(e):
        return e.tensor_scalar(out=BUPMK[:, :], in0=BUPM[:, :],
                               scalar1=HALOM[:, 0:1], scalar2=None, op0=OP.mult)
    em("vector", [], v_bupmk, tinc=("SVEC", 1))
    inc("SVEC", 1, "vbupmk")

    def v_halostats(e):
        return stats_ops(e, XHS, 3)
    em("vector", [("SX", at("SX", "xhalo"))], v_halostats, tinc=("SVEC", 1))
    inc("SVEC", 1, "vhstat")

    em("scalar", [("SVEC", at("SVEC", "vhstat")), ("SW", w_all)],
       lambda e: (e.activation(out=XHSD[0:3, 0:1], in_=VARB[0:3, 0:1],
                               func=AF.Ln, bias=CONS[0:3, 0:1]),
                  e.activation(out=XHRS[0:3, 0:1], in_=XHSD[0:3, 0:1],
                               func=AF.Exp, scale=-0.5),
                  e.activation(out=JUNKA[0:3, 0:1], in_=XHRS[0:3, 0:1],
                               func=AF.Copy))[2],
       tinc=("SACT", 1))
    inc("SACT", 1, "ahsqrt")

    def v_hrs(e):
        e.scalar_tensor_tensor(out=XHNM[0:3, 0:1], in0=MEANB[0:3, 0:1],
                               scalar=-1.0, in1=XHRS[0:3, 0:1],
                               op0=OP.mult, op1=OP.mult)
        return e.tensor_tensor(out=JUNKV[0:3, 1:2], in0=XHRS[0:3, 0:1],
                               in1=XHNM[0:3, 0:1], op=OP.add)
    em("vector", [("SACT", at("SACT", "ahsqrt"))], v_hrs, tinc=("SVEC", 1))
    inc("SVEC", 1, "vhrs")

    em("scalar", [("SVEC", at("SVEC", "vhrs"))],
       lambda e: e.activation(out=XHN, in_=XHS, func=AF.Identity,
                              scale=XHRS[0:3, 0:1], bias=XHNM[0:3, 0:1]),
       tinc=("SACT", 1))
    inc("SACT", 1, "ahnorm")

    for dc in range(NDC):
        em("tensor",
           [("SACT", at("SACT", "ahnorm")), ("SVEC", at("SVEC", "vinit"))] +
           ([("SACT", at("SACT", ("ahxt", dc - 1)))] if dc > 0 else []),
           (lambda e, dc=dc: e.transpose(out=P_AUX[:, 0:3],
                                         in_=XHN[:, dc * 128:(dc + 1) * 128],
                                         identity=EYE[0:3, 0:3])),
           tinc=("SPE", 1))
        inc("SPE", 1, ("hxt", dc))
        em("scalar", [("SPE", at("SPE", ("hxt", dc)))],
           (lambda e, dc=dc: e.activation(out=XHT[:, dc * 3:(dc + 1) * 3],
                                          in_=P_AUX[:, 0:3], func=AF.Copy)),
           tinc=("SACT", 1))
        inc("SACT", 1, ("ahxt", dc))

    for ep in range(NPT):
        def pe_halo(e, ep=ep):
            r = None
            for dc in range(NDC):
                r = e.matmul(out=P_MM[ep % 2][:, 0:3],
                             lhsT=WUT_SB[:, dc * INNER + ep * 128:
                                         dc * INNER + (ep + 1) * 128],
                             rhs=XHT[:, dc * 3:(dc + 1) * 3],
                             start=(dc == 0), stop=(dc == NDC - 1))
            return r
        waits = [("SACT", at("SACT", ("ahxt", NDC - 1))), ("SW", w_all)]
        if ep >= 2:
            waits.append(("SACT", at("SACT", ("ahev", ep - 2))))
        em("tensor", waits, pe_halo, tinc=("SPE", 1))
        inc("SPE", 1, ("hmm", ep))
        em("scalar", [("SPE", at("SPE", ("hmm", ep))),
                      ("SVEC", at("SVEC", "vbupmk"))],
           (lambda e, ep=ep: e.activation(out=XMIN[:, ep * 520: ep * 520 + 3],
                                          in_=P_MM[ep % 2][:, 0:3], func=AF.Identity,
                                          scale=HALOM[:, 0:1],
                                          bias=BUPMK[:, ep:ep + 1])),
           tinc=("SACT", 1))
        inc("SACT", 1, ("ahev", ep))

    if dbg:
        dma("scalar", [("SACT", at("SACT", ("ahev", NPT - 1)))],
            DXHT, XHT[:, :], "SMI")
        dma("scalar", [("SACT", at("SACT", "ahnorm"))], DXHN, XHN, "SMI")
        dma("scalar", [("SVEC", at("SVEC", "vhrs"))], DXHRS[0:3, 0:2],
            XHMV[0:3, 0:2], "SMI")

    prev_tag = None
    for oi, (g1, g2, od) in enumerate(((G1RS, G2RS, ODR), (G1IS, G2IS, ODI))):
        for q in range(8):
            def pe_outer(e, g1=g1, g2=g2, q=q):
                r = None
                for c in range(4):
                    r = e.matmul(out=P_AUX[:, 0:256],
                                 lhsT=g1[0:1, c * 1024 + q * 128:
                                         c * 1024 + (q + 1) * 128],
                                 rhs=g2[0:1, c * 256:(c + 1) * 256],
                                 start=(c == 0), stop=(c == 3))
                return r
            waits = [("SW", w_all), ("SACT", at("SACT", ("ahev", NPT - 1)))]
            if prev_tag is not None:
                waits.append(("SACT", at("SACT", ("wbev", prev_tag))))
            em("tensor", waits, pe_outer, tinc=("SPE", 1))
            inc("SPE", 1, ("wbmm", (oi, q)))
            waits = [("SPE", at("SPE", ("wbmm", (oi, q))))]
            if prev_tag is not None:
                waits.append(("SMI", at("SMI", ("wbod", prev_tag))))
            em("scalar", waits,
               (lambda e: (e.activation(out=OSB[:, :], in_=P_AUX[:, 0:256],
                                        func=AF.Copy),
                           e.activation(out=JUNKA[:, 0:1], in_=OSB[:, 255:256],
                                        func=AF.Copy))[1]),
               tinc=("SACT", 1))
            inc("SACT", 1, ("wbev", (oi, q)))
            dma("scalar", [("SACT", at("SACT", ("wbev", (oi, q))))],
                od[q * 128:(q + 1) * 128, :], OSB[:, :],
                "SMI", ("wbod", (oi, q)))
            prev_tag = (oi, q)
    wb_last = prev_tag


    xs4 = XS.rearrange("(t c p) d -> t c p d", c=NCH, p=128)
    xn4 = XN.rearrange("(t c p) d -> t c p d", c=NCH, p=128)

    def emit_xload(t, hf):
        waits = []
        if t == 0 and hf == 0:
            waits += [("SPE", at("SPE", ("wbmm", wb_last))),
                      ("SMI", at("SMI", ("wbod", wb_last))),
                      ("SACT", at("SACT", ("ahev", NPT - 1)))]
        if t >= 1:
            waits.append(("SACT", at("SACT", ("aln", t - 1, 2 * hf + 1))))
        dma("sync", waits,
            XB[hf][:, :].rearrange("p (c d) -> p c d", c=2),
            xs4[t, 2 * hf:2 * hf + 2].rearrange("c p d -> p c d"),
            "SX", ("x", t, hf))

    def emit_ln(t, c):
        hf, ci = c // 2, c % 2
        def v_stats(e, hf=hf, ci=ci):
            return stats_ops(e, XB[hf][:, ci * D:(ci + 1) * D], 128)
        w = [("SX", at("SX", ("x", t, hf)))]
        if (t, c) != (0, 0):
            pc = (t, c - 1) if c > 0 else (t - 1, NCH - 1)
            w.append(("SACT", at("SACT", ("aln", pc[0], pc[1]))))
        em("vector", w, v_stats, tinc=("SVEC", 1))
        inc("SVEC", 1, ("vstat", t, c))

        em("scalar", [("SVEC", at("SVEC", ("vstat", t, c)))],
           (lambda e: (e.activation(out=SD1[:, 0:1], in_=VARB[:, 0:1],
                                    func=AF.Ln, bias=CONS[:, 0:1]),
                       e.activation(out=RSTD[:, 0:1], in_=SD1[:, 0:1],
                                    func=AF.Exp, scale=-0.5),
                       e.activation(out=JUNKA[:, 0:1], in_=RSTD[:, 0:1],
                                    func=AF.Copy))[2]),
           tinc=("SACT", 1))
        inc("SACT", 1, ("asq", t, c))

        def v_rs(e):
            e.scalar_tensor_tensor(out=NMR[:, 0:1], in0=MEANB[:, 0:1],
                                   scalar=-1.0, in1=RSTD[:, 0:1],
                                   op0=OP.mult, op1=OP.mult)
            return e.tensor_tensor(out=JUNKV[:, 1:2], in0=RSTD[:, 0:1],
                                   in1=NMR[:, 0:1], op=OP.add)
        em("vector", [("SACT", at("SACT", ("asq", t, c)))], v_rs,
           tinc=("SVEC", 1))
        inc("SVEC", 1, ("vrs", t, c))

        w = [("SVEC", at("SVEC", ("vrs", t, c)))]
        if (t, c) >= (0, 2):
            pc = (t, c - 2) if c >= 2 else (t - 1, NCH - 2 + c)
            w.append(("SXNW", at("SXNW", ("xnw", pc[0], pc[1]))))
        em("scalar", w,
           (lambda e, c=c, hf=hf, ci=ci: (
               e.activation(out=XNB[c % 2][:, :],
                            in_=XB[hf][:, ci * D:(ci + 1) * D],
                            func=AF.Identity, scale=RSTD[:, 0:1],
                            bias=NMR[:, 0:1]),
               e.activation(out=JUNKA[:, 0:1], in_=XNB[c % 2][:, D - 1:D],
                            func=AF.Copy))[1]),
           tinc=("SACT", 1))
        inc("SACT", 1, ("aln", t, c))
        dma("scalar", [("SACT", at("SACT", ("aln", t, c)))],
            xn4[t, c], XNB[c % 2][:, :], "SXNW", ("xnw", t, c))

    def emit_xnt(t):
        bi = t % 2
        for dc in range(NDC):
            w = [("SXNW", at("SXNW", ("xnw", t, NCH - 1)))]
            if t >= 2:
                w.append(("SPE", at("SPE", ("mm", t - 2, NPT - 1))))
            dma("sync", w, XNT[bi][:, dc * TILE:(dc + 1) * TILE],
                XN[t * TILE:(t + 1) * TILE, dc * 128:(dc + 1) * 128],
                "SXNT", ("xnt", t, dc), transpose=True)

        def v_settle(e, bi=bi):
            r = None
            for dc in range(NDC):
                r = e.tensor_copy(JUNKA[:, 2:3],
                                  XNT[bi][:, (dc + 1) * TILE - 1:
                                          (dc + 1) * TILE])
            return r
        em("vector", [("SXNT", at("SXNT", ("xnt", t, NDC - 1)))], v_settle,
           tinc=("SVEC", 1))
        inc("SVEC", 1, ("vxs", t))

    def emit_mmaev(t, ep):
        bi = t % 2
        def pe_mm(e, ep=ep, bi=bi):
            r = None
            for dc in range(NDC):
                r = e.matmul(out=P_MM[ep % 2][:, :],
                             lhsT=WUT_SB[:, dc * INNER + ep * 128:
                                         dc * INNER + (ep + 1) * 128],
                             rhs=XNT[bi][:, dc * TILE:(dc + 1) * TILE],
                             start=(dc == 0), stop=(dc == NDC - 1))
            return r
        w = [("SVEC", at("SVEC", ("vxs", t)))]
        if t == 0 and ep < 2:
            w.append(("SACT", at("SACT", ("ahev", NPT - 1))))
        if (t, ep) >= (0, 2):
            pe_, ep_ = (t, ep - 2) if ep >= 2 else (t - 1, NPT - 2 + ep)
            w.append(("SACT", at("SACT", ("aev", pe_, ep_))))
        em("tensor", w, pe_mm, tinc=("SPE", 1))
        inc("SPE", 1, ("mm", t, ep))

        w = [("SPE", at("SPE", ("mm", t, ep)))]
        if t > 0:
            w.append(("SVEC", at("SVEC", ("vhc", t - 1, ep))))
        em("scalar", w,
           (lambda e, ep=ep: (e.activation(
               out=XMIN[:, ep * 520 + 3: ep * 520 + 515],
               in_=P_MM[ep % 2][:, :], func=AF.Identity,
               bias=BUPM[:, ep:ep + 1]),
               e.activation(out=JUNKA[:, 0:1],
                            in_=XMIN[:, ep * 520 + 514: ep * 520 + 515],
                            func=AF.Copy))[1]),
           tinc=("SACT", 1))
        inc("SACT", 1, ("aev", t, ep))

    def emit_cvsilu(t, pt):
        def pe_cv(e, pt=pt):
            r = None
            for k in range(4):
                j = pt * 4 + k
                r = e.matmul(out=P_CV[pt % 2][:, :],
                             lhsT=DIAG[:, j * 128:(j + 1) * 128],
                             rhs=XMIN[:, pt * 520 + k: pt * 520 + k + 512],
                             start=(k == 0), stop=(k == 3))
            return r
        w = [("SACT", at("SACT", ("aev", t, pt))),
             ("SVEC", at("SVEC", "vdiag"))]
        if (t, pt) >= (0, 2):
            pe_, pt_ = (t, pt - 2) if pt >= 2 else (t - 1, NPT - 2 + pt)
            w.append(("SACT", at("SACT", ("asilu", pe_, pt_))))
        em("tensor", w, pe_cv, tinc=("SPE", 1))
        inc("SPE", 1, ("cv", t, pt))

        w = [("SPE", at("SPE", ("cv", t, pt)))]
        if t > 0:
            w.append(("SPE", at("SPE", ("gt", t - 1, pt))))
            w.append(("SAG", at("SAG", ("ag", t - 1))))
        em("scalar", w,
           (lambda e, pt=pt: (e.activation(
               out=AGT[:, pt * TILE:(pt + 1) * TILE],
               in_=P_CV[pt % 2][:, :], func=AF.Silu,
               bias=CBS[:, pt:pt + 1]),
               e.activation(out=JUNKA[:, 0:1],
                            in_=AGT[:, (pt + 1) * TILE - 1:(pt + 1) * TILE],
                            func=AF.Copy))[1]),
           tinc=("SACT", 1))
        inc("SACT", 1, ("asilu", t, pt))

    def emit_vhc(t, pt):
        def v_hc(e, pt=pt):
            return e.tensor_copy(XMIN[:, pt * 520: pt * 520 + 3],
                                 XMIN[:, pt * 520 + 512: pt * 520 + 515])
        em("vector", [("SPE", at("SPE", ("cv", t, pt)))], v_hc,
           tinc=("SVEC", 1))
        inc("SVEC", 1, ("vhc", t, pt))

    def emit_spill(t):
        dma("scalar", [("SACT", at("SACT", ("asilu", t, NPT - 1)))],
            AG[:, :, t * TILE:(t + 1) * TILE].rearrange("q p s -> p q s"),
            AGT[:, :].rearrange("p (q s) -> p q s", q=NPT), "SAG", ("ag", t))

    def prevslot(t, pt, d):
        k = t * NPT + pt - d
        return (k // NPT, k % NPT) if k >= 0 else None

    def emit_gt(t, pt):
        def pe_gt(e, pt=pt):
            e.matmul(out=P_G[:, 0:512],
                     lhsT=BDF_SB[:, pt * 128:(pt + 1) * 128],
                     rhs=AGT[:, pt * TILE:(pt + 1) * TILE],
                     start=True, stop=True)
            e.matmul(out=P_G[:, 512:1024],
                     lhsT=BDI_SB[:, pt * 128:(pt + 1) * 128],
                     rhs=AGT[:, pt * TILE:(pt + 1) * TILE],
                     start=True, stop=True)
            return e.matmul(out=P_G[:, 1024:1536],
                            lhsT=BDH_SB[:, pt * 128:(pt + 1) * 128],
                            rhs=AGT[:, pt * TILE:(pt + 1) * TILE],
                            start=True, stop=True)
        w = [("SACT", at("SACT", ("asilu", t, pt)))]
        p1 = prevslot(t, pt, 1)
        if p1:
            w.append(("SACT", at("SACT", ("a1",) + p1)))
            w.append(("SVEC", at("SVEC", ("v1",) + p1)))
        em("tensor", w, pe_gt, tinc=("SPE", 1))
        inc("SPE", 1, ("gt", t, pt))

    def emit_a1(t, pt):
        pb = pt % 2
        def a_1(e, pb=pb):
            e.activation(out=FIH[pb][:, :], in_=P_G[:, :], func=AF.Exp,
                         scale=-1.0)
            e.activation(out=LNH[:, :], in_=FIH[pb][:, 1024:1536],
                         func=AF.Ln, bias=CONS[:, 1:2])
            e.activation(out=GB[:, :], in_=LNH[:, :], func=AF.Exp,
                         scale=-1.0)
            return e.activation(out=JUNKA[:, 0:2],
                                in_=FIH[pb][:, 1022:1024], func=AF.Copy)
        w = [("SPE", at("SPE", ("gt", t, pt)))]
        p2 = prevslot(t, pt, 2)
        if p2:
            w.append(("SVEC", at("SVEC", ("v2",) + p2)))
        em("scalar", w, a_1, tinc=("SACT", 1))
        inc("SACT", 1, ("a1", t, pt))

    def emit_a3(t, pt):
        def a_3(e, pt=pt):
            return e.activation(out=LNH[:, :], in_=FB[pt % 2][:, :], func=AF.Ln,
                                accum_out=LNFACC[:, pt:pt + 1])
        em("scalar", [("SVEC", at("SVEC", ("v2", t, pt)))], a_3,
           tinc=("SACT", 1))
        inc("SACT", 1, ("a3", t, pt))

    def emit_a2(t, pt):
        pb = pt % 2
        def a_2(e, pb=pb):
            e.activation(out=LNH[:, :], in_=UB[pb][:, :], func=AF.Ln,
                         bias=CONS[:, 3:4])
            e.activation(out=WB[pb][:, :], in_=LNH[:, :], func=AF.Exp,
                         scale=-1.0)
            return e.activation(out=JUNKA[:, 0:1], in_=WB[pb][:, 511:512],
                                func=AF.Copy)
        em("scalar", [("SVEC", at("SVEC", ("v1", t, pt)))], a_2,
           tinc=("SACT", 1))
        inc("SACT", 1, ("a2", t, pt))

    def emit_v1(t, pt):
        pb = pt % 2
        p3 = prevslot(t, pt, 3)
        def v_1(e, pb=pb, p3=p3):
            if p3:
                pp = p3[1]
                e.tensor_tensor(out=PFSUM[:, pp:pp + 1],
                                in0=PFSUM[:, pp:pp + 1],
                                in1=LNFACC[:, pp:pp + 1], op=OP.add)
            e.tensor_tensor(out=UB[pb][:, :], in0=FIH[pb][:, 0:512],
                            in1=FIH[pb][:, 512:1024], op=OP.add)
            e.tensor_tensor(out=JUNKV[:, 3:4], in0=UB[pb][:, 511:512],
                            in1=UB[pb][:, 511:512], op=OP.add)
            return e.scalar_tensor_tensor(out=GHB[pb][:, :],
                                          in0=P_G[:, 1024:1536],
                                          scalar=GEPS, in1=GB[:, :],
                                          op0=OP.add, op1=OP.max)
        em("vector", [("SACT", at("SACT", ("a1", t, pt)))], v_1,
           tinc=("SVEC", 1))
        inc("SVEC", 1, ("v1", t, pt))

    def emit_v2(t, pt):
        pb = pt % 2
        def v_2(e, pt=pt, pb=pb):
            e.scalar_tensor_tensor(out=FB[pb][:, :], in0=FIH[pb][:, 512:1024],
                                   scalar=1.0, in1=WB[pb][:, :],
                                   op0=OP.add, op1=OP.mult)
            e.scalar_tensor_tensor(out=TB[:, :], in0=FIH[pb][:, 0:512],
                                   scalar=1.0, in1=WB[pb][:, :],
                                   op0=OP.add, op1=OP.mult)
            e.tensor_tensor(out=GGB[:, :], in0=TB[:, :], in1=GHB[pb][:, :],
                            op=OP.mult)
            e.tensor_tensor_scan(out=RSOUT[:, :], data0=FB[pb][:, :],
                                 data1=GGB[:, :],
                                 initial=RSTATE[:, pt:pt + 1],
                                 op0=OP.mult, op1=OP.add)
            e.tensor_copy(RSTATE[:, pt:pt + 1], RSOUT[:, 511:512])
            return e.tensor_tensor(out=JUNKV[:, 4:5], in0=FB[pb][:, 511:512],
                                   in1=FB[pb][:, 511:512], op=OP.add)
        em("vector", [("SACT", at("SACT", ("a2", t, pt)))], v_2,
           tinc=("SVEC", 1))
        inc("SVEC", 1, ("v2", t, pt))

    def emit_chain_slot(t, pt):
        emit_gt(t, pt)
        p1 = prevslot(t, pt, 1)
        if p1:
            emit_a2(*p1)
        emit_a1(t, pt)
        p2 = prevslot(t, pt, 2)
        if p2:
            emit_a3(*p2)
        emit_v1(t, pt)
        if p1:
            emit_v2(*p1)

    def emit_prep(t, slot):
        if slot == 0:
            emit_spill_t = t - 1
            if emit_spill_t >= 0:
                emit_spill(emit_spill_t)
            emit_xload(t, 0)
            emit_ln(t, 0)
        elif slot == 1:
            emit_ln(t, 1)
        elif slot == 2:
            emit_xload(t, 1)
            emit_ln(t, 2)
        elif slot == 3:
            emit_ln(t, 3)
        elif slot == 4:
            emit_xnt(t)
        elif 5 <= slot <= 12:
            for ep in (2 * (slot - 5), 2 * (slot - 5) + 1):
                emit_mmaev(t, ep)
        if 8 <= slot <= 15:
            for pt2 in (2 * (slot - 8), 2 * (slot - 8) + 1):
                emit_cvsilu(t, pt2)

    for slot in range(16):
        emit_prep(0, slot)

    for t in range(nt):
        for pt in range(NPT):
            emit_chain_slot(t, pt)
            if t < nt - 1:
                if pt < 4:
                    for pt2 in range(4 * pt, 4 * pt + 4):
                        emit_vhc(t, pt2)
                emit_prep(t + 1, pt)
        if t == nt - 1:
            emit_spill(nt - 1)
            L = (nt - 1, NPT - 1)
            emit_a3(*prevslot(nt - 1, NPT - 1, 1))
            emit_a2(*L)
            emit_v2(*L)
            emit_a3(*L)

    def v_ccb(e):
        for pp in (NPT - 3, NPT - 2, NPT - 1):
            e.tensor_tensor(out=PFSUM[:, pp:pp + 1], in0=PFSUM[:, pp:pp + 1],
                            in1=LNFACC[:, pp:pp + 1], op=OP.add)
        e.tensor_copy(CCB[:, 0:16], RSTATE[:, :])
        return e.tensor_tensor(out=JUNKV[:, 0:16], in0=CCB[:, 0:16],
                               in1=CCB[:, 0:16], op=OP.add)
    em("vector", [("SVEC", at("SVEC", ("v2", nt - 1, NPT - 1))),
                  ("SACT", at("SACT", ("a3", nt - 1, NPT - 1))),
                  ("SACT", at("SACT", ("a3", nt - 1, NPT - 2))),
                  ("SACT", at("SACT", ("a3", nt - 1, NPT - 3)))], v_ccb,
       tinc=("SVEC", 1))
    inc("SVEC", 1, "vccb")

    em("scalar", [("SVEC", at("SVEC", "vccb"))],
       (lambda e: (e.activation(out=CCB[:, 16:32], in_=PFSUM[:, :], func=AF.Exp,
                                bias=CONS[:, 2:3]),
                   e.activation(out=JUNKA[:, 0:16], in_=CCB[:, 16:32],
                                func=AF.Copy))[1]),
       tinc=("SACT", 1))
    inc("SACT", 1, "apf")

    dma("gpsimd", [("SVEC", at("SVEC", "vccb")), ("SACT", at("SACT", "apf"))],
        CCIN, CCB[:, :], "SGP", "ccin")

    def g_cc(e):
        return e.collective_compute(
            "AllGather", OP.bypass, replica_groups=[list(range(8))],
            ins=[CCIN.opt()], outs=[CCOUT.opt()])
    em("gpsimd", [("SGP", at("SGP", "ccin"))], g_cc, tinc=("SCC", 1))
    inc("SCC", 1, "cc")

    dma("gpsimd", [("SCC", at("SCC", "cc"))],
        GATH[:, :].rearrange("p (r c) -> p r c", r=8),
        CCOUT.rearrange("(r p) c -> p r c", p=128), "SGP", "gath")

    def v_hlast(e):
        e.tensor_scalar(out=SEL1[:, :], in0=GATH[:, 0:32],
                        scalar1=PAIRM[:, 0:1], scalar2=None, op0=OP.mult)
        e.tensor_scalar(out=SEL2[:, :], in0=GATH[:, 0:32],
                        scalar1=PAIRM[:, 8:9], scalar2=None, op0=OP.mult)
        for r in range(1, 8):
            e.scalar_tensor_tensor(out=SEL1[:, :],
                                   in0=GATH[:, 32 * r:32 * (r + 1)],
                                   scalar=PAIRM[:, r:r + 1], in1=SEL1[:, :],
                                   op0=OP.mult, op1=OP.add)
            e.scalar_tensor_tensor(out=SEL2[:, :],
                                   in0=GATH[:, 32 * r:32 * (r + 1)],
                                   scalar=PAIRM[:, 8 + r:9 + r], in1=SEL2[:, :],
                                   op0=OP.mult, op1=OP.add)
        e.tensor_tensor(out=TMP16[:, :], in0=SEL2[:, 16:32], in1=SEL1[:, 0:16],
                        op=OP.mult)
        e.tensor_tensor(out=HL[:, :], in0=SEL2[:, 0:16], in1=TMP16[:, :],
                        op=OP.add)
        return e.tensor_tensor(out=JUNKV[:, 0:16], in0=HL[:, :],
                               in1=HL[:, :], op=OP.add)
    em("vector", [("SGP", at("SGP", "gath"))], v_hlast, tinc=("SVEC", 1))
    inc("SVEC", 1, "vhl")

    sag_full = cnt["SAG"]
    st1.close()

    cur[0] = "ph2"

    WDT_SB = sb(st, "wdt", [128, 16 * D], BF16)
    WRT = sb(st, "wrt", [128, 4 * 512], BF16)
    WIT = sb(st, "wit", [128, 4 * 512], BF16)
    WINEG = sb(st, "wineg", [128, 4 * 512], BF16)
    UCH = [sb(st, "uch0", [128, TILE], BF16), sb(st, "uch1", [128, TILE], BF16)]
    ACTT = [sb(st, "actt0", [128, TILE], BF16), sb(st, "actt1", [128, TILE], BF16)]
    T2B = sb(st, "t2b", [128, TILE], BF16)
    YT = [sb(st, "yt0", [128, NPT * TILE], BF16),
          sb(st, "yt1", [128, NPT * TILE], BF16)]
    XBC = [sb(st, "xbc0", [128, D], F32), sb(st, "xbc1", [128, D], F32)]
    R1 = sb(st, "r1", [128, NCH * D], F32)
    XN2B = sb(st, "xn2b", [128, NCH * D], BF16)
    XN2T = [sb(st, "xn2t0", [128, NDC * TILE], BF16),
            sb(st, "xn2t1", [128, NDC * TILE], BF16)]
    OUTC = [sb(st, "outc0", [128, D], F32), sb(st, "outc1", [128, D], F32)]

    P2_Z = [psm(st, "pz0", [128, TILE]), psm(st, "pz1", [128, TILE])]
    P2_AT = [psm(st, "pat0", [128, D]), psm(st, "pat1", [128, D])]
    P2_FR = psm(st, "pfr", [128, 512])
    P2_FI = psm(st, "pfi", [128, 512])

    for dc in range(NDC):
        dma("gpsimd", [], WUT_SB[:, dc * INNER:(dc + 1) * INNER],
            WUT[dc * 128:(dc + 1) * 128, INNER:2 * INNER], "SW")
    inc("SW", 0, "wz")
    for ec in range(16):
        dma("gpsimd", [], WDT_SB[:, ec * D:(ec + 1) * D],
            WDT[ec * 128:(ec + 1) * 128, :], "SW")
    inc("SW", 0, "wdt")
    for w_sb, od in ((WRT, ODR), (WIT, ODI)):
        od4 = od.rearrange("(d b) (g e) -> d b g e", b=32, e=16)
        for d in range(32):
            pp, r0 = d // 8, (d % 8) * 16
            dma("gpsimd", [("SMI", at("SMI", ("wbod", wb_last)))],
                w_sb[r0:r0 + 16, pp * 512:(pp + 1) * 512]
                .rearrange("g (b e) -> g b e", b=32),
                od4[d].rearrange("b g e -> g b e"), "SW")
    inc("SW", 0, "wffn")

    def v_wineg(e):
        return e.tensor_scalar(out=WINEG[:, :], in0=WIT[:, :], scalar1=-1.0,
                               scalar2=None, op0=OP.mult)
    em("vector", [("SW", at("SW", "wffn"))], v_wineg, tinc=("SVEC", 1))
    inc("SVEC", 1, "vwineg")

    xn2_4 = XN2.rearrange("(t c p) d -> t c p d", c=NCH, p=128)

    def emit_ffn_block(t):
        bi = t % 2
        def v_settle3(e, bi=bi):
            r = None
            for dc in range(NDC):
                r = e.tensor_copy(JUNKA[:, 2:3],
                                  XN2T[bi][:, (dc + 1) * TILE - 1:
                                           (dc + 1) * TILE])
            return r
        em("vector", [("SXN2T", at("SXN2T", ("xn2t", t, NDC - 1)))], v_settle3,
           tinc=("SVEC", 1))
        inc("SVEC", 1, ("vxs3", t))

        for c in range(NCH):
            def pe_ffn(e, c=c, bi=bi):
                for dc in range(4):
                    e.matmul(out=P2_FR[:, :],
                             lhsT=XN2T[bi][:, dc * TILE + c * 128:
                                           dc * TILE + (c + 1) * 128],
                             rhs=WRT[:, dc * 512:(dc + 1) * 512],
                             start=(dc == 0), stop=False)
                for dc in range(4):
                    e.matmul(out=P2_FR[:, :],
                             lhsT=XN2T[bi][:, (4 + dc) * TILE + c * 128:
                                           (4 + dc) * TILE + (c + 1) * 128],
                             rhs=WINEG[:, dc * 512:(dc + 1) * 512],
                             start=False, stop=False)
                e.matmul(out=P2_FR[:, :], lhsT=ONESB[0:1, :], rhs=BRB[0:1, :],
                         start=False, stop=True)
                for dc in range(4):
                    e.matmul(out=P2_FI[:, :],
                             lhsT=XN2T[bi][:, dc * TILE + c * 128:
                                           dc * TILE + (c + 1) * 128],
                             rhs=WIT[:, dc * 512:(dc + 1) * 512],
                             start=(dc == 0), stop=False)
                for dc in range(4):
                    e.matmul(out=P2_FI[:, :],
                             lhsT=XN2T[bi][:, (4 + dc) * TILE + c * 128:
                                           (4 + dc) * TILE + (c + 1) * 128],
                             rhs=WRT[:, dc * 512:(dc + 1) * 512],
                             start=False, stop=False)
                return e.matmul(out=P2_FI[:, :], lhsT=ONESB[0:1, :],
                                rhs=BIB[0:1, :], start=False, stop=True)
            w = [("SVEC", at("SVEC", ("vxs3", t))),
                 ("SW", at("SW", "wffn")), ("SVEC", at("SVEC", "vwineg"))]
            if (t, c) >= (0, 1):
                pe_, pc_ = (t, c - 1) if c >= 1 else (t - 1, NCH - 1)
                w.append(("SVEC", at("SVEC", ("vfin", pe_, pc_))))
            em("tensor", w, pe_ffn, tinc=("SPE", 1))
            inc("SPE", 1, ("ffn", t, c))

            def v_fin(e, c=c, bi=bi):
                e.tensor_tensor(out=OUTC[c % 2][:, 0:512],
                                in0=R1[:, c * D: c * D + 512],
                                in1=P2_FR[:, :], op=OP.add)
                e.tensor_tensor(out=OUTC[c % 2][:, 512:1024],
                                in0=R1[:, c * D + 512: c * D + 1024],
                                in1=P2_FI[:, :], op=OP.add)
                return e.tensor_tensor(out=JUNKV[:, 5:6],
                                       in0=OUTC[c % 2][:, 1023:1024],
                                       in1=OUTC[c % 2][:, 1023:1024],
                                       op=OP.add)
            w = [("SPE", at("SPE", ("ffn", t, c)))]
            if (t, c) >= (0, 2):
                pe_, pc_ = (t, c - 2) if c >= 2 else (t - 1, NCH - 2 + c)
                w.append(("SOUT", at("SOUT", ("out", pe_, pc_))))
            em("vector", w, v_fin, tinc=("SVEC", 1))
            inc("SVEC", 1, ("vfin", t, c))

            dma("scalar", [("SVEC", at("SVEC", ("vfin", t, c)))],
                OUT[(t * NCH + c) * 128:(t * NCH + c + 1) * 128, :],
                OUTC[c % 2][:, :], "SOUT", ("out", t, c))



    def emit_zxnt(t):
        bi = t % 2
        for dc in range(NDC):
            w = []
            if t >= 2:
                w.append(("SPE", at("SPE", ("z", t - 2, NPT - 1))))
            dma("sync", w, XNT[bi][:, dc * TILE:(dc + 1) * TILE],
                XN[t * TILE:(t + 1) * TILE, dc * 128:(dc + 1) * 128],
                "SXNT", ("xnt2", t, dc), transpose=True)

        def v_settle2(e, bi=bi):
            r = None
            for dc in range(NDC):
                r = e.tensor_copy(JUNKA[:, 2:3],
                                  XNT[bi][:, (dc + 1) * TILE - 1:
                                          (dc + 1) * TILE])
            return r
        em("vector", [("SXNT", at("SXNT", ("xnt2", t, NDC - 1)))], v_settle2,
           tinc=("SVEC", 1))
        inc("SVEC", 1, ("vxs2", t))

    def emit_actld(t, pt):
        w = []
        if t == 0 and pt < 2:
            w.append(("SAG", sag_full))
        if (t, pt) >= (0, 2):
            pe_, pt_ = (t, pt - 2) if pt >= 2 else (t - 1, NPT - 2 + pt)
            w.append(("SACT", at("SACT", ("at2", pe_, pt_))))
        dma("scalar", w, ACTT[pt % 2][:, :],
            AG[pt, :, t * TILE:(t + 1) * TILE], "SACTLD", ("actld", t, pt))

    def emit_zslot(t, pt):
        bi = t % 2
        def pe_z(e, pt=pt, bi=bi):
            r = None
            for dc in range(NDC):
                r = e.matmul(out=P2_Z[pt % 2][:, :],
                             lhsT=WUT_SB[:, dc * INNER + pt * 128:
                                         dc * INNER + (pt + 1) * 128],
                             rhs=XNT[bi][:, dc * TILE:(dc + 1) * TILE],
                             start=(dc == 0), stop=(dc == NDC - 1))
            return r
        w = [("SVEC", at("SVEC", ("vxs2", t))), ("SW", at("SW", "wz"))]
        if (t, pt) >= (0, 2):
            pe_, pt_ = (t, pt - 2) if pt >= 2 else (t - 1, NPT - 2 + pt)
            w.append(("SACT", at("SACT", ("au", pe_, pt_))))
        em("tensor", w, pe_z, tinc=("SPE", 1))
        inc("SPE", 1, ("z", t, pt))

        w = [("SPE", at("SPE", ("z", t, pt)))]
        if (t, pt) >= (0, 2):
            pe_, pt_ = (t, pt - 2) if pt >= 2 else (t - 1, NPT - 2 + pt)
            w.append(("SVEC", at("SVEC", ("vy", pe_, pt_))))
        em("scalar", w,
           (lambda e, pt=pt: (e.activation(out=UCH[pt % 2][:, :],
                                           in_=P2_Z[pt % 2][:, :],
                                           func=AF.Silu,
                                           bias=BUPZ[:, pt:pt + 1]),
                              e.activation(out=JUNKA[:, 0:1],
                                           in_=UCH[pt % 2][:, 511:512],
                                           func=AF.Copy))[1]),
           tinc=("SACT", 1))
        inc("SACT", 1, ("au", t, pt))

        w = [("SACTLD", at("SACTLD", ("actld", t, pt))),
             ("SVEC", at("SVEC", "vhl"))]
        if (t, pt) >= (0, 1):
            pe_, pt_ = (t, pt - 1) if pt >= 1 else (t - 1, NPT - 1)
            w.append(("SVEC", at("SVEC", ("vy", pe_, pt_))))
        em("scalar", w,
           (lambda e, pt=pt: (e.activation(out=T2B[:, :],
                                           in_=ACTT[pt % 2][:, :],
                                           func=AF.Identity,
                                           scale=SKIPS[:, pt:pt + 1],
                                           bias=HL[:, pt:pt + 1]),
                              e.activation(out=JUNKA[:, 0:1],
                                           in_=T2B[:, 511:512],
                                           func=AF.Copy))[1]),
           tinc=("SACT", 1))
        inc("SACT", 1, ("at2", t, pt))
        if pt + 2 < NPT:
            emit_actld(t, pt + 2)

        def v_y(e, pt=pt, bi=bi):
            e.tensor_tensor(out=YT[bi][:, pt * TILE:(pt + 1) * TILE],
                            in0=T2B[:, :], in1=UCH[pt % 2][:, :],
                            op=OP.mult)
            return e.tensor_tensor(out=JUNKV[:, 2:3],
                                   in0=YT[bi][:, (pt + 1) * TILE - 1:
                                              (pt + 1) * TILE],
                                   in1=YT[bi][:, (pt + 1) * TILE - 1:
                                              (pt + 1) * TILE], op=OP.add)
        w = [("SACT", at("SACT", ("at2", t, pt))),
             ("SACT", at("SACT", ("au", t, pt)))]
        if t >= 2:
            w.append(("SPE", at("SPE", ("dn", t - 2, NCH - 1))))
        em("vector", w, v_y, tinc=("SVEC", 1))
        inc("SVEC", 1, ("vy", t, pt))

    def emit_x2(t, c):
        w = []
        if (t, c) >= (0, 2):
            pe_, pc_ = (t, c - 2) if c >= 2 else (t - 1, NCH - 2 + c)
            w.append(("SVEC", at("SVEC", ("vr1", pe_, pc_))))
        dma("sync", w, XBC[c % 2][:, :], xs4[t, c], "SX", ("x2", t, c))

    def emit_dnslot(t, c):
        bi = t % 2
        def pe_dn(e, c=c, bi=bi):
            r = None
            for ec in range(16):
                for n in range(2):
                    r = e.matmul(out=P2_AT[c % 2][:, n * 512:(n + 1) * 512],
                                 lhsT=YT[bi][:, ec * TILE + c * 128:
                                             ec * TILE + (c + 1) * 128],
                                 rhs=WDT_SB[:, ec * D + n * 512:
                                            ec * D + (n + 1) * 512],
                                 start=(ec == 0), stop=False)
            for n in range(2):
                r = e.matmul(out=P2_AT[c % 2][:, n * 512:(n + 1) * 512],
                             lhsT=ONESB[0:1, :],
                             rhs=BDOWNB[0:1, n * 512:(n + 1) * 512],
                             start=False, stop=True)
            return r
        w = [("SVEC", at("SVEC", ("vy", t, NPT - 1))),
             ("SW", at("SW", "wdt"))]
        if (t, c) >= (0, 2):
            pe_, pc_ = (t, c - 2) if c >= 2 else (t - 1, NCH - 2 + c)
            w.append(("SVEC", at("SVEC", ("vr1", pe_, pc_))))
        em("tensor", w, pe_dn, tinc=("SPE", 1))
        inc("SPE", 1, ("dn", t, c))

        def v_r1(e, c=c):
            return e.tensor_tensor(out=R1[:, c * D:(c + 1) * D],
                                   in0=XBC[c % 2][:, :],
                                   in1=P2_AT[c % 2][:, :], op=OP.add)
        em("vector", [("SPE", at("SPE", ("dn", t, c))),
                      ("SX", at("SX", ("x2", t, c)))], v_r1,
           tinc=("SVEC", 1))
        inc("SVEC", 1, ("vr1", t, c))
        if c + 2 < NCH:
            emit_x2(t, c + 2)

        def v_st2(e, c=c):
            return stats_ops(e, R1[:, c * D:(c + 1) * D], 128)
        w = [("SVEC", at("SVEC", ("vr1", t, c)))]
        if (t, c) != (0, 0):
            pc = (t, c - 1) if c > 0 else (t - 1, NCH - 1)
            w.append(("SACT", at("SACT", ("aln2", pc[0], pc[1]))))
        em("vector", w, v_st2, tinc=("SVEC", 1))
        inc("SVEC", 1, ("vst2", t, c))

        em("scalar", [("SVEC", at("SVEC", ("vst2", t, c)))],
           (lambda e: (e.activation(out=SD1[:, 0:1], in_=VARB[:, 0:1],
                                    func=AF.Ln, bias=CONS[:, 0:1]),
                       e.activation(out=RSTD[:, 0:1], in_=SD1[:, 0:1],
                                    func=AF.Exp, scale=-0.5),
                       e.activation(out=JUNKA[:, 0:1], in_=RSTD[:, 0:1],
                                    func=AF.Copy))[2]),
           tinc=("SACT", 1))
        inc("SACT", 1, ("asq2", t, c))

        def v_rs2(e):
            e.scalar_tensor_tensor(out=NMR[:, 0:1], in0=MEANB[:, 0:1],
                                   scalar=-1.0, in1=RSTD[:, 0:1],
                                   op0=OP.mult, op1=OP.mult)
            return e.tensor_tensor(out=JUNKV[:, 1:2], in0=RSTD[:, 0:1],
                                   in1=NMR[:, 0:1], op=OP.add)
        em("vector", [("SACT", at("SACT", ("asq2", t, c)))], v_rs2,
           tinc=("SVEC", 1))
        inc("SVEC", 1, ("vrs2", t, c))

        w = [("SVEC", at("SVEC", ("vrs2", t, c)))]
        if c == 0 and t > 0:
            w.append(("SXN2W", at("SXN2W", ("xn2w", t - 1))))
        em("scalar", w,
           (lambda e, c=c: (e.activation(out=XN2B[:, c * D:(c + 1) * D],
                                         in_=R1[:, c * D:(c + 1) * D],
                                         func=AF.Identity,
                                         scale=RSTD[:, 0:1],
                                         bias=NMR[:, 0:1]),
                            e.activation(out=JUNKA[:, 0:1],
                                         in_=XN2B[:, (c + 1) * D - 1:
                                                 (c + 1) * D],
                                         func=AF.Copy))[1]),
           tinc=("SACT", 1))
        inc("SACT", 1, ("aln2", t, c))

    def emit_xn2w(t):
        dma("scalar", [("SACT", at("SACT", ("aln2", t, NCH - 1)))],
            xn2_4[t].rearrange("c p d -> p c d"),
            XN2B[:, :].rearrange("p (c d) -> p c d", c=NCH),
            "SXN2W", ("xn2w", t))

    def emit_xn2t(t):
        bi = t % 2
        for dc in range(NDC):
            w = [("SXN2W", at("SXN2W", ("xn2w", t)))]
            if t >= 2:
                w.append(("SPE", at("SPE", ("ffn", t - 2, NCH - 1))))
            dma("sync", w, XN2T[bi][:, dc * TILE:(dc + 1) * TILE],
                XN2[t * TILE:(t + 1) * TILE, dc * 128:(dc + 1) * 128],
                "SXN2T", ("xn2t", t, dc), transpose=True)

    for t in range(nt):
        emit_zxnt(t)
        emit_actld(t, 0)
        emit_actld(t, 1)
        for pt in range(NPT):
            emit_zslot(t, pt)
            if t >= 1:
                if pt < NCH:
                    emit_dnslot(t - 1, pt)
                elif pt == 4:
                    emit_xn2w(t - 1)
                elif pt == 5:
                    emit_xn2t(t - 1)
                elif pt == 6:
                    emit_ffn_block(t - 1)
        emit_x2(t, 0)
        emit_x2(t, 1)

    for c in range(NCH):
        emit_dnslot(nt - 1, c)
    emit_xn2w(nt - 1)
    emit_xn2t(nt - 1)
    emit_ffn_block(nt - 1)

    if dbg:
        DXN = nc.dram_tensor("DXN", [t_tok, D], BF16, kind="ExternalOutput").ap()
        DAG = nc.dram_tensor("DAG", [NPT, 128, t_tok], BF16,
                             kind="ExternalOutput").ap()
        DXN2 = nc.dram_tensor("DXN2", [t_tok, D], BF16,
                              kind="ExternalOutput").ap()
        DCCB = nc.dram_tensor("DCCB", [128, 32], F32, kind="ExternalOutput").ap()
        DGATH = nc.dram_tensor("DGATH", [128, 256], F32,
                               kind="ExternalOutput").ap()
        DHL = nc.dram_tensor("DHL", [128, NPT], F32, kind="ExternalOutput").ap()
        DWRT = nc.dram_tensor("DWRT", [128, 4 * 512], BF16,
                              kind="ExternalOutput").ap()
        fo = cnt["SOUT"]
        dma("gpsimd", [("SOUT", fo)], DXN, XN, "SGP")
        dma("gpsimd", [], DAG, AG, "SGP")
        dma("gpsimd", [], DXN2, XN2, "SGP")
        dma("gpsimd", [], DCCB, CCB[:, :], "SGP")
        dma("gpsimd", [], DGATH, GATH[:, :], "SGP")
        dma("gpsimd", [], DHL, HL[:, :], "SGP")
        dma("gpsimd", [], DWRT, WRT[:, :], "SGP")
        em("gpsimd", [("SGP", cnt["SGP"])], lambda e: e.nop())

    final_out = cnt["SOUT"]
    em("scalar", [("SOUT", final_out)], lambda e: e.nop())

    for name in ("ph1", "ph2"):
        pl = plans[name]
        with nc.Block(name) as blk:
            @blk.sync
            def _(e, pl=pl):
                for eng, fn in pl:
                    if eng == "sync":
                        fn(e)

            @blk.gpsimd
            def _(e, pl=pl):
                for eng, fn in pl:
                    if eng == "gpsimd":
                        fn(e)

            @blk.tensor
            def _(e, pl=pl):
                for eng, fn in pl:
                    if eng == "tensor":
                        fn(e)

            @blk.scalar
            def _(e, pl=pl):
                for eng, fn in pl:
                    if eng == "scalar":
                        fn(e)

            @blk.vector
            def _(e, pl=pl):
                for eng, fn in pl:
                    if eng == "vector":
                        fn(e)

    st.close()
    return nc


def _host_inputs(inputs, core):
    b, h = core // 2, core % 2
    x = inputs["L0_attn_x"]
    w_up = inputs["L0_attn_w_up"]
    conv_w = inputs["L0_attn_conv_w"][:, 0, :]
    start = h * T
    xs = np.ascontiguousarray(x[b, start:start + T])
    if h == 0:
        xh = np.zeros((3, D), np.float32)
        halom = np.zeros((128, 1), np.float32)
    else:
        xh = np.ascontiguousarray(x[b, start - 3:start])
        halom = np.ones((128, 1), np.float32)
    pairm = np.zeros((128, 16), np.float32)
    pairm[:, 2 * b] = 1.0
    pairm[:, 8 + 2 * b + 1] = 1.0

    def pcol(v):
        return np.ascontiguousarray(v.reshape(NPT, 128).T)

    bd = {}
    for nm, w in (("BDF", inputs["L0_attn_w_f"]), ("BDI", inputs["L0_attn_w_i"]),
                  ("BDH", inputs["L0_attn_w_h"])):
        m = np.zeros((NPT, 128, 128), np.float32)
        for g in range(128):
            tt, j = g // 8, g % 8
            m[tt, 16 * j:16 * (j + 1), 16 * j:16 * (j + 1)] = w[g].T
        bd[nm] = np.ascontiguousarray(m.transpose(1, 0, 2).reshape(128, NPT * 128))

    g1r = inputs["L0_ffn_G1_R"][0]
    g2r = inputs["L0_ffn_G2_R"][:, :, 0, :]
    g1i = inputs["L0_ffn_G1_I"][0]
    g2i = inputs["L0_ffn_G2_I"][:, :, 0, :]

    def g1pack(g1):
        return np.ascontiguousarray(g1.transpose(1, 2, 0).reshape(1, 4096))

    def g2pack(g2):
        return np.ascontiguousarray(g2.transpose(0, 2, 1).reshape(1, 1024))

    cwm = np.ascontiguousarray(
        conv_w.reshape(NPT, 128, 4).transpose(1, 0, 2).reshape(128, 4 * NPT))

    return {
        "XS": xs, "XH": xh,
        "WUT": np.ascontiguousarray(w_up.T),
        "WDT": np.ascontiguousarray(inputs["L0_attn_w_down"].T),
        "BDF": bd["BDF"], "BDI": bd["BDI"], "BDH": bd["BDH"],
        "CWD": cwm,
        "BUPM": pcol(inputs["L0_attn_b_up"][:INNER]),
        "BUPZ": pcol(inputs["L0_attn_b_up"][INNER:]),
        "CBD": pcol(inputs["L0_attn_conv_b"]),
        "SKIPD": pcol(inputs["L0_attn_skip"]),
        "HALOM": halom, "PAIRM": pairm,
        "EYE": np.eye(128, dtype=np.float32),
        "BDOWND": np.ascontiguousarray(inputs["L0_attn_b_down"].reshape(1, D)),
        "BRD": np.ascontiguousarray(inputs["L0_ffn_bias_real"].reshape(1, 512)),
        "BID": np.ascontiguousarray(inputs["L0_ffn_bias_imag"].reshape(1, 512)),
        "G1R": g1pack(g1r), "G2R": g2pack(g2r),
        "G1I": g1pack(g1i), "G2I": g2pack(g2i),
        "CONS": np.tile(np.array([[LNEPS, 1.0, T * EPS, 2.0]], np.float32),
                        (128, 1)),
    }


def kernel(**inputs):
    inputs = {k: np.ascontiguousarray(np.asarray(v)) for k, v in inputs.items()}
    nc = build_nc()
    in_maps = [_host_inputs(inputs, k) for k in range(8)]
    res = run_bass_kernel_spmd(nc, in_maps, core_ids=list(range(8)))
    out = np.empty((B, S, D), np.float32)
    for k in range(8):
        b, h = k // 2, k % 2
        out[b, h * T:(h + 1) * T] = res.results[k]["OUT"]
    return out




# version 15
# speedup vs baseline: 1.1453x; 1.0045x over previous
import numpy as np
from collections import defaultdict
from contextlib import ExitStack

import concourse.bass as bass
import concourse.mybir as mybir
from concourse.bass_utils import run_bass_kernel_spmd

F32 = mybir.dt.float32
BF16 = mybir.dt.bfloat16
AF = mybir.ActivationFunctionType
OP = mybir.AluOpType

B, S, D = 4, 8192, 1024
INNER = 2048
NPT = INNER // 128
NDC = D // 128
T = 4096
TILE = 512
NT = T // TILE
NCH = TILE // 128
EPS = 1e-8
LNEPS = 1e-5
GEPS = 0.5 + 1e-8


def build_nc(n_tiles=NT, dbg=False):
    nt = n_tiles
    t_tok = nt * TILE
    nc = bass.Bass("TRN2", target_bir_lowering=False,
                   detect_race_conditions=False)

    EI = dict(kind="ExternalInput")
    XS = nc.dram_tensor("XS", [t_tok, D], F32, **EI).ap()
    XH = nc.dram_tensor("XH", [3, D], F32, **EI).ap()
    WUT = nc.dram_tensor("WUT", [D, 2 * INNER], F32, **EI).ap()
    WDT = nc.dram_tensor("WDT", [INNER, D], F32, **EI).ap()
    BDF_D = nc.dram_tensor("BDF", [128, NPT * 128], F32, **EI).ap()
    BDI_D = nc.dram_tensor("BDI", [128, NPT * 128], F32, **EI).ap()
    BDH_D = nc.dram_tensor("BDH", [128, NPT * 128], F32, **EI).ap()
    CWD = nc.dram_tensor("CWD", [128, 4 * NPT], F32, **EI).ap()
    BUPM_D = nc.dram_tensor("BUPM", [128, NPT], F32, **EI).ap()
    BUPZ_D = nc.dram_tensor("BUPZ", [128, NPT], F32, **EI).ap()
    CB_D = nc.dram_tensor("CBD", [128, NPT], F32, **EI).ap()
    SKIP_D = nc.dram_tensor("SKIPD", [128, NPT], F32, **EI).ap()
    HALOM_D = nc.dram_tensor("HALOM", [128, 1], F32, **EI).ap()
    PAIRM_D = nc.dram_tensor("PAIRM", [128, 16], F32, **EI).ap()
    EYE_D = nc.dram_tensor("EYE", [128, 128], F32, **EI).ap()
    BDOWN_D = nc.dram_tensor("BDOWND", [1, D], F32, **EI).ap()
    BR_D = nc.dram_tensor("BRD", [1, 512], F32, **EI).ap()
    BI_D = nc.dram_tensor("BID", [1, 512], F32, **EI).ap()
    G1R_D = nc.dram_tensor("G1R", [1, 4096], F32, **EI).ap()
    G2R_D = nc.dram_tensor("G2R", [1, 1024], F32, **EI).ap()
    G1I_D = nc.dram_tensor("G1I", [1, 4096], F32, **EI).ap()
    G2I_D = nc.dram_tensor("G2I", [1, 1024], F32, **EI).ap()
    CONS_D = nc.dram_tensor("CONS", [128, 4], F32, **EI).ap()

    OUT = nc.dram_tensor("OUT", [t_tok, D], F32, kind="ExternalOutput").ap()

    XN = nc.dram_tensor("XNs", [t_tok, D], BF16).ap()
    XN2 = nc.dram_tensor("XN2s", [t_tok, D], BF16).ap()
    AG = nc.dram_tensor("AGs", [NPT, 128, t_tok], BF16).ap()
    ODR = nc.dram_tensor("ODRs", [1024, 256], BF16).ap()
    ODI = nc.dram_tensor("ODIs", [1024, 256], BF16).ap()
    CCIN = nc.dram_tensor("CCINs", [128, 32], F32).ap()
    CCOUT = nc.dram_tensor("CCOUTs", [8 * 128, 32], F32).ap()

    st = ExitStack()
    st1 = ExitStack()

    def sb(stk, name, shape, dt):
        return stk.enter_context(nc.sbuf_tensor(name, shape, dt))

    def psm(stk, name, shape, dt=F32):
        return stk.enter_context(nc.psum_tensor(name, shape, dt))

    WUT_SB = sb(st, "wut", [128, NDC * INNER], BF16)
    EYE = sb(st, "eye", [128, 128], F32)
    EYEB = sb(st, "eyeb", [128, 128], BF16)
    BUPM = sb(st, "bupm", [128, NPT], F32)
    BUPZ = sb(st, "bupz", [128, NPT], F32)
    CBS = sb(st, "cbs", [128, NPT], F32)
    SKIPS = sb(st, "skips", [128, NPT], F32)
    HALOM = sb(st, "halom", [128, 1], F32)
    PAIRM = sb(st, "pairm", [128, 16], F32)
    ONESB = sb(st, "onesb", [1, 128], BF16)
    BDOWNB = sb(st, "bdownb", [1, D], BF16)
    BRB = sb(st, "brb", [1, 512], BF16)
    BIB = sb(st, "bib", [1, 512], BF16)
    RSTATE = sb(st, "rstate", [128, NPT], F32)
    PFSUM = sb(st, "pfsum", [128, NPT], F32)
    LNFACC = sb(st, "lnfacc", [128, NPT], F32)
    CCB = sb(st, "ccb", [128, 32], F32)
    GATH = sb(st, "gath", [128, 256], F32)
    SEL1 = sb(st, "sel1", [128, 32], F32)
    SEL2 = sb(st, "sel2", [128, 32], F32)
    HL = sb(st, "hl", [128, NPT], F32)
    TMP16 = sb(st, "tmp16", [128, NPT], F32)
    STATB = sb(st, "statb", [128, 12], F32)
    MV = sb(st, "mv", [128, 2], F32)
    SD1 = sb(st, "sd1", [128, 1], F32)
    RSTD = sb(st, "rstd", [128, 1], F32)
    NMR = sb(st, "nmr", [128, 1], F32)
    CONS = sb(st, "cons", [128, 4], F32)
    SCRQ = sb(st, "scrq", [128, 1024], F32)
    SMB = sb(st, "smb", [128, 1], F32)
    SSQ = sb(st, "ssq", [128, 1], F32)
    MEANB = sb(st, "meanb", [128, 1], F32)
    MSQ = sb(st, "msq", [128, 1], F32)
    VARB = sb(st, "varb", [128, 1], F32)
    JUNKV = sb(st, "junkv", [128, 16], F32)
    JUNKA = sb(st, "junka", [128, 16], F32)
    XNT = [sb(st, "xnt0", [128, NDC * TILE], BF16),
           sb(st, "xnt1", [128, NDC * TILE], BF16)]

    BDF_SB = sb(st1, "bdf", [128, NPT * 128], BF16)
    BDI_SB = sb(st1, "bdi", [128, NPT * 128], BF16)
    BDH_SB = sb(st1, "bdh", [128, NPT * 128], BF16)
    DIAG = sb(st1, "diag", [128, 4 * NPT * 128], BF16)
    CWS = sb(st1, "cws", [128, 4 * NPT], F32)
    BUPMK = sb(st1, "bupmk", [128, NPT], F32)
    XMIN = sb(st1, "xmin", [128, NPT * 520], BF16)
    XB = [sb(st1, "xb0", [128, 2 * D], F32), sb(st1, "xb1", [128, 2 * D], F32)]
    XNB = [sb(st1, "xnb0", [128, D], BF16), sb(st1, "xnb1", [128, D], BF16)]
    AGT = sb(st1, "agt", [128, NPT * TILE], BF16)
    FIH = [sb(st1, "fih0", [128, 3 * TILE], F32),
           sb(st1, "fih1", [128, 3 * TILE], F32)]
    UB = [sb(st1, "ub0", [128, TILE], F32), sb(st1, "ub1", [128, TILE], F32)]
    WB = [sb(st1, "wb0", [128, TILE], F32), sb(st1, "wb1", [128, TILE], F32)]
    LNH = sb(st1, "lnh", [128, TILE], F32)
    GB = sb(st1, "gb", [128, TILE], F32)
    GHB = [sb(st1, "ghb0", [128, TILE], F32), sb(st1, "ghb1", [128, TILE], F32)]
    FB = [sb(st1, "fb0", [128, TILE], F32), sb(st1, "fb1", [128, TILE], F32)]
    TB = sb(st1, "tb", [128, TILE], F32)
    GGB = sb(st1, "ggb", [128, TILE], F32)
    RSOUT = sb(st1, "rsout", [128, TILE], F32)

    P_MM = [psm(st1, "pmm0", [128, TILE]), psm(st1, "pmm1", [128, TILE])]
    P_CV = [psm(st1, "pcv0", [128, TILE]), psm(st1, "pcv1", [128, TILE])]

    G1RS = sb(st1, "g1rs", [1, 4096], BF16)
    G2RS = sb(st1, "g2rs", [1, 1024], BF16)
    G1IS = sb(st1, "g1is", [1, 4096], BF16)
    G2IS = sb(st1, "g2is", [1, 1024], BF16)
    OSB = sb(st1, "osb", [128, 256], BF16)
    XHST = sb(st1, "xhst", [3, 12], F32)
    XHMV = sb(st1, "xhmv", [3, 2], F32)
    XHSD = sb(st1, "xhsd", [3, 1], F32)
    XHRS = sb(st1, "xhrs", [3, 1], F32)
    XHNM = sb(st1, "xhnm", [3, 1], F32)
    XHT = sb(st1, "xht", [128, NDC * 3], BF16)
    XHS = XB[0][0:3, 0:D]
    XHN = XB[0][0:3, D:2 * D]
    P_AUX = psm(st1, "paux", [128, 256])
    P_G = psm(st1, "pg", [128, 3 * TILE])

    sem_names = ["SX", "SXNT", "SXNW", "SAG", "SW", "SMI", "SCC", "SACTLD",
                 "SXN2W", "SXN2T", "SOUT", "SPE", "SACT", "SVEC", "SGP"]
    SEM = {n: st.enter_context(nc.semaphore(n)) for n in sem_names}

    cnt = defaultdict(int)
    mark = {}

    def inc(s, n, tag=None):
        if s in ("SVEC", "SACT") and last_nops:
            n = last_nops.pop()
        cnt[s] += n
        if tag is not None:
            mark[(s, tag)] = cnt[s]
        return cnt[s]

    def at(s, tag):
        return mark[(s, tag)]

    plans = {"ph1": [], "ph2": []}
    cur = ["ph1"]

    _WRAP = ("activation", "tensor_scalar", "tensor_tensor",
             "scalar_tensor_tensor", "tensor_tensor_scan", "tensor_copy",
             "tensor_reduce", "reciprocal", "memset", "memzero", "select",
             "copy", "add", "mul", "sign", "sqrt", "square", "reduce_max",
             "reduce_sum", "bn_stats", "bn_aggr")

    class _Counter:
        def __init__(self):
            self.n = 0

        def __getattr__(self, name):
            if name in _WRAP:
                def f(*a, **k):
                    self.n += 1
                    return self
                return f
            def g(*a, **k):
                return self
            return g

        def then_inc(self, *a, **k):
            return self

    GROUP_SEMS = False

    class _Proxy:
        def __init__(self, e, sem, base, nops):
            self._e = e
            self._sem = sem
            self._n = base
            self._base = base
            self._nops = nops

        def __getattr__(self, name):
            e = self._e
            if name in _WRAP:
                def f(*a, **k):
                    if not GROUP_SEMS:
                        e.wait_ge(self._sem, self._n)
                    elif self._n == self._base:
                        e.wait_ge(self._sem, self._base)
                    r = getattr(e, name)(*a, **k)
                    if not GROUP_SEMS or self._n == self._base + self._nops - 1:
                        r.then_inc(self._sem,
                                   1 if not GROUP_SEMS else self._nops)
                    self._n += 1
                    return r
                return f
            return getattr(e, name)

    _selfsem = {"vector": "SVEC", "scalar": "SACT"}
    last_nops = []

    def em(eng, waits, fn, tinc=None):
        w = [(SEM[s], v) for (s, v) in waits]
        if eng in _selfsem:
            c = _Counter()
            fn(c)
            last_nops.append(c.n)
            ssem = SEM[_selfsem[eng]]
            base = cnt[_selfsem[eng]]

            if tinc is not None and tinc[0] == _selfsem[eng]:
                tinc = None
            ti = (SEM[tinc[0]], tinc[1]) if tinc else None

            def run(e, w=w, fn=fn, ssem=ssem, base=base, nops=c.n, ti=ti):
                for sh, v in w:
                    e.wait_ge(sh, v)
                p = _Proxy(e, ssem, base, nops)
                r = fn(p)
                if ti is not None:
                    r.then_inc(ti[0], ti[1])
                assert p._n == base + nops, (
                    f"op count mismatch: {p._n} != {base}+{nops}")
            plans[cur[0]].append((eng, run))
            return

        ti = (SEM[tinc[0]], tinc[1]) if tinc else None

        def run(e, w=w, fn=fn, ti=ti):
            for sh, v in w:
                e.wait_ge(sh, v)
            r = fn(e)
            if ti is not None:
                r.then_inc(ti[0], ti[1])
        plans[cur[0]].append((eng, run))

    def _inc_orig(s, n, tag=None):
        cnt[s] += n
        if tag is not None:
            mark[(s, tag)] = cnt[s]
        return cnt[s]

    def dma(eng, waits, out_ap, in_ap, s, tag=None, transpose=False):
        inc(s, 16, tag)
        em(eng, waits,
           (lambda e, o=out_ap, i=in_ap, tr=transpose:
            e.dma_start(out=o, in_=i, transpose=tr)),
           tinc=(s, 16))

    if dbg:
        DXHT = nc.dram_tensor("DXHT", [128, NDC * 3], BF16,
                              kind="ExternalOutput").ap()
        DXHN = nc.dram_tensor("DXHN", [3, D], F32, kind="ExternalOutput").ap()
        DXHRS = nc.dram_tensor("DXHRS", [3, 2], F32,
                               kind="ExternalOutput").ap()
        DXMIN = nc.dram_tensor("DXMIN", [128, NPT * 520], BF16,
                               kind="ExternalOutput").ap()


    RD = 1.0 / float(D)

    def stats_ops(e, src_ap, np_):
        e.tensor_scalar(out=SCRQ[0:np_, :], in0=src_ap, scalar1=0.0,
                        scalar2=0.0, op0=OP.add, op1=OP.add,
                        accum_out=SMB[0:np_, 0:1])
        e.scalar_tensor_tensor(out=SCRQ[0:np_, :], in0=src_ap, scalar=1.0,
                               in1=src_ap, op0=OP.mult, op1=OP.mult,
                               accum_out=SSQ[0:np_, 0:1])
        e.tensor_scalar(out=MEANB[0:np_, 0:1], in0=SMB[0:np_, 0:1],
                        scalar1=RD, scalar2=None, op0=OP.mult)
        e.tensor_tensor(out=MSQ[0:np_, 0:1], in0=MEANB[0:np_, 0:1],
                        in1=MEANB[0:np_, 0:1], op=OP.mult)
        e.scalar_tensor_tensor(out=VARB[0:np_, 0:1],
                               in0=SSQ[0:np_, 0:1], scalar=RD,
                               in1=MSQ[0:np_, 0:1], op0=OP.mult,
                               op1=OP.subtract)
        return e.tensor_tensor(out=JUNKV[0:np_, 0:1], in0=VARB[0:np_, 0:1],
                               in1=VARB[0:np_, 0:1], op=OP.add)
    for dc in range(NDC):
        dma("gpsimd", [], WUT_SB[:, dc * INNER:(dc + 1) * INNER],
            WUT[dc * 128:(dc + 1) * 128, 0:INNER], "SW")
    for bd_d, bd_s in ((BDF_D, BDF_SB), (BDI_D, BDI_SB), (BDH_D, BDH_SB)):
        dma("gpsimd", [], bd_s[:, :], bd_d, "SW")
    dma("gpsimd", [], CWS[:, :], CWD, "SW")
    dma("gpsimd", [], EYE[:, :], EYE_D, "SW")
    dma("gpsimd", [], BUPM[:, :], BUPM_D, "SW")
    dma("gpsimd", [], BUPZ[:, :], BUPZ_D, "SW")
    dma("gpsimd", [], CBS[:, :], CB_D, "SW")
    dma("gpsimd", [], SKIPS[:, :], SKIP_D, "SW")
    dma("gpsimd", [], HALOM[:, :], HALOM_D, "SW")
    dma("gpsimd", [], PAIRM[:, :], PAIRM_D, "SW")
    dma("gpsimd", [], BDOWNB[:, :], BDOWN_D, "SW")
    dma("gpsimd", [], BRB[:, :], BR_D, "SW")
    dma("gpsimd", [], BIB[:, :], BI_D, "SW")
    dma("gpsimd", [], G1RS[:, :], G1R_D, "SW")
    dma("gpsimd", [], G2RS[:, :], G2R_D, "SW")
    dma("gpsimd", [], G1IS[:, :], G1I_D, "SW")
    dma("gpsimd", [], G2IS[:, :], G2I_D, "SW")
    dma("gpsimd", [], CONS[:, :], CONS_D, "SW")
    w_all = cnt["SW"]

    dma("sync", [], XHS, XH, "SX", "xhalo")

    def v_init(e):
        e.memset(RSTATE[:, :], 0.0)
        e.memset(PFSUM[:, :], 0.0)
        e.memset(ONESB[:, :], 1.0)
        return e.tensor_copy(EYEB[:, :], EYE[:, :])
    em("vector", [("SW", w_all)], v_init, tinc=("SVEC", 1))
    inc("SVEC", 1, "vinit")

    def v_diag(e):
        r = None
        for pt in range(NPT):
            for k in range(4):
                j = pt * 4 + k
                r = e.tensor_scalar(out=DIAG[:, j * 128:(j + 1) * 128],
                                    in0=EYE[:, :], scalar1=CWS[:, j:j + 1],
                                    scalar2=None, op0=OP.mult)
        return r
    em("vector", [], v_diag, tinc=("SVEC", 1))
    inc("SVEC", 1, "vdiag")

    def v_bupmk(e):
        return e.tensor_scalar(out=BUPMK[:, :], in0=BUPM[:, :],
                               scalar1=HALOM[:, 0:1], scalar2=None, op0=OP.mult)
    em("vector", [], v_bupmk, tinc=("SVEC", 1))
    inc("SVEC", 1, "vbupmk")

    def v_halostats(e):
        return stats_ops(e, XHS, 3)
    em("vector", [("SX", at("SX", "xhalo"))], v_halostats, tinc=("SVEC", 1))
    inc("SVEC", 1, "vhstat")

    em("scalar", [("SVEC", at("SVEC", "vhstat")), ("SW", w_all)],
       lambda e: (e.activation(out=XHSD[0:3, 0:1], in_=VARB[0:3, 0:1],
                               func=AF.Ln, bias=CONS[0:3, 0:1]),
                  e.activation(out=XHRS[0:3, 0:1], in_=XHSD[0:3, 0:1],
                               func=AF.Exp, scale=-0.5),
                  e.activation(out=JUNKA[0:3, 0:1], in_=XHRS[0:3, 0:1],
                               func=AF.Copy))[2],
       tinc=("SACT", 1))
    inc("SACT", 1, "ahsqrt")

    def v_hrs(e):
        e.scalar_tensor_tensor(out=XHNM[0:3, 0:1], in0=MEANB[0:3, 0:1],
                               scalar=-1.0, in1=XHRS[0:3, 0:1],
                               op0=OP.mult, op1=OP.mult)
        return e.tensor_tensor(out=JUNKV[0:3, 1:2], in0=XHRS[0:3, 0:1],
                               in1=XHNM[0:3, 0:1], op=OP.add)
    em("vector", [("SACT", at("SACT", "ahsqrt"))], v_hrs, tinc=("SVEC", 1))
    inc("SVEC", 1, "vhrs")

    em("scalar", [("SVEC", at("SVEC", "vhrs"))],
       lambda e: e.activation(out=XHN, in_=XHS, func=AF.Identity,
                              scale=XHRS[0:3, 0:1], bias=XHNM[0:3, 0:1]),
       tinc=("SACT", 1))
    inc("SACT", 1, "ahnorm")

    for dc in range(NDC):
        em("tensor",
           [("SACT", at("SACT", "ahnorm")), ("SVEC", at("SVEC", "vinit"))] +
           ([("SACT", at("SACT", ("ahxt", dc - 1)))] if dc > 0 else []),
           (lambda e, dc=dc: e.transpose(out=P_AUX[:, 0:3],
                                         in_=XHN[:, dc * 128:(dc + 1) * 128],
                                         identity=EYE[0:3, 0:3])),
           tinc=("SPE", 1))
        inc("SPE", 1, ("hxt", dc))
        em("scalar", [("SPE", at("SPE", ("hxt", dc)))],
           (lambda e, dc=dc: e.activation(out=XHT[:, dc * 3:(dc + 1) * 3],
                                          in_=P_AUX[:, 0:3], func=AF.Copy)),
           tinc=("SACT", 1))
        inc("SACT", 1, ("ahxt", dc))

    for ep in range(NPT):
        def pe_halo(e, ep=ep):
            r = None
            for dc in range(NDC):
                r = e.matmul(out=P_MM[ep % 2][:, 0:3],
                             lhsT=WUT_SB[:, dc * INNER + ep * 128:
                                         dc * INNER + (ep + 1) * 128],
                             rhs=XHT[:, dc * 3:(dc + 1) * 3],
                             start=(dc == 0), stop=(dc == NDC - 1))
            return r
        waits = [("SACT", at("SACT", ("ahxt", NDC - 1))), ("SW", w_all)]
        if ep >= 2:
            waits.append(("SACT", at("SACT", ("ahev", ep - 2))))
        em("tensor", waits, pe_halo, tinc=("SPE", 1))
        inc("SPE", 1, ("hmm", ep))
        em("scalar", [("SPE", at("SPE", ("hmm", ep))),
                      ("SVEC", at("SVEC", "vbupmk"))],
           (lambda e, ep=ep: e.activation(out=XMIN[:, ep * 520: ep * 520 + 3],
                                          in_=P_MM[ep % 2][:, 0:3], func=AF.Identity,
                                          scale=HALOM[:, 0:1],
                                          bias=BUPMK[:, ep:ep + 1])),
           tinc=("SACT", 1))
        inc("SACT", 1, ("ahev", ep))

    if dbg:
        dma("scalar", [("SACT", at("SACT", ("ahev", NPT - 1)))],
            DXHT, XHT[:, :], "SMI")
        dma("scalar", [("SACT", at("SACT", "ahnorm"))], DXHN, XHN, "SMI")
        dma("scalar", [("SVEC", at("SVEC", "vhrs"))], DXHRS[0:3, 0:2],
            XHMV[0:3, 0:2], "SMI")

    prev_tag = None
    for oi, (g1, g2, od) in enumerate(((G1RS, G2RS, ODR), (G1IS, G2IS, ODI))):
        for q in range(8):
            def pe_outer(e, g1=g1, g2=g2, q=q):
                r = None
                for c in range(4):
                    r = e.matmul(out=P_AUX[:, 0:256],
                                 lhsT=g1[0:1, c * 1024 + q * 128:
                                         c * 1024 + (q + 1) * 128],
                                 rhs=g2[0:1, c * 256:(c + 1) * 256],
                                 start=(c == 0), stop=(c == 3))
                return r
            waits = [("SW", w_all), ("SACT", at("SACT", ("ahev", NPT - 1)))]
            if prev_tag is not None:
                waits.append(("SACT", at("SACT", ("wbev", prev_tag))))
            em("tensor", waits, pe_outer, tinc=("SPE", 1))
            inc("SPE", 1, ("wbmm", (oi, q)))
            waits = [("SPE", at("SPE", ("wbmm", (oi, q))))]
            if prev_tag is not None:
                waits.append(("SMI", at("SMI", ("wbod", prev_tag))))
            em("scalar", waits,
               (lambda e: (e.activation(out=OSB[:, :], in_=P_AUX[:, 0:256],
                                        func=AF.Copy),
                           e.activation(out=JUNKA[:, 0:1], in_=OSB[:, 255:256],
                                        func=AF.Copy))[1]),
               tinc=("SACT", 1))
            inc("SACT", 1, ("wbev", (oi, q)))
            dma("scalar", [("SACT", at("SACT", ("wbev", (oi, q))))],
                od[q * 128:(q + 1) * 128, :], OSB[:, :],
                "SMI", ("wbod", (oi, q)))
            prev_tag = (oi, q)
    wb_last = prev_tag


    xs4 = XS.rearrange("(t c p) d -> t c p d", c=NCH, p=128)
    xn4 = XN.rearrange("(t c p) d -> t c p d", c=NCH, p=128)

    def emit_xload(t, hf):
        waits = []
        if t == 0 and hf == 0:
            waits += [("SPE", at("SPE", ("wbmm", wb_last))),
                      ("SMI", at("SMI", ("wbod", wb_last))),
                      ("SACT", at("SACT", ("ahev", NPT - 1)))]
        if t >= 1:
            waits.append(("SACT", at("SACT", ("aln", t - 1, 2 * hf + 1))))
        dma("sync", waits,
            XB[hf][:, :].rearrange("p (c d) -> p c d", c=2),
            xs4[t, 2 * hf:2 * hf + 2].rearrange("c p d -> p c d"),
            "SX", ("x", t, hf))

    def emit_ln(t, c):
        hf, ci = c // 2, c % 2
        def v_stats(e, hf=hf, ci=ci):
            return stats_ops(e, XB[hf][:, ci * D:(ci + 1) * D], 128)
        w = [("SX", at("SX", ("x", t, hf)))]
        if (t, c) != (0, 0):
            pc = (t, c - 1) if c > 0 else (t - 1, NCH - 1)
            w.append(("SACT", at("SACT", ("aln", pc[0], pc[1]))))
        em("vector", w, v_stats, tinc=("SVEC", 1))
        inc("SVEC", 1, ("vstat", t, c))

        em("scalar", [("SVEC", at("SVEC", ("vstat", t, c)))],
           (lambda e: (e.activation(out=SD1[:, 0:1], in_=VARB[:, 0:1],
                                    func=AF.Ln, bias=CONS[:, 0:1]),
                       e.activation(out=RSTD[:, 0:1], in_=SD1[:, 0:1],
                                    func=AF.Exp, scale=-0.5),
                       e.activation(out=JUNKA[:, 0:1], in_=RSTD[:, 0:1],
                                    func=AF.Copy))[2]),
           tinc=("SACT", 1))
        inc("SACT", 1, ("asq", t, c))

        def v_rs(e):
            e.scalar_tensor_tensor(out=NMR[:, 0:1], in0=MEANB[:, 0:1],
                                   scalar=-1.0, in1=RSTD[:, 0:1],
                                   op0=OP.mult, op1=OP.mult)
            return e.tensor_tensor(out=JUNKV[:, 1:2], in0=RSTD[:, 0:1],
                                   in1=NMR[:, 0:1], op=OP.add)
        em("vector", [("SACT", at("SACT", ("asq", t, c)))], v_rs,
           tinc=("SVEC", 1))
        inc("SVEC", 1, ("vrs", t, c))

        w = [("SVEC", at("SVEC", ("vrs", t, c)))]
        if (t, c) >= (0, 2):
            pc = (t, c - 2) if c >= 2 else (t - 1, NCH - 2 + c)
            w.append(("SXNW", at("SXNW", ("xnw", pc[0], pc[1]))))
        em("scalar", w,
           (lambda e, c=c, hf=hf, ci=ci: (
               e.activation(out=XNB[c % 2][:, :],
                            in_=XB[hf][:, ci * D:(ci + 1) * D],
                            func=AF.Identity, scale=RSTD[:, 0:1],
                            bias=NMR[:, 0:1]),
               e.activation(out=JUNKA[:, 0:1], in_=XNB[c % 2][:, D - 1:D],
                            func=AF.Copy))[1]),
           tinc=("SACT", 1))
        inc("SACT", 1, ("aln", t, c))
        dma("scalar", [("SACT", at("SACT", ("aln", t, c)))],
            xn4[t, c], XNB[c % 2][:, :], "SXNW", ("xnw", t, c))

    def emit_xnt(t):
        bi = t % 2
        for dc in range(NDC):
            w = [("SXNW", at("SXNW", ("xnw", t, NCH - 1)))]
            if t >= 2:
                w.append(("SPE", at("SPE", ("mm", t - 2, NPT - 1))))
            dma("sync", w, XNT[bi][:, dc * TILE:(dc + 1) * TILE],
                XN[t * TILE:(t + 1) * TILE, dc * 128:(dc + 1) * 128],
                "SXNT", ("xnt", t, dc), transpose=True)

        def v_settle(e, bi=bi):
            r = None
            for dc in range(NDC):
                r = e.tensor_copy(JUNKA[:, 2:3],
                                  XNT[bi][:, (dc + 1) * TILE - 1:
                                          (dc + 1) * TILE])
            return r
        em("vector", [("SXNT", at("SXNT", ("xnt", t, NDC - 1)))], v_settle,
           tinc=("SVEC", 1))
        inc("SVEC", 1, ("vxs", t))

    def emit_mmaev(t, ep):
        bi = t % 2
        def pe_mm(e, ep=ep, bi=bi):
            r = None
            for dc in range(NDC):
                r = e.matmul(out=P_MM[ep % 2][:, :],
                             lhsT=WUT_SB[:, dc * INNER + ep * 128:
                                         dc * INNER + (ep + 1) * 128],
                             rhs=XNT[bi][:, dc * TILE:(dc + 1) * TILE],
                             start=(dc == 0), stop=(dc == NDC - 1))
            return r
        w = [("SVEC", at("SVEC", ("vxs", t)))]
        if t == 0 and ep < 2:
            w.append(("SACT", at("SACT", ("ahev", NPT - 1))))
        if (t, ep) >= (0, 2):
            pe_, ep_ = (t, ep - 2) if ep >= 2 else (t - 1, NPT - 2 + ep)
            w.append(("SACT", at("SACT", ("aev", pe_, ep_))))
        em("tensor", w, pe_mm, tinc=("SPE", 1))
        inc("SPE", 1, ("mm", t, ep))

        w = [("SPE", at("SPE", ("mm", t, ep)))]
        if t > 0:
            w.append(("SVEC", at("SVEC", ("vhc", t - 1, ep))))
        em("scalar", w,
           (lambda e, ep=ep: (e.activation(
               out=XMIN[:, ep * 520 + 3: ep * 520 + 515],
               in_=P_MM[ep % 2][:, :], func=AF.Identity,
               bias=BUPM[:, ep:ep + 1]),
               e.activation(out=JUNKA[:, 0:1],
                            in_=XMIN[:, ep * 520 + 514: ep * 520 + 515],
                            func=AF.Copy))[1]),
           tinc=("SACT", 1))
        inc("SACT", 1, ("aev", t, ep))

    def emit_cvsilu(t, pt):
        def pe_cv(e, pt=pt):
            r = None
            for k in range(4):
                j = pt * 4 + k
                r = e.matmul(out=P_CV[pt % 2][:, :],
                             lhsT=DIAG[:, j * 128:(j + 1) * 128],
                             rhs=XMIN[:, pt * 520 + k: pt * 520 + k + 512],
                             start=(k == 0), stop=(k == 3))
            return r
        w = [("SACT", at("SACT", ("aev", t, pt))),
             ("SVEC", at("SVEC", "vdiag"))]
        if (t, pt) >= (0, 2):
            pe_, pt_ = (t, pt - 2) if pt >= 2 else (t - 1, NPT - 2 + pt)
            w.append(("SACT", at("SACT", ("asilu", pe_, pt_))))
        em("tensor", w, pe_cv, tinc=("SPE", 1))
        inc("SPE", 1, ("cv", t, pt))

        w = [("SPE", at("SPE", ("cv", t, pt)))]
        if t > 0:
            w.append(("SPE", at("SPE", ("gt", t - 1, pt))))
            w.append(("SAG", at("SAG", ("ag", t - 1))))
        em("scalar", w,
           (lambda e, pt=pt: (e.activation(
               out=AGT[:, pt * TILE:(pt + 1) * TILE],
               in_=P_CV[pt % 2][:, :], func=AF.Silu,
               bias=CBS[:, pt:pt + 1]),
               e.activation(out=JUNKA[:, 0:1],
                            in_=AGT[:, (pt + 1) * TILE - 1:(pt + 1) * TILE],
                            func=AF.Copy))[1]),
           tinc=("SACT", 1))
        inc("SACT", 1, ("asilu", t, pt))

    def emit_vhc(t, pt):
        def v_hc(e, pt=pt):
            return e.tensor_copy(XMIN[:, pt * 520: pt * 520 + 3],
                                 XMIN[:, pt * 520 + 512: pt * 520 + 515])
        em("vector", [("SPE", at("SPE", ("cv", t, pt)))], v_hc,
           tinc=("SVEC", 1))
        inc("SVEC", 1, ("vhc", t, pt))

    def emit_spill(t):
        dma("scalar", [("SACT", at("SACT", ("asilu", t, NPT - 1)))],
            AG[:, :, t * TILE:(t + 1) * TILE].rearrange("q p s -> p q s"),
            AGT[:, :].rearrange("p (q s) -> p q s", q=NPT), "SAG", ("ag", t))

    def prevslot(t, pt, d):
        k = t * NPT + pt - d
        return (k // NPT, k % NPT) if k >= 0 else None

    def emit_gt(t, pt):
        def pe_gt(e, pt=pt):
            e.matmul(out=P_G[:, 0:512],
                     lhsT=BDF_SB[:, pt * 128:(pt + 1) * 128],
                     rhs=AGT[:, pt * TILE:(pt + 1) * TILE],
                     start=True, stop=True)
            e.matmul(out=P_G[:, 512:1024],
                     lhsT=BDI_SB[:, pt * 128:(pt + 1) * 128],
                     rhs=AGT[:, pt * TILE:(pt + 1) * TILE],
                     start=True, stop=True)
            return e.matmul(out=P_G[:, 1024:1536],
                            lhsT=BDH_SB[:, pt * 128:(pt + 1) * 128],
                            rhs=AGT[:, pt * TILE:(pt + 1) * TILE],
                            start=True, stop=True)
        w = [("SACT", at("SACT", ("asilu", t, pt)))]
        p1 = prevslot(t, pt, 1)
        if p1:
            w.append(("SACT", at("SACT", ("a1",) + p1)))
            w.append(("SVEC", at("SVEC", ("v1",) + p1)))
        em("tensor", w, pe_gt, tinc=("SPE", 1))
        inc("SPE", 1, ("gt", t, pt))

    def emit_a1(t, pt):
        pb = pt % 2
        def a_1(e, pb=pb):
            e.activation(out=FIH[pb][:, :], in_=P_G[:, :], func=AF.Exp,
                         scale=-1.0)
            e.activation(out=LNH[:, :], in_=FIH[pb][:, 1024:1536],
                         func=AF.Ln, bias=CONS[:, 1:2])
            e.activation(out=GB[:, :], in_=LNH[:, :], func=AF.Exp,
                         scale=-1.0)
            return e.activation(out=JUNKA[:, 0:2],
                                in_=FIH[pb][:, 1022:1024], func=AF.Copy)
        w = [("SPE", at("SPE", ("gt", t, pt)))]
        p2 = prevslot(t, pt, 2)
        if p2:
            w.append(("SVEC", at("SVEC", ("v2",) + p2)))
        em("scalar", w, a_1, tinc=("SACT", 1))
        inc("SACT", 1, ("a1", t, pt))

    def emit_a3(t, pt):
        def a_3(e, pt=pt):
            return e.activation(out=LNH[:, :], in_=FB[pt % 2][:, :], func=AF.Ln,
                                accum_out=LNFACC[:, pt:pt + 1])
        em("scalar", [("SVEC", at("SVEC", ("v2", t, pt)))], a_3,
           tinc=("SACT", 1))
        inc("SACT", 1, ("a3", t, pt))

    def emit_a2(t, pt):
        pb = pt % 2
        def a_2(e, pb=pb):
            e.activation(out=LNH[:, :], in_=UB[pb][:, :], func=AF.Ln,
                         bias=CONS[:, 3:4])
            e.activation(out=WB[pb][:, :], in_=LNH[:, :], func=AF.Exp,
                         scale=-1.0)
            return e.activation(out=JUNKA[:, 0:1], in_=WB[pb][:, 511:512],
                                func=AF.Copy)
        em("scalar", [("SVEC", at("SVEC", ("v1", t, pt)))], a_2,
           tinc=("SACT", 1))
        inc("SACT", 1, ("a2", t, pt))

    def emit_v1(t, pt):
        pb = pt % 2
        p3 = prevslot(t, pt, 3)
        def v_1(e, pb=pb, p3=p3):
            if p3:
                pp = p3[1]
                e.tensor_tensor(out=PFSUM[:, pp:pp + 1],
                                in0=PFSUM[:, pp:pp + 1],
                                in1=LNFACC[:, pp:pp + 1], op=OP.add)
            e.tensor_tensor(out=UB[pb][:, :], in0=FIH[pb][:, 0:512],
                            in1=FIH[pb][:, 512:1024], op=OP.add)
            e.tensor_tensor(out=JUNKV[:, 3:4], in0=UB[pb][:, 511:512],
                            in1=UB[pb][:, 511:512], op=OP.add)
            return e.scalar_tensor_tensor(out=GHB[pb][:, :],
                                          in0=P_G[:, 1024:1536],
                                          scalar=GEPS, in1=GB[:, :],
                                          op0=OP.add, op1=OP.max)
        em("vector", [("SACT", at("SACT", ("a1", t, pt)))], v_1,
           tinc=("SVEC", 1))
        inc("SVEC", 1, ("v1", t, pt))

    def emit_v2(t, pt):
        pb = pt % 2
        def v_2(e, pt=pt, pb=pb):
            e.scalar_tensor_tensor(out=FB[pb][:, :], in0=FIH[pb][:, 512:1024],
                                   scalar=1.0, in1=WB[pb][:, :],
                                   op0=OP.add, op1=OP.mult)
            e.scalar_tensor_tensor(out=TB[:, :], in0=FIH[pb][:, 0:512],
                                   scalar=1.0, in1=WB[pb][:, :],
                                   op0=OP.add, op1=OP.mult)
            e.tensor_tensor(out=GGB[:, :], in0=TB[:, :], in1=GHB[pb][:, :],
                            op=OP.mult)
            e.tensor_tensor_scan(out=RSOUT[:, :], data0=FB[pb][:, :],
                                 data1=GGB[:, :],
                                 initial=RSTATE[:, pt:pt + 1],
                                 op0=OP.mult, op1=OP.add)
            e.tensor_copy(RSTATE[:, pt:pt + 1], RSOUT[:, 511:512])
            return e.tensor_tensor(out=JUNKV[:, 4:5], in0=FB[pb][:, 511:512],
                                   in1=FB[pb][:, 511:512], op=OP.add)
        em("vector", [("SACT", at("SACT", ("a2", t, pt)))], v_2,
           tinc=("SVEC", 1))
        inc("SVEC", 1, ("v2", t, pt))

    def emit_chain_slot(t, pt):
        emit_gt(t, pt)
        p1 = prevslot(t, pt, 1)
        if p1:
            emit_a2(*p1)
        emit_a1(t, pt)
        p2 = prevslot(t, pt, 2)
        if p2:
            emit_a3(*p2)
        emit_v1(t, pt)
        if p1:
            emit_v2(*p1)

    def emit_prep(t, slot):
        if slot == 0:
            emit_spill_t = t - 1
            if emit_spill_t >= 0:
                emit_spill(emit_spill_t)
            emit_xload(t, 0)
            emit_ln(t, 0)
        elif slot == 1:
            emit_ln(t, 1)
        elif slot == 2:
            emit_xload(t, 1)
            emit_ln(t, 2)
        elif slot == 3:
            emit_ln(t, 3)
        elif slot == 4:
            emit_xnt(t)
        elif 5 <= slot <= 12:
            for ep in (2 * (slot - 5), 2 * (slot - 5) + 1):
                emit_mmaev(t, ep)
        if 8 <= slot <= 15:
            for pt2 in (2 * (slot - 8), 2 * (slot - 8) + 1):
                emit_cvsilu(t, pt2)

    for slot in range(16):
        emit_prep(0, slot)

    for t in range(nt):
        for pt in range(NPT):
            emit_chain_slot(t, pt)
            if t < nt - 1:
                if pt < 4:
                    for pt2 in range(4 * pt, 4 * pt + 4):
                        emit_vhc(t, pt2)
                emit_prep(t + 1, pt)
        if t == nt - 1:
            emit_spill(nt - 1)
            L = (nt - 1, NPT - 1)
            emit_a3(*prevslot(nt - 1, NPT - 1, 1))
            emit_a2(*L)
            emit_v2(*L)
            emit_a3(*L)

    def v_ccb(e):
        for pp in (NPT - 3, NPT - 2, NPT - 1):
            e.tensor_tensor(out=PFSUM[:, pp:pp + 1], in0=PFSUM[:, pp:pp + 1],
                            in1=LNFACC[:, pp:pp + 1], op=OP.add)
        e.tensor_copy(CCB[:, 0:16], RSTATE[:, :])
        return e.tensor_tensor(out=JUNKV[:, 0:16], in0=CCB[:, 0:16],
                               in1=CCB[:, 0:16], op=OP.add)
    em("vector", [("SVEC", at("SVEC", ("v2", nt - 1, NPT - 1))),
                  ("SACT", at("SACT", ("a3", nt - 1, NPT - 1))),
                  ("SACT", at("SACT", ("a3", nt - 1, NPT - 2))),
                  ("SACT", at("SACT", ("a3", nt - 1, NPT - 3)))], v_ccb,
       tinc=("SVEC", 1))
    inc("SVEC", 1, "vccb")

    em("scalar", [("SVEC", at("SVEC", "vccb"))],
       (lambda e: (e.activation(out=CCB[:, 16:32], in_=PFSUM[:, :], func=AF.Exp,
                                bias=CONS[:, 2:3]),
                   e.activation(out=JUNKA[:, 0:16], in_=CCB[:, 16:32],
                                func=AF.Copy))[1]),
       tinc=("SACT", 1))
    inc("SACT", 1, "apf")

    sag_full = cnt["SAG"]
    st1.close()

    cur[0] = "ph2"

    WDT_SB = sb(st, "wdt", [128, 16 * D], BF16)
    WRT = sb(st, "wrt", [128, 4 * 512], BF16)
    WIT = sb(st, "wit", [128, 4 * 512], BF16)
    WINEG = sb(st, "wineg", [128, 4 * 512], BF16)
    UCH = [sb(st, "uch0", [128, TILE], BF16), sb(st, "uch1", [128, TILE], BF16)]
    ACTT = [sb(st, "actt0", [128, TILE], BF16), sb(st, "actt1", [128, TILE], BF16)]
    T2B = sb(st, "t2b", [128, TILE], BF16)
    YT = [sb(st, "yt0", [128, NPT * TILE], BF16),
          sb(st, "yt1", [128, NPT * TILE], BF16)]
    XBC = [sb(st, "xbc0", [128, D], F32), sb(st, "xbc1", [128, D], F32)]
    R1 = sb(st, "r1", [128, NCH * D], F32)
    XN2B = sb(st, "xn2b", [128, NCH * D], BF16)
    XN2T = [sb(st, "xn2t0", [128, NDC * TILE], BF16),
            sb(st, "xn2t1", [128, NDC * TILE], BF16)]
    OUTC = [sb(st, "outc0", [128, D], F32), sb(st, "outc1", [128, D], F32)]

    P2_Z = [psm(st, "pz0", [128, TILE]), psm(st, "pz1", [128, TILE])]
    P2_AT = [psm(st, "pat0", [128, D]), psm(st, "pat1", [128, D])]
    P2_FR = psm(st, "pfr", [128, 512])
    P2_FI = psm(st, "pfi", [128, 512])

    for dc in range(NDC):
        dma("gpsimd", [], WUT_SB[:, dc * INNER:(dc + 1) * INNER],
            WUT[dc * 128:(dc + 1) * 128, INNER:2 * INNER], "SW")
    inc("SW", 0, "wz")
    dma("gpsimd", [("SVEC", at("SVEC", "vccb")), ("SACT", at("SACT", "apf"))],
        CCIN, CCB[:, :], "SGP", "ccin")

    def g_cc(e):
        return e.collective_compute(
            "AllGather", OP.bypass, replica_groups=[list(range(8))],
            ins=[CCIN.opt()], outs=[CCOUT.opt()])
    em("gpsimd", [("SGP", at("SGP", "ccin"))], g_cc, tinc=("SCC", 1))
    inc("SCC", 1, "cc")

    dma("gpsimd", [("SCC", at("SCC", "cc"))],
        GATH[:, :].rearrange("p (r c) -> p r c", r=8),
        CCOUT.rearrange("(r p) c -> p r c", p=128), "SGP", "gath")

    def v_hlast(e):
        e.tensor_scalar(out=SEL1[:, :], in0=GATH[:, 0:32],
                        scalar1=PAIRM[:, 0:1], scalar2=None, op0=OP.mult)
        e.tensor_scalar(out=SEL2[:, :], in0=GATH[:, 0:32],
                        scalar1=PAIRM[:, 8:9], scalar2=None, op0=OP.mult)
        for r in range(1, 8):
            e.scalar_tensor_tensor(out=SEL1[:, :],
                                   in0=GATH[:, 32 * r:32 * (r + 1)],
                                   scalar=PAIRM[:, r:r + 1], in1=SEL1[:, :],
                                   op0=OP.mult, op1=OP.add)
            e.scalar_tensor_tensor(out=SEL2[:, :],
                                   in0=GATH[:, 32 * r:32 * (r + 1)],
                                   scalar=PAIRM[:, 8 + r:9 + r], in1=SEL2[:, :],
                                   op0=OP.mult, op1=OP.add)
        e.tensor_tensor(out=TMP16[:, :], in0=SEL2[:, 16:32], in1=SEL1[:, 0:16],
                        op=OP.mult)
        e.tensor_tensor(out=HL[:, :], in0=SEL2[:, 0:16], in1=TMP16[:, :],
                        op=OP.add)
        return e.tensor_tensor(out=JUNKV[:, 0:16], in0=HL[:, :],
                               in1=HL[:, :], op=OP.add)
    em("vector", [("SGP", at("SGP", "gath"))], v_hlast, tinc=("SVEC", 1))
    inc("SVEC", 1, "vhl")


    for ec in range(16):
        dma("gpsimd", [], WDT_SB[:, ec * D:(ec + 1) * D],
            WDT[ec * 128:(ec + 1) * 128, :], "SW")
    inc("SW", 0, "wdt")
    for w_sb, od in ((WRT, ODR), (WIT, ODI)):
        od4 = od.rearrange("(d b) (g e) -> d b g e", b=32, e=16)
        for d in range(32):
            pp, r0 = d // 8, (d % 8) * 16
            dma("gpsimd", [("SMI", at("SMI", ("wbod", wb_last)))],
                w_sb[r0:r0 + 16, pp * 512:(pp + 1) * 512]
                .rearrange("g (b e) -> g b e", b=32),
                od4[d].rearrange("b g e -> g b e"), "SW")
    inc("SW", 0, "wffn")

    def v_wineg(e):
        return e.tensor_scalar(out=WINEG[:, :], in0=WIT[:, :], scalar1=-1.0,
                               scalar2=None, op0=OP.mult)
    em("vector", [("SW", at("SW", "wffn"))], v_wineg, tinc=("SVEC", 1))
    inc("SVEC", 1, "vwineg")

    xn2_4 = XN2.rearrange("(t c p) d -> t c p d", c=NCH, p=128)

    def emit_ffn_block(t):
        bi = t % 2
        def v_settle3(e, bi=bi):
            r = None
            for dc in range(NDC):
                r = e.tensor_copy(JUNKA[:, 2:3],
                                  XN2T[bi][:, (dc + 1) * TILE - 1:
                                           (dc + 1) * TILE])
            return r
        em("vector", [("SXN2T", at("SXN2T", ("xn2t", t, NDC - 1)))], v_settle3,
           tinc=("SVEC", 1))
        inc("SVEC", 1, ("vxs3", t))

        for c in range(NCH):
            def pe_ffn(e, c=c, bi=bi):
                for dc in range(4):
                    e.matmul(out=P2_FR[:, :],
                             lhsT=XN2T[bi][:, dc * TILE + c * 128:
                                           dc * TILE + (c + 1) * 128],
                             rhs=WRT[:, dc * 512:(dc + 1) * 512],
                             start=(dc == 0), stop=False)
                for dc in range(4):
                    e.matmul(out=P2_FR[:, :],
                             lhsT=XN2T[bi][:, (4 + dc) * TILE + c * 128:
                                           (4 + dc) * TILE + (c + 1) * 128],
                             rhs=WINEG[:, dc * 512:(dc + 1) * 512],
                             start=False, stop=False)
                e.matmul(out=P2_FR[:, :], lhsT=ONESB[0:1, :], rhs=BRB[0:1, :],
                         start=False, stop=True)
                for dc in range(4):
                    e.matmul(out=P2_FI[:, :],
                             lhsT=XN2T[bi][:, dc * TILE + c * 128:
                                           dc * TILE + (c + 1) * 128],
                             rhs=WIT[:, dc * 512:(dc + 1) * 512],
                             start=(dc == 0), stop=False)
                for dc in range(4):
                    e.matmul(out=P2_FI[:, :],
                             lhsT=XN2T[bi][:, (4 + dc) * TILE + c * 128:
                                           (4 + dc) * TILE + (c + 1) * 128],
                             rhs=WRT[:, dc * 512:(dc + 1) * 512],
                             start=False, stop=False)
                return e.matmul(out=P2_FI[:, :], lhsT=ONESB[0:1, :],
                                rhs=BIB[0:1, :], start=False, stop=True)
            w = [("SVEC", at("SVEC", ("vxs3", t))),
                 ("SW", at("SW", "wffn")), ("SVEC", at("SVEC", "vwineg"))]
            if (t, c) >= (0, 1):
                pe_, pc_ = (t, c - 1) if c >= 1 else (t - 1, NCH - 1)
                w.append(("SVEC", at("SVEC", ("vfin", pe_, pc_))))
            em("tensor", w, pe_ffn, tinc=("SPE", 1))
            inc("SPE", 1, ("ffn", t, c))

            def v_fin(e, c=c, bi=bi):
                e.tensor_tensor(out=OUTC[c % 2][:, 0:512],
                                in0=R1[:, c * D: c * D + 512],
                                in1=P2_FR[:, :], op=OP.add)
                e.tensor_tensor(out=OUTC[c % 2][:, 512:1024],
                                in0=R1[:, c * D + 512: c * D + 1024],
                                in1=P2_FI[:, :], op=OP.add)
                return e.tensor_tensor(out=JUNKV[:, 5:6],
                                       in0=OUTC[c % 2][:, 1023:1024],
                                       in1=OUTC[c % 2][:, 1023:1024],
                                       op=OP.add)
            w = [("SPE", at("SPE", ("ffn", t, c)))]
            if (t, c) >= (0, 2):
                pe_, pc_ = (t, c - 2) if c >= 2 else (t - 1, NCH - 2 + c)
                w.append(("SOUT", at("SOUT", ("out", pe_, pc_))))
            em("vector", w, v_fin, tinc=("SVEC", 1))
            inc("SVEC", 1, ("vfin", t, c))

            dma("scalar", [("SVEC", at("SVEC", ("vfin", t, c)))],
                OUT[(t * NCH + c) * 128:(t * NCH + c + 1) * 128, :],
                OUTC[c % 2][:, :], "SOUT", ("out", t, c))



    def emit_zxnt(t):
        bi = t % 2
        for dc in range(NDC):
            w = []
            if t >= 2:
                w.append(("SPE", at("SPE", ("z", t - 2, NPT - 1))))
            dma("sync", w, XNT[bi][:, dc * TILE:(dc + 1) * TILE],
                XN[t * TILE:(t + 1) * TILE, dc * 128:(dc + 1) * 128],
                "SXNT", ("xnt2", t, dc), transpose=True)

        def v_settle2(e, bi=bi):
            r = None
            for dc in range(NDC):
                r = e.tensor_copy(JUNKA[:, 2:3],
                                  XNT[bi][:, (dc + 1) * TILE - 1:
                                          (dc + 1) * TILE])
            return r
        em("vector", [("SXNT", at("SXNT", ("xnt2", t, NDC - 1)))], v_settle2,
           tinc=("SVEC", 1))
        inc("SVEC", 1, ("vxs2", t))

    def emit_actld(t, pt):
        w = []
        if t == 0 and pt < 2:
            w.append(("SAG", sag_full))
        if (t, pt) >= (0, 2):
            pe_, pt_ = (t, pt - 2) if pt >= 2 else (t - 1, NPT - 2 + pt)
            w.append(("SACT", at("SACT", ("at2", pe_, pt_))))
        dma("scalar", w, ACTT[pt % 2][:, :],
            AG[pt, :, t * TILE:(t + 1) * TILE], "SACTLD", ("actld", t, pt))

    def emit_zslot(t, pt):
        bi = t % 2
        def pe_z(e, pt=pt, bi=bi):
            r = None
            for dc in range(NDC):
                r = e.matmul(out=P2_Z[pt % 2][:, :],
                             lhsT=WUT_SB[:, dc * INNER + pt * 128:
                                         dc * INNER + (pt + 1) * 128],
                             rhs=XNT[bi][:, dc * TILE:(dc + 1) * TILE],
                             start=(dc == 0), stop=(dc == NDC - 1))
            return r
        w = [("SVEC", at("SVEC", ("vxs2", t))), ("SW", at("SW", "wz"))]
        if (t, pt) >= (0, 2):
            pe_, pt_ = (t, pt - 2) if pt >= 2 else (t - 1, NPT - 2 + pt)
            w.append(("SACT", at("SACT", ("au", pe_, pt_))))
        em("tensor", w, pe_z, tinc=("SPE", 1))
        inc("SPE", 1, ("z", t, pt))

        w = [("SPE", at("SPE", ("z", t, pt)))]
        if (t, pt) >= (0, 2):
            pe_, pt_ = (t, pt - 2) if pt >= 2 else (t - 1, NPT - 2 + pt)
            w.append(("SVEC", at("SVEC", ("vy", pe_, pt_))))
        em("scalar", w,
           (lambda e, pt=pt: (e.activation(out=UCH[pt % 2][:, :],
                                           in_=P2_Z[pt % 2][:, :],
                                           func=AF.Silu,
                                           bias=BUPZ[:, pt:pt + 1]),
                              e.activation(out=JUNKA[:, 0:1],
                                           in_=UCH[pt % 2][:, 511:512],
                                           func=AF.Copy))[1]),
           tinc=("SACT", 1))
        inc("SACT", 1, ("au", t, pt))

        w = [("SACTLD", at("SACTLD", ("actld", t, pt))),
             ("SVEC", at("SVEC", "vhl"))]
        if (t, pt) >= (0, 1):
            pe_, pt_ = (t, pt - 1) if pt >= 1 else (t - 1, NPT - 1)
            w.append(("SVEC", at("SVEC", ("vy", pe_, pt_))))
        em("scalar", w,
           (lambda e, pt=pt: (e.activation(out=T2B[:, :],
                                           in_=ACTT[pt % 2][:, :],
                                           func=AF.Identity,
                                           scale=SKIPS[:, pt:pt + 1],
                                           bias=HL[:, pt:pt + 1]),
                              e.activation(out=JUNKA[:, 0:1],
                                           in_=T2B[:, 511:512],
                                           func=AF.Copy))[1]),
           tinc=("SACT", 1))
        inc("SACT", 1, ("at2", t, pt))
        if pt + 2 < NPT:
            emit_actld(t, pt + 2)

        def v_y(e, pt=pt, bi=bi):
            e.tensor_tensor(out=YT[bi][:, pt * TILE:(pt + 1) * TILE],
                            in0=T2B[:, :], in1=UCH[pt % 2][:, :],
                            op=OP.mult)
            return e.tensor_tensor(out=JUNKV[:, 2:3],
                                   in0=YT[bi][:, (pt + 1) * TILE - 1:
                                              (pt + 1) * TILE],
                                   in1=YT[bi][:, (pt + 1) * TILE - 1:
                                              (pt + 1) * TILE], op=OP.add)
        w = [("SACT", at("SACT", ("at2", t, pt))),
             ("SACT", at("SACT", ("au", t, pt)))]
        if t >= 2:
            w.append(("SPE", at("SPE", ("dn", t - 2, NCH - 1))))
        em("vector", w, v_y, tinc=("SVEC", 1))
        inc("SVEC", 1, ("vy", t, pt))

    def emit_x2(t, c):
        w = []
        if (t, c) >= (0, 2):
            pe_, pc_ = (t, c - 2) if c >= 2 else (t - 1, NCH - 2 + c)
            w.append(("SVEC", at("SVEC", ("vr1", pe_, pc_))))
        dma("sync", w, XBC[c % 2][:, :], xs4[t, c], "SX", ("x2", t, c))

    def emit_dnslot(t, c):
        bi = t % 2
        def pe_dn(e, c=c, bi=bi):
            r = None
            for ec in range(16):
                for n in range(2):
                    r = e.matmul(out=P2_AT[c % 2][:, n * 512:(n + 1) * 512],
                                 lhsT=YT[bi][:, ec * TILE + c * 128:
                                             ec * TILE + (c + 1) * 128],
                                 rhs=WDT_SB[:, ec * D + n * 512:
                                            ec * D + (n + 1) * 512],
                                 start=(ec == 0), stop=False)
            for n in range(2):
                r = e.matmul(out=P2_AT[c % 2][:, n * 512:(n + 1) * 512],
                             lhsT=ONESB[0:1, :],
                             rhs=BDOWNB[0:1, n * 512:(n + 1) * 512],
                             start=False, stop=True)
            return r
        w = [("SVEC", at("SVEC", ("vy", t, NPT - 1))),
             ("SW", at("SW", "wdt"))]
        if (t, c) >= (0, 2):
            pe_, pc_ = (t, c - 2) if c >= 2 else (t - 1, NCH - 2 + c)
            w.append(("SVEC", at("SVEC", ("vr1", pe_, pc_))))
        em("tensor", w, pe_dn, tinc=("SPE", 1))
        inc("SPE", 1, ("dn", t, c))

        def v_r1(e, c=c):
            return e.tensor_tensor(out=R1[:, c * D:(c + 1) * D],
                                   in0=XBC[c % 2][:, :],
                                   in1=P2_AT[c % 2][:, :], op=OP.add)
        em("vector", [("SPE", at("SPE", ("dn", t, c))),
                      ("SX", at("SX", ("x2", t, c)))], v_r1,
           tinc=("SVEC", 1))
        inc("SVEC", 1, ("vr1", t, c))
        if c + 2 < NCH:
            emit_x2(t, c + 2)

        def v_st2(e, c=c):
            return stats_ops(e, R1[:, c * D:(c + 1) * D], 128)
        w = [("SVEC", at("SVEC", ("vr1", t, c)))]
        if (t, c) != (0, 0):
            pc = (t, c - 1) if c > 0 else (t - 1, NCH - 1)
            w.append(("SACT", at("SACT", ("aln2", pc[0], pc[1]))))
        em("vector", w, v_st2, tinc=("SVEC", 1))
        inc("SVEC", 1, ("vst2", t, c))

        em("scalar", [("SVEC", at("SVEC", ("vst2", t, c)))],
           (lambda e: (e.activation(out=SD1[:, 0:1], in_=VARB[:, 0:1],
                                    func=AF.Ln, bias=CONS[:, 0:1]),
                       e.activation(out=RSTD[:, 0:1], in_=SD1[:, 0:1],
                                    func=AF.Exp, scale=-0.5),
                       e.activation(out=JUNKA[:, 0:1], in_=RSTD[:, 0:1],
                                    func=AF.Copy))[2]),
           tinc=("SACT", 1))
        inc("SACT", 1, ("asq2", t, c))

        def v_rs2(e):
            e.scalar_tensor_tensor(out=NMR[:, 0:1], in0=MEANB[:, 0:1],
                                   scalar=-1.0, in1=RSTD[:, 0:1],
                                   op0=OP.mult, op1=OP.mult)
            return e.tensor_tensor(out=JUNKV[:, 1:2], in0=RSTD[:, 0:1],
                                   in1=NMR[:, 0:1], op=OP.add)
        em("vector", [("SACT", at("SACT", ("asq2", t, c)))], v_rs2,
           tinc=("SVEC", 1))
        inc("SVEC", 1, ("vrs2", t, c))

        w = [("SVEC", at("SVEC", ("vrs2", t, c)))]
        if c == 0 and t > 0:
            w.append(("SXN2W", at("SXN2W", ("xn2w", t - 1))))
        em("scalar", w,
           (lambda e, c=c: (e.activation(out=XN2B[:, c * D:(c + 1) * D],
                                         in_=R1[:, c * D:(c + 1) * D],
                                         func=AF.Identity,
                                         scale=RSTD[:, 0:1],
                                         bias=NMR[:, 0:1]),
                            e.activation(out=JUNKA[:, 0:1],
                                         in_=XN2B[:, (c + 1) * D - 1:
                                                 (c + 1) * D],
                                         func=AF.Copy))[1]),
           tinc=("SACT", 1))
        inc("SACT", 1, ("aln2", t, c))

    def emit_xn2w(t):
        dma("scalar", [("SACT", at("SACT", ("aln2", t, NCH - 1)))],
            xn2_4[t].rearrange("c p d -> p c d"),
            XN2B[:, :].rearrange("p (c d) -> p c d", c=NCH),
            "SXN2W", ("xn2w", t))

    def emit_xn2t(t):
        bi = t % 2
        for dc in range(NDC):
            w = [("SXN2W", at("SXN2W", ("xn2w", t)))]
            if t >= 2:
                w.append(("SPE", at("SPE", ("ffn", t - 2, NCH - 1))))
            dma("scalar", w, XN2T[bi][:, dc * TILE:(dc + 1) * TILE],
                XN2[t * TILE:(t + 1) * TILE, dc * 128:(dc + 1) * 128],
                "SXN2T", ("xn2t", t, dc), transpose=True)

    for t in range(nt):
        emit_zxnt(t)
        emit_actld(t, 0)
        emit_actld(t, 1)
        for pt in range(NPT):
            emit_zslot(t, pt)
            if t >= 1:
                if pt < NCH:
                    emit_dnslot(t - 1, pt)
                elif pt == 4:
                    emit_xn2w(t - 1)
                elif pt == 5:
                    emit_xn2t(t - 1)
                elif pt == 6:
                    emit_ffn_block(t - 1)
        emit_x2(t, 0)
        emit_x2(t, 1)

    for c in range(NCH):
        emit_dnslot(nt - 1, c)
    emit_xn2w(nt - 1)
    emit_xn2t(nt - 1)
    emit_ffn_block(nt - 1)

    if dbg:
        DXN = nc.dram_tensor("DXN", [t_tok, D], BF16, kind="ExternalOutput").ap()
        DAG = nc.dram_tensor("DAG", [NPT, 128, t_tok], BF16,
                             kind="ExternalOutput").ap()
        DXN2 = nc.dram_tensor("DXN2", [t_tok, D], BF16,
                              kind="ExternalOutput").ap()
        DCCB = nc.dram_tensor("DCCB", [128, 32], F32, kind="ExternalOutput").ap()
        DGATH = nc.dram_tensor("DGATH", [128, 256], F32,
                               kind="ExternalOutput").ap()
        DHL = nc.dram_tensor("DHL", [128, NPT], F32, kind="ExternalOutput").ap()
        DWRT = nc.dram_tensor("DWRT", [128, 4 * 512], BF16,
                              kind="ExternalOutput").ap()
        fo = cnt["SOUT"]
        dma("gpsimd", [("SOUT", fo)], DXN, XN, "SGP")
        dma("gpsimd", [], DAG, AG, "SGP")
        dma("gpsimd", [], DXN2, XN2, "SGP")
        dma("gpsimd", [], DCCB, CCB[:, :], "SGP")
        dma("gpsimd", [], DGATH, GATH[:, :], "SGP")
        dma("gpsimd", [], DHL, HL[:, :], "SGP")
        dma("gpsimd", [], DWRT, WRT[:, :], "SGP")
        em("gpsimd", [("SGP", cnt["SGP"])], lambda e: e.nop())

    final_out = cnt["SOUT"]
    em("scalar", [("SOUT", final_out)], lambda e: e.nop())

    for name in ("ph1", "ph2"):
        pl = plans[name]
        with nc.Block(name) as blk:
            @blk.sync
            def _(e, pl=pl):
                for eng, fn in pl:
                    if eng == "sync":
                        fn(e)

            @blk.gpsimd
            def _(e, pl=pl):
                for eng, fn in pl:
                    if eng == "gpsimd":
                        fn(e)

            @blk.tensor
            def _(e, pl=pl):
                for eng, fn in pl:
                    if eng == "tensor":
                        fn(e)

            @blk.scalar
            def _(e, pl=pl):
                for eng, fn in pl:
                    if eng == "scalar":
                        fn(e)

            @blk.vector
            def _(e, pl=pl):
                for eng, fn in pl:
                    if eng == "vector":
                        fn(e)

    st.close()
    return nc


def _host_inputs(inputs, core):
    b, h = core // 2, core % 2
    x = inputs["L0_attn_x"]
    w_up = inputs["L0_attn_w_up"]
    conv_w = inputs["L0_attn_conv_w"][:, 0, :]
    start = h * T
    xs = np.ascontiguousarray(x[b, start:start + T])
    if h == 0:
        xh = np.zeros((3, D), np.float32)
        halom = np.zeros((128, 1), np.float32)
    else:
        xh = np.ascontiguousarray(x[b, start - 3:start])
        halom = np.ones((128, 1), np.float32)
    pairm = np.zeros((128, 16), np.float32)
    pairm[:, 2 * b] = 1.0
    pairm[:, 8 + 2 * b + 1] = 1.0

    def pcol(v):
        return np.ascontiguousarray(v.reshape(NPT, 128).T)

    bd = {}
    for nm, w in (("BDF", inputs["L0_attn_w_f"]), ("BDI", inputs["L0_attn_w_i"]),
                  ("BDH", inputs["L0_attn_w_h"])):
        m = np.zeros((NPT, 128, 128), np.float32)
        for g in range(128):
            tt, j = g // 8, g % 8
            m[tt, 16 * j:16 * (j + 1), 16 * j:16 * (j + 1)] = w[g].T
        bd[nm] = np.ascontiguousarray(m.transpose(1, 0, 2).reshape(128, NPT * 128))

    g1r = inputs["L0_ffn_G1_R"][0]
    g2r = inputs["L0_ffn_G2_R"][:, :, 0, :]
    g1i = inputs["L0_ffn_G1_I"][0]
    g2i = inputs["L0_ffn_G2_I"][:, :, 0, :]

    def g1pack(g1):
        return np.ascontiguousarray(g1.transpose(1, 2, 0).reshape(1, 4096))

    def g2pack(g2):
        return np.ascontiguousarray(g2.transpose(0, 2, 1).reshape(1, 1024))

    cwm = np.ascontiguousarray(
        conv_w.reshape(NPT, 128, 4).transpose(1, 0, 2).reshape(128, 4 * NPT))

    return {
        "XS": xs, "XH": xh,
        "WUT": np.ascontiguousarray(w_up.T),
        "WDT": np.ascontiguousarray(inputs["L0_attn_w_down"].T),
        "BDF": bd["BDF"], "BDI": bd["BDI"], "BDH": bd["BDH"],
        "CWD": cwm,
        "BUPM": pcol(inputs["L0_attn_b_up"][:INNER]),
        "BUPZ": pcol(inputs["L0_attn_b_up"][INNER:]),
        "CBD": pcol(inputs["L0_attn_conv_b"]),
        "SKIPD": pcol(inputs["L0_attn_skip"]),
        "HALOM": halom, "PAIRM": pairm,
        "EYE": np.eye(128, dtype=np.float32),
        "BDOWND": np.ascontiguousarray(inputs["L0_attn_b_down"].reshape(1, D)),
        "BRD": np.ascontiguousarray(inputs["L0_ffn_bias_real"].reshape(1, 512)),
        "BID": np.ascontiguousarray(inputs["L0_ffn_bias_imag"].reshape(1, 512)),
        "G1R": g1pack(g1r), "G2R": g2pack(g2r),
        "G1I": g1pack(g1i), "G2I": g2pack(g2i),
        "CONS": np.tile(np.array([[LNEPS, 1.0, T * EPS, 2.0]], np.float32),
                        (128, 1)),
    }


def kernel(**inputs):
    inputs = {k: np.ascontiguousarray(np.asarray(v)) for k, v in inputs.items()}
    nc = build_nc()
    in_maps = [_host_inputs(inputs, k) for k in range(8)]
    res = run_bass_kernel_spmd(nc, in_maps, core_ids=list(range(8)))
    out = np.empty((B, S, D), np.float32)
    for k in range(8):
        b, h = k // 2, k % 2
        out[b, h * T:(h + 1) * T] = res.results[k]["OUT"]
    return out




# version 16
# speedup vs baseline: 1.1565x; 1.0097x over previous
import numpy as np
from collections import defaultdict
from contextlib import ExitStack

import concourse.bass as bass
import concourse.mybir as mybir
from concourse.bass_utils import run_bass_kernel_spmd

F32 = mybir.dt.float32
BF16 = mybir.dt.bfloat16
AF = mybir.ActivationFunctionType
OP = mybir.AluOpType

B, S, D = 4, 8192, 1024
INNER = 2048
NPT = INNER // 128
NDC = D // 128
T = 4096
TILE = 512
NT = T // TILE
NCH = TILE // 128
EPS = 1e-8
LNEPS = 1e-5
GEPS = 0.5 + 1e-8


def build_nc(n_tiles=NT, dbg=False):
    nt = n_tiles
    t_tok = nt * TILE
    nc = bass.Bass("TRN2", target_bir_lowering=False,
                   detect_race_conditions=False)

    EI = dict(kind="ExternalInput")
    XS = nc.dram_tensor("XS", [t_tok, D], F32, **EI).ap()
    XH = nc.dram_tensor("XH", [3, D], F32, **EI).ap()
    WUT = nc.dram_tensor("WUT", [D, 2 * INNER], F32, **EI).ap()
    WDT = nc.dram_tensor("WDT", [INNER, D], F32, **EI).ap()
    BDF_D = nc.dram_tensor("BDF", [128, NPT * 128], F32, **EI).ap()
    BDI_D = nc.dram_tensor("BDI", [128, NPT * 128], F32, **EI).ap()
    BDH_D = nc.dram_tensor("BDH", [128, NPT * 128], F32, **EI).ap()
    CWD = nc.dram_tensor("CWD", [128, 4 * NPT], F32, **EI).ap()
    BUPM_D = nc.dram_tensor("BUPM", [128, NPT], F32, **EI).ap()
    BUPZ_D = nc.dram_tensor("BUPZ", [128, NPT], F32, **EI).ap()
    CB_D = nc.dram_tensor("CBD", [128, NPT], F32, **EI).ap()
    SKIP_D = nc.dram_tensor("SKIPD", [128, NPT], F32, **EI).ap()
    HALOM_D = nc.dram_tensor("HALOM", [128, 1], F32, **EI).ap()
    PAIRM_D = nc.dram_tensor("PAIRM", [128, 16], F32, **EI).ap()
    EYE_D = nc.dram_tensor("EYE", [128, 128], F32, **EI).ap()
    BDOWN_D = nc.dram_tensor("BDOWND", [1, D], F32, **EI).ap()
    BR_D = nc.dram_tensor("BRD", [1, 512], F32, **EI).ap()
    BI_D = nc.dram_tensor("BID", [1, 512], F32, **EI).ap()
    G1R_D = nc.dram_tensor("G1R", [1, 4096], F32, **EI).ap()
    G2R_D = nc.dram_tensor("G2R", [1, 1024], F32, **EI).ap()
    G1I_D = nc.dram_tensor("G1I", [1, 4096], F32, **EI).ap()
    G2I_D = nc.dram_tensor("G2I", [1, 1024], F32, **EI).ap()
    CONS_D = nc.dram_tensor("CONS", [128, 4], F32, **EI).ap()

    OUT = nc.dram_tensor("OUT", [t_tok, D], F32, kind="ExternalOutput").ap()

    XN = nc.dram_tensor("XNs", [t_tok, D], BF16).ap()
    XN2 = nc.dram_tensor("XN2s", [t_tok, D], BF16).ap()
    AG = nc.dram_tensor("AGs", [NPT, 128, t_tok], BF16).ap()
    ODR = nc.dram_tensor("ODRs", [1024, 256], BF16).ap()
    ODI = nc.dram_tensor("ODIs", [1024, 256], BF16).ap()
    CCIN = nc.dram_tensor("CCINs", [128, 32], F32).ap()
    CCOUT = nc.dram_tensor("CCOUTs", [8 * 128, 32], F32).ap()

    st = ExitStack()
    st1 = ExitStack()

    def sb(stk, name, shape, dt):
        return stk.enter_context(nc.sbuf_tensor(name, shape, dt))

    def psm(stk, name, shape, dt=F32):
        return stk.enter_context(nc.psum_tensor(name, shape, dt))

    WUT_SB = sb(st, "wut", [128, NDC * INNER], BF16)
    EYE = sb(st, "eye", [128, 128], F32)
    EYEB = sb(st, "eyeb", [128, 128], BF16)
    BUPM = sb(st, "bupm", [128, NPT], F32)
    BUPZ = sb(st, "bupz", [128, NPT], F32)
    CBS = sb(st, "cbs", [128, NPT], F32)
    SKIPS = sb(st, "skips", [128, NPT], F32)
    HALOM = sb(st, "halom", [128, 1], F32)
    PAIRM = sb(st, "pairm", [128, 16], F32)
    ONESB = sb(st, "onesb", [1, 128], BF16)
    BDOWNB = sb(st, "bdownb", [1, D], BF16)
    BRB = sb(st, "brb", [1, 512], BF16)
    BIB = sb(st, "bib", [1, 512], BF16)
    RSTATE = sb(st, "rstate", [128, NPT], F32)
    PFSUM = sb(st, "pfsum", [128, NPT], F32)
    LNFACC = sb(st, "lnfacc", [128, NPT], F32)
    CCB = sb(st, "ccb", [128, 32], F32)
    GATH = sb(st, "gath", [128, 256], F32)
    SEL1 = sb(st, "sel1", [128, 32], F32)
    SEL2 = sb(st, "sel2", [128, 32], F32)
    HL = sb(st, "hl", [128, NPT], F32)
    TMP16 = sb(st, "tmp16", [128, NPT], F32)
    STATB = sb(st, "statb", [128, 12], F32)
    MV = sb(st, "mv", [128, 2], F32)
    SD1 = sb(st, "sd1", [128, 1], F32)
    RSTD = sb(st, "rstd", [128, 1], F32)
    NMR = sb(st, "nmr", [128, 1], F32)
    CONS = sb(st, "cons", [128, 4], F32)
    SCRQ = sb(st, "scrq", [128, 1024], F32)
    SMB = sb(st, "smb", [128, 1], F32)
    SSQ = sb(st, "ssq", [128, 1], F32)
    MEANB = sb(st, "meanb", [128, 1], F32)
    MSQ = sb(st, "msq", [128, 1], F32)
    VARB = sb(st, "varb", [128, 1], F32)
    JUNKV = sb(st, "junkv", [128, 16], F32)
    JUNKA = sb(st, "junka", [128, 16], F32)
    XNT = [sb(st, "xnt0", [128, NDC * TILE], BF16),
           sb(st, "xnt1", [128, NDC * TILE], BF16)]

    BDF_SB = sb(st1, "bdf", [128, NPT * 128], BF16)
    BDI_SB = sb(st1, "bdi", [128, NPT * 128], BF16)
    BDH_SB = sb(st1, "bdh", [128, NPT * 128], BF16)
    DIAG = sb(st1, "diag", [128, 4 * NPT * 128], BF16)
    CWS = sb(st1, "cws", [128, 4 * NPT], F32)
    BUPMK = sb(st1, "bupmk", [128, NPT], F32)
    XMIN = sb(st1, "xmin", [128, NPT * 520], BF16)
    XB = [sb(st1, "xb0", [128, 2 * D], F32), sb(st1, "xb1", [128, 2 * D], F32)]
    XNB = [sb(st1, "xnb0", [128, D], BF16), sb(st1, "xnb1", [128, D], BF16)]
    AGT = sb(st1, "agt", [128, NPT * TILE], BF16)
    FIH = [sb(st1, "fih0", [128, 3 * TILE], F32),
           sb(st1, "fih1", [128, 3 * TILE], F32)]
    UB = [sb(st1, "ub0", [128, TILE], F32), sb(st1, "ub1", [128, TILE], F32)]
    WB = [sb(st1, "wb0", [128, TILE], F32), sb(st1, "wb1", [128, TILE], F32)]
    LNH = sb(st1, "lnh", [128, TILE], F32)
    GB = sb(st1, "gb", [128, TILE], F32)
    GHB = [sb(st1, "ghb0", [128, TILE], F32), sb(st1, "ghb1", [128, TILE], F32)]
    FB = [sb(st1, "fb0", [128, TILE], F32), sb(st1, "fb1", [128, TILE], F32)]
    TB = sb(st1, "tb", [128, TILE], F32)
    GGB = sb(st1, "ggb", [128, TILE], F32)
    RSOUT = sb(st1, "rsout", [128, TILE], F32)

    P_MM = [psm(st1, "pmm0", [128, TILE]), psm(st1, "pmm1", [128, TILE])]
    P_CV = [psm(st1, "pcv0", [128, TILE]), psm(st1, "pcv1", [128, TILE])]

    G1RS = sb(st1, "g1rs", [1, 4096], BF16)
    G2RS = sb(st1, "g2rs", [1, 1024], BF16)
    G1IS = sb(st1, "g1is", [1, 4096], BF16)
    G2IS = sb(st1, "g2is", [1, 1024], BF16)
    OSB = sb(st1, "osb", [128, 256], BF16)
    XHST = sb(st1, "xhst", [3, 12], F32)
    XHMV = sb(st1, "xhmv", [3, 2], F32)
    XHSD = sb(st1, "xhsd", [3, 1], F32)
    XHRS = sb(st1, "xhrs", [3, 1], F32)
    XHNM = sb(st1, "xhnm", [3, 1], F32)
    XHT = sb(st1, "xht", [128, NDC * 3], BF16)
    XHS = XB[0][0:3, 0:D]
    XHN = XB[0][0:3, D:2 * D]
    P_AUX = psm(st1, "paux", [128, 256])
    P_G = psm(st1, "pg", [128, 3 * TILE])

    sem_names = ["SX", "SXNT", "SXNW", "SAG", "SW", "SMI", "SCC", "SACTLD",
                 "SXN2W", "SXN2T", "SOUT", "SPE", "SACT", "SVEC", "SGP"]
    SEM = {n: st.enter_context(nc.semaphore(n)) for n in sem_names}

    cnt = defaultdict(int)
    mark = {}

    def inc(s, n, tag=None):
        if s in ("SVEC", "SACT") and last_nops:
            n = last_nops.pop()
        cnt[s] += n
        if tag is not None:
            mark[(s, tag)] = cnt[s]
        return cnt[s]

    def at(s, tag):
        return mark[(s, tag)]

    plans = {"ph1": [], "ph2": []}
    cur = ["ph1"]

    _WRAP = ("activation", "tensor_scalar", "tensor_tensor",
             "scalar_tensor_tensor", "tensor_tensor_scan", "tensor_copy",
             "tensor_reduce", "reciprocal", "memset", "memzero", "select",
             "copy", "add", "mul", "sign", "sqrt", "square", "reduce_max",
             "reduce_sum", "bn_stats", "bn_aggr")

    class _Counter:
        def __init__(self):
            self.n = 0

        def __getattr__(self, name):
            if name in _WRAP:
                def f(*a, **k):
                    self.n += 1
                    return self
                return f
            def g(*a, **k):
                return self
            return g

        def then_inc(self, *a, **k):
            return self

    GROUP_SEMS = False

    class _Proxy:
        def __init__(self, e, sem, base, nops):
            self._e = e
            self._sem = sem
            self._n = base
            self._base = base
            self._nops = nops

        def __getattr__(self, name):
            e = self._e
            if name in _WRAP:
                def f(*a, **k):
                    if not GROUP_SEMS:
                        e.wait_ge(self._sem, self._n)
                    elif self._n == self._base:
                        e.wait_ge(self._sem, self._base)
                    r = getattr(e, name)(*a, **k)
                    if not GROUP_SEMS or self._n == self._base + self._nops - 1:
                        r.then_inc(self._sem,
                                   1 if not GROUP_SEMS else self._nops)
                    self._n += 1
                    return r
                return f
            return getattr(e, name)

    _selfsem = {"vector": "SVEC", "scalar": "SACT"}
    last_nops = []

    def em(eng, waits, fn, tinc=None):
        w = [(SEM[s], v) for (s, v) in waits]
        if eng in _selfsem:
            c = _Counter()
            fn(c)
            last_nops.append(c.n)
            ssem = SEM[_selfsem[eng]]
            base = cnt[_selfsem[eng]]

            if tinc is not None and tinc[0] == _selfsem[eng]:
                tinc = None
            ti = (SEM[tinc[0]], tinc[1]) if tinc else None

            def run(e, w=w, fn=fn, ssem=ssem, base=base, nops=c.n, ti=ti):
                for sh, v in w:
                    e.wait_ge(sh, v)
                p = _Proxy(e, ssem, base, nops)
                r = fn(p)
                if ti is not None:
                    r.then_inc(ti[0], ti[1])
                assert p._n == base + nops, (
                    f"op count mismatch: {p._n} != {base}+{nops}")
            plans[cur[0]].append((eng, run))
            return

        ti = (SEM[tinc[0]], tinc[1]) if tinc else None

        def run(e, w=w, fn=fn, ti=ti):
            for sh, v in w:
                e.wait_ge(sh, v)
            r = fn(e)
            if ti is not None:
                r.then_inc(ti[0], ti[1])
        plans[cur[0]].append((eng, run))

    def _inc_orig(s, n, tag=None):
        cnt[s] += n
        if tag is not None:
            mark[(s, tag)] = cnt[s]
        return cnt[s]

    def dma(eng, waits, out_ap, in_ap, s, tag=None, transpose=False):
        inc(s, 16, tag)
        em(eng, waits,
           (lambda e, o=out_ap, i=in_ap, tr=transpose:
            e.dma_start(out=o, in_=i, transpose=tr)),
           tinc=(s, 16))

    if dbg:
        DXHT = nc.dram_tensor("DXHT", [128, NDC * 3], BF16,
                              kind="ExternalOutput").ap()
        DXHN = nc.dram_tensor("DXHN", [3, D], F32, kind="ExternalOutput").ap()
        DXHRS = nc.dram_tensor("DXHRS", [3, 2], F32,
                               kind="ExternalOutput").ap()
        DXMIN = nc.dram_tensor("DXMIN", [128, NPT * 520], BF16,
                               kind="ExternalOutput").ap()


    RD = 1.0 / float(D)

    def stats_ops(e, src_ap, np_):
        e.tensor_scalar(out=SCRQ[0:np_, :], in0=src_ap, scalar1=0.0,
                        scalar2=0.0, op0=OP.add, op1=OP.add,
                        accum_out=SMB[0:np_, 0:1])
        e.scalar_tensor_tensor(out=SCRQ[0:np_, :], in0=src_ap, scalar=1.0,
                               in1=src_ap, op0=OP.mult, op1=OP.mult,
                               accum_out=SSQ[0:np_, 0:1])
        e.tensor_scalar(out=MEANB[0:np_, 0:1], in0=SMB[0:np_, 0:1],
                        scalar1=RD, scalar2=None, op0=OP.mult)
        e.tensor_tensor(out=MSQ[0:np_, 0:1], in0=MEANB[0:np_, 0:1],
                        in1=MEANB[0:np_, 0:1], op=OP.mult)
        e.scalar_tensor_tensor(out=VARB[0:np_, 0:1],
                               in0=SSQ[0:np_, 0:1], scalar=RD,
                               in1=MSQ[0:np_, 0:1], op0=OP.mult,
                               op1=OP.subtract)
        return e.tensor_tensor(out=JUNKV[0:np_, 0:1], in0=VARB[0:np_, 0:1],
                               in1=VARB[0:np_, 0:1], op=OP.add)
    for dc in range(NDC):
        dma("gpsimd", [], WUT_SB[:, dc * INNER:(dc + 1) * INNER],
            WUT[dc * 128:(dc + 1) * 128, 0:INNER], "SW")
    for bd_d, bd_s in ((BDF_D, BDF_SB), (BDI_D, BDI_SB), (BDH_D, BDH_SB)):
        dma("gpsimd", [], bd_s[:, :], bd_d, "SW")
    dma("gpsimd", [], CWS[:, :], CWD, "SW")
    dma("gpsimd", [], EYE[:, :], EYE_D, "SW")
    dma("gpsimd", [], BUPM[:, :], BUPM_D, "SW")
    dma("gpsimd", [], BUPZ[:, :], BUPZ_D, "SW")
    dma("gpsimd", [], CBS[:, :], CB_D, "SW")
    dma("gpsimd", [], SKIPS[:, :], SKIP_D, "SW")
    dma("gpsimd", [], HALOM[:, :], HALOM_D, "SW")
    dma("gpsimd", [], PAIRM[:, :], PAIRM_D, "SW")
    dma("gpsimd", [], BDOWNB[:, :], BDOWN_D, "SW")
    dma("gpsimd", [], BRB[:, :], BR_D, "SW")
    dma("gpsimd", [], BIB[:, :], BI_D, "SW")
    dma("gpsimd", [], G1RS[:, :], G1R_D, "SW")
    dma("gpsimd", [], G2RS[:, :], G2R_D, "SW")
    dma("gpsimd", [], G1IS[:, :], G1I_D, "SW")
    dma("gpsimd", [], G2IS[:, :], G2I_D, "SW")
    dma("gpsimd", [], CONS[:, :], CONS_D, "SW")
    w_all = cnt["SW"]

    dma("sync", [], XHS, XH, "SX", "xhalo")

    def v_init(e):
        e.memset(RSTATE[:, :], 0.0)
        e.memset(PFSUM[:, :], 0.0)
        e.memset(ONESB[:, :], 1.0)
        return e.tensor_copy(EYEB[:, :], EYE[:, :])
    em("vector", [("SW", w_all)], v_init, tinc=("SVEC", 1))
    inc("SVEC", 1, "vinit")

    def v_diag(e):
        r = None
        for pt in range(NPT):
            for k in range(4):
                j = pt * 4 + k
                r = e.tensor_scalar(out=DIAG[:, j * 128:(j + 1) * 128],
                                    in0=EYE[:, :], scalar1=CWS[:, j:j + 1],
                                    scalar2=None, op0=OP.mult)
        return r
    em("vector", [], v_diag, tinc=("SVEC", 1))
    inc("SVEC", 1, "vdiag")

    def v_bupmk(e):
        return e.tensor_scalar(out=BUPMK[:, :], in0=BUPM[:, :],
                               scalar1=HALOM[:, 0:1], scalar2=None, op0=OP.mult)
    em("vector", [], v_bupmk, tinc=("SVEC", 1))
    inc("SVEC", 1, "vbupmk")

    def v_halostats(e):
        return stats_ops(e, XHS, 3)
    em("vector", [("SX", at("SX", "xhalo"))], v_halostats, tinc=("SVEC", 1))
    inc("SVEC", 1, "vhstat")

    em("scalar", [("SVEC", at("SVEC", "vhstat")), ("SW", w_all)],
       lambda e: (e.activation(out=XHSD[0:3, 0:1], in_=VARB[0:3, 0:1],
                               func=AF.Ln, bias=CONS[0:3, 0:1]),
                  e.activation(out=XHRS[0:3, 0:1], in_=XHSD[0:3, 0:1],
                               func=AF.Exp, scale=-0.5),
                  e.activation(out=JUNKA[0:3, 0:1], in_=XHRS[0:3, 0:1],
                               func=AF.Copy))[2],
       tinc=("SACT", 1))
    inc("SACT", 1, "ahsqrt")

    def v_hrs(e):
        e.scalar_tensor_tensor(out=XHNM[0:3, 0:1], in0=MEANB[0:3, 0:1],
                               scalar=-1.0, in1=XHRS[0:3, 0:1],
                               op0=OP.mult, op1=OP.mult)
        return e.tensor_tensor(out=JUNKV[0:3, 1:2], in0=XHRS[0:3, 0:1],
                               in1=XHNM[0:3, 0:1], op=OP.add)
    em("vector", [("SACT", at("SACT", "ahsqrt"))], v_hrs, tinc=("SVEC", 1))
    inc("SVEC", 1, "vhrs")

    em("scalar", [("SVEC", at("SVEC", "vhrs"))],
       lambda e: e.activation(out=XHN, in_=XHS, func=AF.Identity,
                              scale=XHRS[0:3, 0:1], bias=XHNM[0:3, 0:1]),
       tinc=("SACT", 1))
    inc("SACT", 1, "ahnorm")

    for dc in range(NDC):
        em("tensor",
           [("SACT", at("SACT", "ahnorm")), ("SVEC", at("SVEC", "vinit"))] +
           ([("SACT", at("SACT", ("ahxt", dc - 1)))] if dc > 0 else []),
           (lambda e, dc=dc: e.transpose(out=P_AUX[:, 0:3],
                                         in_=XHN[:, dc * 128:(dc + 1) * 128],
                                         identity=EYE[0:3, 0:3])),
           tinc=("SPE", 1))
        inc("SPE", 1, ("hxt", dc))
        em("scalar", [("SPE", at("SPE", ("hxt", dc)))],
           (lambda e, dc=dc: e.activation(out=XHT[:, dc * 3:(dc + 1) * 3],
                                          in_=P_AUX[:, 0:3], func=AF.Copy)),
           tinc=("SACT", 1))
        inc("SACT", 1, ("ahxt", dc))

    for ep in range(NPT):
        def pe_halo(e, ep=ep):
            r = None
            for dc in range(NDC):
                r = e.matmul(out=P_MM[ep % 2][:, 0:3],
                             lhsT=WUT_SB[:, dc * INNER + ep * 128:
                                         dc * INNER + (ep + 1) * 128],
                             rhs=XHT[:, dc * 3:(dc + 1) * 3],
                             start=(dc == 0), stop=(dc == NDC - 1))
            return r
        waits = [("SACT", at("SACT", ("ahxt", NDC - 1))), ("SW", w_all)]
        if ep >= 2:
            waits.append(("SACT", at("SACT", ("ahev", ep - 2))))
        em("tensor", waits, pe_halo, tinc=("SPE", 1))
        inc("SPE", 1, ("hmm", ep))
        em("scalar", [("SPE", at("SPE", ("hmm", ep))),
                      ("SVEC", at("SVEC", "vbupmk"))],
           (lambda e, ep=ep: e.activation(out=XMIN[:, ep * 520: ep * 520 + 3],
                                          in_=P_MM[ep % 2][:, 0:3], func=AF.Identity,
                                          scale=HALOM[:, 0:1],
                                          bias=BUPMK[:, ep:ep + 1])),
           tinc=("SACT", 1))
        inc("SACT", 1, ("ahev", ep))

    if dbg:
        dma("scalar", [("SACT", at("SACT", ("ahev", NPT - 1)))],
            DXHT, XHT[:, :], "SMI")
        dma("scalar", [("SACT", at("SACT", "ahnorm"))], DXHN, XHN, "SMI")
        dma("scalar", [("SVEC", at("SVEC", "vhrs"))], DXHRS[0:3, 0:2],
            XHMV[0:3, 0:2], "SMI")

    prev_tag = None
    for oi, (g1, g2, od) in enumerate(((G1RS, G2RS, ODR), (G1IS, G2IS, ODI))):
        for q in range(8):
            def pe_outer(e, g1=g1, g2=g2, q=q):
                r = None
                for c in range(4):
                    r = e.matmul(out=P_AUX[:, 0:256],
                                 lhsT=g1[0:1, c * 1024 + q * 128:
                                         c * 1024 + (q + 1) * 128],
                                 rhs=g2[0:1, c * 256:(c + 1) * 256],
                                 start=(c == 0), stop=(c == 3))
                return r
            waits = [("SW", w_all), ("SACT", at("SACT", ("ahev", NPT - 1)))]
            if prev_tag is not None:
                waits.append(("SACT", at("SACT", ("wbev", prev_tag))))
            em("tensor", waits, pe_outer, tinc=("SPE", 1))
            inc("SPE", 1, ("wbmm", (oi, q)))
            waits = [("SPE", at("SPE", ("wbmm", (oi, q))))]
            if prev_tag is not None:
                waits.append(("SMI", at("SMI", ("wbod", prev_tag))))
            em("scalar", waits,
               (lambda e: (e.activation(out=OSB[:, :], in_=P_AUX[:, 0:256],
                                        func=AF.Copy),
                           e.activation(out=JUNKA[:, 0:1], in_=OSB[:, 255:256],
                                        func=AF.Copy))[1]),
               tinc=("SACT", 1))
            inc("SACT", 1, ("wbev", (oi, q)))
            dma("scalar", [("SACT", at("SACT", ("wbev", (oi, q))))],
                od[q * 128:(q + 1) * 128, :], OSB[:, :],
                "SMI", ("wbod", (oi, q)))
            prev_tag = (oi, q)
    wb_last = prev_tag


    xs4 = XS.rearrange("(t c p) d -> t c p d", c=NCH, p=128)
    xn4 = XN.rearrange("(t c p) d -> t c p d", c=NCH, p=128)

    def emit_xload(t, hf):
        waits = []
        if t == 0 and hf == 0:
            waits += [("SPE", at("SPE", ("wbmm", wb_last))),
                      ("SMI", at("SMI", ("wbod", wb_last))),
                      ("SACT", at("SACT", ("ahev", NPT - 1)))]
        if t >= 1:
            waits.append(("SVEC", at("SVEC", ("aln", t - 1, 2 * hf + 1))))
        dma("sync", waits,
            XB[hf][:, :].rearrange("p (c d) -> p c d", c=2),
            xs4[t, 2 * hf:2 * hf + 2].rearrange("c p d -> p c d"),
            "SX", ("x", t, hf))

    def emit_ln(t, c):
        hf, ci = c // 2, c % 2
        def v_stats(e, hf=hf, ci=ci):
            return stats_ops(e, XB[hf][:, ci * D:(ci + 1) * D], 128)
        w = [("SX", at("SX", ("x", t, hf)))]
        if (t, c) != (0, 0):
            pc = (t, c - 1) if c > 0 else (t - 1, NCH - 1)
            w.append(("SVEC", at("SVEC", ("aln", pc[0], pc[1]))))
        em("vector", w, v_stats, tinc=("SVEC", 1))
        inc("SVEC", 1, ("vstat", t, c))

        em("scalar", [("SVEC", at("SVEC", ("vstat", t, c)))],
           (lambda e: (e.activation(out=SD1[:, 0:1], in_=VARB[:, 0:1],
                                    func=AF.Ln, bias=CONS[:, 0:1]),
                       e.activation(out=RSTD[:, 0:1], in_=SD1[:, 0:1],
                                    func=AF.Exp, scale=-0.5),
                       e.activation(out=JUNKA[:, 0:1], in_=RSTD[:, 0:1],
                                    func=AF.Copy))[2]),
           tinc=("SACT", 1))
        inc("SACT", 1, ("asq", t, c))

        def v_rs(e):
            e.scalar_tensor_tensor(out=NMR[:, 0:1], in0=MEANB[:, 0:1],
                                   scalar=-1.0, in1=RSTD[:, 0:1],
                                   op0=OP.mult, op1=OP.mult)
            return e.tensor_tensor(out=JUNKV[:, 1:2], in0=RSTD[:, 0:1],
                                   in1=NMR[:, 0:1], op=OP.add)
        em("vector", [("SACT", at("SACT", ("asq", t, c)))], v_rs,
           tinc=("SVEC", 1))
        inc("SVEC", 1, ("vrs", t, c))

        w = [("SVEC", at("SVEC", ("vrs", t, c)))]
        if (t, c) >= (0, 2):
            pc = (t, c - 2) if c >= 2 else (t - 1, NCH - 2 + c)
            w.append(("SXNW", at("SXNW", ("xnw", pc[0], pc[1]))))
        em("vector", w,
           (lambda e, c=c, hf=hf, ci=ci: (
               e.tensor_scalar(out=XNB[c % 2][:, :],
                               in0=XB[hf][:, ci * D:(ci + 1) * D],
                               scalar1=RSTD[:, 0:1], scalar2=NMR[:, 0:1],
                               op0=OP.mult, op1=OP.add),
               e.tensor_copy(JUNKA[:, 0:1], XNB[c % 2][:, D - 1:D]))[1]),
           tinc=("SVEC", 1))
        inc("SVEC", 1, ("aln", t, c))
        dma("scalar", [("SVEC", at("SVEC", ("aln", t, c)))],
            xn4[t, c], XNB[c % 2][:, :], "SXNW", ("xnw", t, c))

    def emit_xnt(t):
        bi = t % 2
        for dc in range(NDC):
            w = [("SXNW", at("SXNW", ("xnw", t, NCH - 1)))]
            if t >= 2:
                w.append(("SPE", at("SPE", ("mm", t - 2, NPT - 1))))
            dma("sync", w, XNT[bi][:, dc * TILE:(dc + 1) * TILE],
                XN[t * TILE:(t + 1) * TILE, dc * 128:(dc + 1) * 128],
                "SXNT", ("xnt", t, dc), transpose=True)

        def v_settle(e, bi=bi):
            r = None
            for dc in range(NDC):
                r = e.tensor_copy(JUNKA[:, 2:3],
                                  XNT[bi][:, (dc + 1) * TILE - 1:
                                          (dc + 1) * TILE])
            return r
        em("vector", [("SXNT", at("SXNT", ("xnt", t, NDC - 1)))], v_settle,
           tinc=("SVEC", 1))
        inc("SVEC", 1, ("vxs", t))

    def emit_mmaev(t, ep):
        bi = t % 2
        def pe_mm(e, ep=ep, bi=bi):
            r = None
            for dc in range(NDC):
                r = e.matmul(out=P_MM[ep % 2][:, :],
                             lhsT=WUT_SB[:, dc * INNER + ep * 128:
                                         dc * INNER + (ep + 1) * 128],
                             rhs=XNT[bi][:, dc * TILE:(dc + 1) * TILE],
                             start=(dc == 0), stop=(dc == NDC - 1))
            return r
        w = [("SVEC", at("SVEC", ("vxs", t)))]
        if t == 0 and ep < 2:
            w.append(("SACT", at("SACT", ("ahev", NPT - 1))))
        if (t, ep) >= (0, 2):
            pe_, ep_ = (t, ep - 2) if ep >= 2 else (t - 1, NPT - 2 + ep)
            w.append(("SVEC", at("SVEC", ("aev", pe_, ep_))))
        em("tensor", w, pe_mm, tinc=("SPE", 1))
        inc("SPE", 1, ("mm", t, ep))

        w = [("SPE", at("SPE", ("mm", t, ep)))]
        if t > 0:
            w.append(("SVEC", at("SVEC", ("vhc", t - 1, ep))))
        em("vector", w,
           (lambda e, ep=ep: (e.tensor_scalar(
               out=XMIN[:, ep * 520 + 3: ep * 520 + 515],
               in0=P_MM[ep % 2][:, :], scalar1=BUPM[:, ep:ep + 1],
               scalar2=None, op0=OP.add),
               e.tensor_copy(JUNKA[:, 0:1],
                             XMIN[:, ep * 520 + 514: ep * 520 + 515]))[1]),
           tinc=("SVEC", 1))
        inc("SVEC", 1, ("aev", t, ep))

    def emit_cvsilu(t, pt):
        def pe_cv(e, pt=pt):
            r = None
            for k in range(4):
                j = pt * 4 + k
                r = e.matmul(out=P_CV[pt % 2][:, :],
                             lhsT=DIAG[:, j * 128:(j + 1) * 128],
                             rhs=XMIN[:, pt * 520 + k: pt * 520 + k + 512],
                             start=(k == 0), stop=(k == 3))
            return r
        w = [("SVEC", at("SVEC", ("aev", t, pt))),
             ("SVEC", at("SVEC", "vdiag"))]
        if (t, pt) >= (0, 2):
            pe_, pt_ = (t, pt - 2) if pt >= 2 else (t - 1, NPT - 2 + pt)
            w.append(("SACT", at("SACT", ("asilu", pe_, pt_))))
        em("tensor", w, pe_cv, tinc=("SPE", 1))
        inc("SPE", 1, ("cv", t, pt))

        w = [("SPE", at("SPE", ("cv", t, pt)))]
        if t > 0:
            w.append(("SPE", at("SPE", ("gt", t - 1, pt))))
            w.append(("SAG", at("SAG", ("ag", t - 1))))
        em("scalar", w,
           (lambda e, pt=pt: (e.activation(
               out=AGT[:, pt * TILE:(pt + 1) * TILE],
               in_=P_CV[pt % 2][:, :], func=AF.Silu,
               bias=CBS[:, pt:pt + 1]),
               e.activation(out=JUNKA[:, 0:1],
                            in_=AGT[:, (pt + 1) * TILE - 1:(pt + 1) * TILE],
                            func=AF.Copy))[1]),
           tinc=("SACT", 1))
        inc("SACT", 1, ("asilu", t, pt))

    def emit_vhc(t, pt):
        def v_hc(e, pt=pt):
            return e.tensor_copy(XMIN[:, pt * 520: pt * 520 + 3],
                                 XMIN[:, pt * 520 + 512: pt * 520 + 515])
        em("vector", [("SPE", at("SPE", ("cv", t, pt)))], v_hc,
           tinc=("SVEC", 1))
        inc("SVEC", 1, ("vhc", t, pt))

    def emit_spill(t):
        dma("scalar", [("SACT", at("SACT", ("asilu", t, NPT - 1)))],
            AG[:, :, t * TILE:(t + 1) * TILE].rearrange("q p s -> p q s"),
            AGT[:, :].rearrange("p (q s) -> p q s", q=NPT), "SAG", ("ag", t))

    def prevslot(t, pt, d):
        k = t * NPT + pt - d
        return (k // NPT, k % NPT) if k >= 0 else None

    def emit_gt(t, pt):
        def pe_gt(e, pt=pt):
            e.matmul(out=P_G[:, 0:512],
                     lhsT=BDF_SB[:, pt * 128:(pt + 1) * 128],
                     rhs=AGT[:, pt * TILE:(pt + 1) * TILE],
                     start=True, stop=True)
            e.matmul(out=P_G[:, 512:1024],
                     lhsT=BDI_SB[:, pt * 128:(pt + 1) * 128],
                     rhs=AGT[:, pt * TILE:(pt + 1) * TILE],
                     start=True, stop=True)
            return e.matmul(out=P_G[:, 1024:1536],
                            lhsT=BDH_SB[:, pt * 128:(pt + 1) * 128],
                            rhs=AGT[:, pt * TILE:(pt + 1) * TILE],
                            start=True, stop=True)
        w = [("SACT", at("SACT", ("asilu", t, pt)))]
        p1 = prevslot(t, pt, 1)
        if p1:
            w.append(("SACT", at("SACT", ("a1",) + p1)))
            w.append(("SVEC", at("SVEC", ("v1",) + p1)))
        em("tensor", w, pe_gt, tinc=("SPE", 1))
        inc("SPE", 1, ("gt", t, pt))

    def emit_a1(t, pt):
        pb = pt % 2
        def a_1(e, pb=pb):
            e.activation(out=FIH[pb][:, :], in_=P_G[:, :], func=AF.Exp,
                         scale=-1.0)
            e.activation(out=LNH[:, :], in_=FIH[pb][:, 1024:1536],
                         func=AF.Ln, bias=CONS[:, 1:2])
            e.activation(out=GB[:, :], in_=LNH[:, :], func=AF.Exp,
                         scale=-1.0)
            return e.activation(out=JUNKA[:, 0:2],
                                in_=FIH[pb][:, 1022:1024], func=AF.Copy)
        w = [("SPE", at("SPE", ("gt", t, pt)))]
        p2 = prevslot(t, pt, 2)
        if p2:
            w.append(("SVEC", at("SVEC", ("v2",) + p2)))
        em("scalar", w, a_1, tinc=("SACT", 1))
        inc("SACT", 1, ("a1", t, pt))

    def emit_a3(t, pt):
        def a_3(e, pt=pt):
            return e.activation(out=LNH[:, :], in_=FB[pt % 2][:, :], func=AF.Ln,
                                accum_out=LNFACC[:, pt:pt + 1])
        em("scalar", [("SVEC", at("SVEC", ("v2", t, pt)))], a_3,
           tinc=("SACT", 1))
        inc("SACT", 1, ("a3", t, pt))

    def emit_a2(t, pt):
        pb = pt % 2
        def a_2(e, pb=pb):
            e.activation(out=LNH[:, :], in_=UB[pb][:, :], func=AF.Ln,
                         bias=CONS[:, 3:4])
            e.activation(out=WB[pb][:, :], in_=LNH[:, :], func=AF.Exp,
                         scale=-1.0)
            return e.activation(out=JUNKA[:, 0:1], in_=WB[pb][:, 511:512],
                                func=AF.Copy)
        em("scalar", [("SVEC", at("SVEC", ("v1", t, pt)))], a_2,
           tinc=("SACT", 1))
        inc("SACT", 1, ("a2", t, pt))

    def emit_v1(t, pt):
        pb = pt % 2
        p3 = prevslot(t, pt, 3)
        def v_1(e, pb=pb, p3=p3):
            if p3:
                pp = p3[1]
                e.tensor_tensor(out=PFSUM[:, pp:pp + 1],
                                in0=PFSUM[:, pp:pp + 1],
                                in1=LNFACC[:, pp:pp + 1], op=OP.add)
            e.tensor_tensor(out=UB[pb][:, :], in0=FIH[pb][:, 0:512],
                            in1=FIH[pb][:, 512:1024], op=OP.add)
            e.tensor_tensor(out=JUNKV[:, 3:4], in0=UB[pb][:, 511:512],
                            in1=UB[pb][:, 511:512], op=OP.add)
            return e.scalar_tensor_tensor(out=GHB[pb][:, :],
                                          in0=P_G[:, 1024:1536],
                                          scalar=GEPS, in1=GB[:, :],
                                          op0=OP.add, op1=OP.max)
        em("vector", [("SACT", at("SACT", ("a1", t, pt)))], v_1,
           tinc=("SVEC", 1))
        inc("SVEC", 1, ("v1", t, pt))

    def emit_v2(t, pt):
        pb = pt % 2
        def v_2(e, pt=pt, pb=pb):
            e.scalar_tensor_tensor(out=FB[pb][:, :], in0=FIH[pb][:, 512:1024],
                                   scalar=1.0, in1=WB[pb][:, :],
                                   op0=OP.add, op1=OP.mult)
            e.scalar_tensor_tensor(out=TB[:, :], in0=FIH[pb][:, 0:512],
                                   scalar=1.0, in1=WB[pb][:, :],
                                   op0=OP.add, op1=OP.mult)
            e.tensor_tensor(out=GGB[:, :], in0=TB[:, :], in1=GHB[pb][:, :],
                            op=OP.mult)
            e.tensor_tensor_scan(out=RSOUT[:, :], data0=FB[pb][:, :],
                                 data1=GGB[:, :],
                                 initial=RSTATE[:, pt:pt + 1],
                                 op0=OP.mult, op1=OP.add)
            e.tensor_copy(RSTATE[:, pt:pt + 1], RSOUT[:, 511:512])
            return e.tensor_tensor(out=JUNKV[:, 4:5], in0=FB[pb][:, 511:512],
                                   in1=FB[pb][:, 511:512], op=OP.add)
        em("vector", [("SACT", at("SACT", ("a2", t, pt)))], v_2,
           tinc=("SVEC", 1))
        inc("SVEC", 1, ("v2", t, pt))

    def emit_chain_slot(t, pt):
        emit_gt(t, pt)
        p1 = prevslot(t, pt, 1)
        if p1:
            emit_a2(*p1)
        emit_a1(t, pt)
        p2 = prevslot(t, pt, 2)
        if p2:
            emit_a3(*p2)
        emit_v1(t, pt)
        if p1:
            emit_v2(*p1)

    def emit_prep(t, slot):
        if slot == 0:
            emit_spill_t = t - 1
            if emit_spill_t >= 0:
                emit_spill(emit_spill_t)
            emit_xload(t, 0)
            emit_ln(t, 0)
        elif slot == 1:
            emit_ln(t, 1)
        elif slot == 2:
            emit_xload(t, 1)
            emit_ln(t, 2)
        elif slot == 3:
            emit_ln(t, 3)
        elif slot == 4:
            emit_xnt(t)
        elif 5 <= slot <= 12:
            for ep in (2 * (slot - 5), 2 * (slot - 5) + 1):
                emit_mmaev(t, ep)
        if 8 <= slot <= 15:
            for pt2 in (2 * (slot - 8), 2 * (slot - 8) + 1):
                emit_cvsilu(t, pt2)

    for slot in range(16):
        emit_prep(0, slot)

    for t in range(nt):
        for pt in range(NPT):
            emit_chain_slot(t, pt)
            if t < nt - 1:
                if pt < 4:
                    for pt2 in range(4 * pt, 4 * pt + 4):
                        emit_vhc(t, pt2)
                emit_prep(t + 1, pt)
        if t == nt - 1:
            emit_spill(nt - 1)
            L = (nt - 1, NPT - 1)
            emit_a3(*prevslot(nt - 1, NPT - 1, 1))
            emit_a2(*L)
            emit_v2(*L)
            emit_a3(*L)

    def v_ccb(e):
        for pp in (NPT - 3, NPT - 2, NPT - 1):
            e.tensor_tensor(out=PFSUM[:, pp:pp + 1], in0=PFSUM[:, pp:pp + 1],
                            in1=LNFACC[:, pp:pp + 1], op=OP.add)
        e.tensor_copy(CCB[:, 0:16], RSTATE[:, :])
        return e.tensor_tensor(out=JUNKV[:, 0:16], in0=CCB[:, 0:16],
                               in1=CCB[:, 0:16], op=OP.add)
    em("vector", [("SVEC", at("SVEC", ("v2", nt - 1, NPT - 1))),
                  ("SACT", at("SACT", ("a3", nt - 1, NPT - 1))),
                  ("SACT", at("SACT", ("a3", nt - 1, NPT - 2))),
                  ("SACT", at("SACT", ("a3", nt - 1, NPT - 3)))], v_ccb,
       tinc=("SVEC", 1))
    inc("SVEC", 1, "vccb")

    em("scalar", [("SVEC", at("SVEC", "vccb"))],
       (lambda e: (e.activation(out=CCB[:, 16:32], in_=PFSUM[:, :], func=AF.Exp,
                                bias=CONS[:, 2:3]),
                   e.activation(out=JUNKA[:, 0:16], in_=CCB[:, 16:32],
                                func=AF.Copy))[1]),
       tinc=("SACT", 1))
    inc("SACT", 1, "apf")

    dma("gpsimd", [("SVEC", at("SVEC", "vccb")), ("SACT", at("SACT", "apf"))],
        CCIN, CCB[:, :], "SGP", "ccin")

    def g_cc(e):
        return e.collective_compute(
            "AllGather", OP.bypass, replica_groups=[list(range(8))],
            ins=[CCIN.opt()], outs=[CCOUT.opt()])
    em("gpsimd", [("SGP", at("SGP", "ccin"))], g_cc, tinc=("SCC", 1))
    inc("SCC", 1, "cc")

    dma("gpsimd", [("SCC", at("SCC", "cc"))],
        GATH[:, :].rearrange("p (r c) -> p r c", r=8),
        CCOUT.rearrange("(r p) c -> p r c", p=128), "SGP", "gath")

    def v_hlast(e):
        e.tensor_scalar(out=SEL1[:, :], in0=GATH[:, 0:32],
                        scalar1=PAIRM[:, 0:1], scalar2=None, op0=OP.mult)
        e.tensor_scalar(out=SEL2[:, :], in0=GATH[:, 0:32],
                        scalar1=PAIRM[:, 8:9], scalar2=None, op0=OP.mult)
        for r in range(1, 8):
            e.scalar_tensor_tensor(out=SEL1[:, :],
                                   in0=GATH[:, 32 * r:32 * (r + 1)],
                                   scalar=PAIRM[:, r:r + 1], in1=SEL1[:, :],
                                   op0=OP.mult, op1=OP.add)
            e.scalar_tensor_tensor(out=SEL2[:, :],
                                   in0=GATH[:, 32 * r:32 * (r + 1)],
                                   scalar=PAIRM[:, 8 + r:9 + r], in1=SEL2[:, :],
                                   op0=OP.mult, op1=OP.add)
        e.tensor_tensor(out=TMP16[:, :], in0=SEL2[:, 16:32], in1=SEL1[:, 0:16],
                        op=OP.mult)
        e.tensor_tensor(out=HL[:, :], in0=SEL2[:, 0:16], in1=TMP16[:, :],
                        op=OP.add)
        return e.tensor_tensor(out=JUNKV[:, 0:16], in0=HL[:, :],
                               in1=HL[:, :], op=OP.add)
    em("vector", [("SGP", at("SGP", "gath"))], v_hlast, tinc=("SVEC", 1))
    inc("SVEC", 1, "vhl")

    sag_full = cnt["SAG"]
    st1.close()

    cur[0] = "ph2"

    WDT_SB = sb(st, "wdt", [128, 16 * D], BF16)
    WRT = sb(st, "wrt", [128, 4 * 512], BF16)
    WIT = sb(st, "wit", [128, 4 * 512], BF16)
    WINEG = sb(st, "wineg", [128, 4 * 512], BF16)
    UCH = [sb(st, "uch0", [128, TILE], BF16), sb(st, "uch1", [128, TILE], BF16)]
    ACTT = [sb(st, "actt0", [128, TILE], BF16), sb(st, "actt1", [128, TILE], BF16)]
    T2B = sb(st, "t2b", [128, TILE], BF16)
    YT = [sb(st, "yt0", [128, NPT * TILE], BF16),
          sb(st, "yt1", [128, NPT * TILE], BF16)]
    XBC = [sb(st, "xbc0", [128, D], F32), sb(st, "xbc1", [128, D], F32)]
    R1 = sb(st, "r1", [128, NCH * D], F32)
    XN2B = sb(st, "xn2b", [128, NCH * D], BF16)
    XN2T = [sb(st, "xn2t0", [128, NDC * TILE], BF16),
            sb(st, "xn2t1", [128, NDC * TILE], BF16)]
    OUTC = [sb(st, "outc0", [128, D], F32), sb(st, "outc1", [128, D], F32)]

    P2_Z = [psm(st, "pz0", [128, TILE]), psm(st, "pz1", [128, TILE])]
    P2_AT = [psm(st, "pat0", [128, D]), psm(st, "pat1", [128, D])]
    P2_FR = psm(st, "pfr", [128, 512])
    P2_FI = psm(st, "pfi", [128, 512])

    for dc in range(NDC):
        dma("gpsimd", [], WUT_SB[:, dc * INNER:(dc + 1) * INNER],
            WUT[dc * 128:(dc + 1) * 128, INNER:2 * INNER], "SW")
    inc("SW", 0, "wz")
    for ec in range(16):
        dma("gpsimd", [], WDT_SB[:, ec * D:(ec + 1) * D],
            WDT[ec * 128:(ec + 1) * 128, :], "SW")
    inc("SW", 0, "wdt")
    for w_sb, od in ((WRT, ODR), (WIT, ODI)):
        od4 = od.rearrange("(d b) (g e) -> d b g e", b=32, e=16)
        for d in range(32):
            pp, r0 = d // 8, (d % 8) * 16
            dma("gpsimd", [("SMI", at("SMI", ("wbod", wb_last)))],
                w_sb[r0:r0 + 16, pp * 512:(pp + 1) * 512]
                .rearrange("g (b e) -> g b e", b=32),
                od4[d].rearrange("b g e -> g b e"), "SW")
    inc("SW", 0, "wffn")

    def v_wineg(e):
        return e.tensor_scalar(out=WINEG[:, :], in0=WIT[:, :], scalar1=-1.0,
                               scalar2=None, op0=OP.mult)
    em("vector", [("SW", at("SW", "wffn"))], v_wineg, tinc=("SVEC", 1))
    inc("SVEC", 1, "vwineg")

    xn2_4 = XN2.rearrange("(t c p) d -> t c p d", c=NCH, p=128)

    def emit_ffn_block(t):
        bi = t % 2
        def v_settle3(e, bi=bi):
            r = None
            for dc in range(NDC):
                r = e.tensor_copy(JUNKA[:, 2:3],
                                  XN2T[bi][:, (dc + 1) * TILE - 1:
                                           (dc + 1) * TILE])
            return r
        em("vector", [("SXN2T", at("SXN2T", ("xn2t", t, NDC - 1)))], v_settle3,
           tinc=("SVEC", 1))
        inc("SVEC", 1, ("vxs3", t))

        for c in range(NCH):
            def pe_ffn(e, c=c, bi=bi):
                for dc in range(4):
                    e.matmul(out=P2_FR[:, :],
                             lhsT=XN2T[bi][:, dc * TILE + c * 128:
                                           dc * TILE + (c + 1) * 128],
                             rhs=WRT[:, dc * 512:(dc + 1) * 512],
                             start=(dc == 0), stop=False)
                for dc in range(4):
                    e.matmul(out=P2_FR[:, :],
                             lhsT=XN2T[bi][:, (4 + dc) * TILE + c * 128:
                                           (4 + dc) * TILE + (c + 1) * 128],
                             rhs=WINEG[:, dc * 512:(dc + 1) * 512],
                             start=False, stop=False)
                e.matmul(out=P2_FR[:, :], lhsT=ONESB[0:1, :], rhs=BRB[0:1, :],
                         start=False, stop=True)
                for dc in range(4):
                    e.matmul(out=P2_FI[:, :],
                             lhsT=XN2T[bi][:, dc * TILE + c * 128:
                                           dc * TILE + (c + 1) * 128],
                             rhs=WIT[:, dc * 512:(dc + 1) * 512],
                             start=(dc == 0), stop=False)
                for dc in range(4):
                    e.matmul(out=P2_FI[:, :],
                             lhsT=XN2T[bi][:, (4 + dc) * TILE + c * 128:
                                           (4 + dc) * TILE + (c + 1) * 128],
                             rhs=WRT[:, dc * 512:(dc + 1) * 512],
                             start=False, stop=False)
                return e.matmul(out=P2_FI[:, :], lhsT=ONESB[0:1, :],
                                rhs=BIB[0:1, :], start=False, stop=True)
            w = [("SVEC", at("SVEC", ("vxs3", t))),
                 ("SW", at("SW", "wffn")), ("SVEC", at("SVEC", "vwineg"))]
            if (t, c) >= (0, 1):
                pe_, pc_ = (t, c - 1) if c >= 1 else (t - 1, NCH - 1)
                w.append(("SVEC", at("SVEC", ("vfin", pe_, pc_))))
            em("tensor", w, pe_ffn, tinc=("SPE", 1))
            inc("SPE", 1, ("ffn", t, c))

            def v_fin(e, c=c, bi=bi):
                e.tensor_tensor(out=OUTC[c % 2][:, 0:512],
                                in0=R1[:, c * D: c * D + 512],
                                in1=P2_FR[:, :], op=OP.add)
                e.tensor_tensor(out=OUTC[c % 2][:, 512:1024],
                                in0=R1[:, c * D + 512: c * D + 1024],
                                in1=P2_FI[:, :], op=OP.add)
                return e.tensor_tensor(out=JUNKV[:, 5:6],
                                       in0=OUTC[c % 2][:, 1023:1024],
                                       in1=OUTC[c % 2][:, 1023:1024],
                                       op=OP.add)
            w = [("SPE", at("SPE", ("ffn", t, c)))]
            if (t, c) >= (0, 2):
                pe_, pc_ = (t, c - 2) if c >= 2 else (t - 1, NCH - 2 + c)
                w.append(("SOUT", at("SOUT", ("out", pe_, pc_))))
            em("vector", w, v_fin, tinc=("SVEC", 1))
            inc("SVEC", 1, ("vfin", t, c))

            dma("scalar", [("SVEC", at("SVEC", ("vfin", t, c)))],
                OUT[(t * NCH + c) * 128:(t * NCH + c + 1) * 128, :],
                OUTC[c % 2][:, :], "SOUT", ("out", t, c))



    def emit_zxnt(t):
        bi = t % 2
        for dc in range(NDC):
            w = []
            if t >= 2:
                w.append(("SPE", at("SPE", ("z", t - 2, NPT - 1))))
            dma("sync", w, XNT[bi][:, dc * TILE:(dc + 1) * TILE],
                XN[t * TILE:(t + 1) * TILE, dc * 128:(dc + 1) * 128],
                "SXNT", ("xnt2", t, dc), transpose=True)

        def v_settle2(e, bi=bi):
            r = None
            for dc in range(NDC):
                r = e.tensor_copy(JUNKA[:, 2:3],
                                  XNT[bi][:, (dc + 1) * TILE - 1:
                                          (dc + 1) * TILE])
            return r
        em("vector", [("SXNT", at("SXNT", ("xnt2", t, NDC - 1)))], v_settle2,
           tinc=("SVEC", 1))
        inc("SVEC", 1, ("vxs2", t))

    def emit_actld(t, pt):
        w = []
        if t == 0 and pt < 2:
            w.append(("SAG", sag_full))
        if (t, pt) >= (0, 2):
            pe_, pt_ = (t, pt - 2) if pt >= 2 else (t - 1, NPT - 2 + pt)
            w.append(("SACT", at("SACT", ("at2", pe_, pt_))))
        dma("scalar", w, ACTT[pt % 2][:, :],
            AG[pt, :, t * TILE:(t + 1) * TILE], "SACTLD", ("actld", t, pt))

    def emit_zslot(t, pt):
        bi = t % 2
        def pe_z(e, pt=pt, bi=bi):
            r = None
            for dc in range(NDC):
                r = e.matmul(out=P2_Z[pt % 2][:, :],
                             lhsT=WUT_SB[:, dc * INNER + pt * 128:
                                         dc * INNER + (pt + 1) * 128],
                             rhs=XNT[bi][:, dc * TILE:(dc + 1) * TILE],
                             start=(dc == 0), stop=(dc == NDC - 1))
            return r
        w = [("SVEC", at("SVEC", ("vxs2", t))), ("SW", at("SW", "wz"))]
        if (t, pt) >= (0, 2):
            pe_, pt_ = (t, pt - 2) if pt >= 2 else (t - 1, NPT - 2 + pt)
            w.append(("SACT", at("SACT", ("au", pe_, pt_))))
        em("tensor", w, pe_z, tinc=("SPE", 1))
        inc("SPE", 1, ("z", t, pt))

        w = [("SPE", at("SPE", ("z", t, pt)))]
        if (t, pt) >= (0, 2):
            pe_, pt_ = (t, pt - 2) if pt >= 2 else (t - 1, NPT - 2 + pt)
            w.append(("SVEC", at("SVEC", ("vy", pe_, pt_))))
        em("scalar", w,
           (lambda e, pt=pt: (e.activation(out=UCH[pt % 2][:, :],
                                           in_=P2_Z[pt % 2][:, :],
                                           func=AF.Silu,
                                           bias=BUPZ[:, pt:pt + 1]),
                              e.activation(out=JUNKA[:, 0:1],
                                           in_=UCH[pt % 2][:, 511:512],
                                           func=AF.Copy))[1]),
           tinc=("SACT", 1))
        inc("SACT", 1, ("au", t, pt))

        w = [("SACTLD", at("SACTLD", ("actld", t, pt))),
             ("SVEC", at("SVEC", "vhl"))]
        if (t, pt) >= (0, 1):
            pe_, pt_ = (t, pt - 1) if pt >= 1 else (t - 1, NPT - 1)
            w.append(("SVEC", at("SVEC", ("vy", pe_, pt_))))
        em("scalar", w,
           (lambda e, pt=pt: (e.activation(out=T2B[:, :],
                                           in_=ACTT[pt % 2][:, :],
                                           func=AF.Identity,
                                           scale=SKIPS[:, pt:pt + 1],
                                           bias=HL[:, pt:pt + 1]),
                              e.activation(out=JUNKA[:, 0:1],
                                           in_=T2B[:, 511:512],
                                           func=AF.Copy))[1]),
           tinc=("SACT", 1))
        inc("SACT", 1, ("at2", t, pt))
        if pt + 2 < NPT:
            emit_actld(t, pt + 2)

        def v_y(e, pt=pt, bi=bi):
            e.tensor_tensor(out=YT[bi][:, pt * TILE:(pt + 1) * TILE],
                            in0=T2B[:, :], in1=UCH[pt % 2][:, :],
                            op=OP.mult)
            return e.tensor_tensor(out=JUNKV[:, 2:3],
                                   in0=YT[bi][:, (pt + 1) * TILE - 1:
                                              (pt + 1) * TILE],
                                   in1=YT[bi][:, (pt + 1) * TILE - 1:
                                              (pt + 1) * TILE], op=OP.add)
        w = [("SACT", at("SACT", ("at2", t, pt))),
             ("SACT", at("SACT", ("au", t, pt)))]
        if t >= 2:
            w.append(("SPE", at("SPE", ("dn", t - 2, NCH - 1))))
        em("vector", w, v_y, tinc=("SVEC", 1))
        inc("SVEC", 1, ("vy", t, pt))

    def emit_x2(t, c):
        w = []
        if (t, c) >= (0, 2):
            pe_, pc_ = (t, c - 2) if c >= 2 else (t - 1, NCH - 2 + c)
            w.append(("SVEC", at("SVEC", ("vr1", pe_, pc_))))
        dma("sync", w, XBC[c % 2][:, :], xs4[t, c], "SX", ("x2", t, c))

    def emit_dnslot(t, c):
        bi = t % 2
        def pe_dn(e, c=c, bi=bi):
            r = None
            for ec in range(16):
                for n in range(2):
                    r = e.matmul(out=P2_AT[c % 2][:, n * 512:(n + 1) * 512],
                                 lhsT=YT[bi][:, ec * TILE + c * 128:
                                             ec * TILE + (c + 1) * 128],
                                 rhs=WDT_SB[:, ec * D + n * 512:
                                            ec * D + (n + 1) * 512],
                                 start=(ec == 0), stop=False)
            for n in range(2):
                r = e.matmul(out=P2_AT[c % 2][:, n * 512:(n + 1) * 512],
                             lhsT=ONESB[0:1, :],
                             rhs=BDOWNB[0:1, n * 512:(n + 1) * 512],
                             start=False, stop=True)
            return r
        w = [("SVEC", at("SVEC", ("vy", t, NPT - 1))),
             ("SW", at("SW", "wdt"))]
        if (t, c) >= (0, 2):
            pe_, pc_ = (t, c - 2) if c >= 2 else (t - 1, NCH - 2 + c)
            w.append(("SVEC", at("SVEC", ("vr1", pe_, pc_))))
        em("tensor", w, pe_dn, tinc=("SPE", 1))
        inc("SPE", 1, ("dn", t, c))

        def v_r1(e, c=c):
            return e.tensor_tensor(out=R1[:, c * D:(c + 1) * D],
                                   in0=XBC[c % 2][:, :],
                                   in1=P2_AT[c % 2][:, :], op=OP.add)
        em("vector", [("SPE", at("SPE", ("dn", t, c))),
                      ("SX", at("SX", ("x2", t, c)))], v_r1,
           tinc=("SVEC", 1))
        inc("SVEC", 1, ("vr1", t, c))
        if c + 2 < NCH:
            emit_x2(t, c + 2)

        def v_st2(e, c=c):
            return stats_ops(e, R1[:, c * D:(c + 1) * D], 128)
        w = [("SVEC", at("SVEC", ("vr1", t, c)))]
        if (t, c) != (0, 0):
            pc = (t, c - 1) if c > 0 else (t - 1, NCH - 1)
            w.append(("SACT", at("SACT", ("aln2", pc[0], pc[1]))))
        em("vector", w, v_st2, tinc=("SVEC", 1))
        inc("SVEC", 1, ("vst2", t, c))

        em("scalar", [("SVEC", at("SVEC", ("vst2", t, c)))],
           (lambda e: (e.activation(out=SD1[:, 0:1], in_=VARB[:, 0:1],
                                    func=AF.Ln, bias=CONS[:, 0:1]),
                       e.activation(out=RSTD[:, 0:1], in_=SD1[:, 0:1],
                                    func=AF.Exp, scale=-0.5),
                       e.activation(out=JUNKA[:, 0:1], in_=RSTD[:, 0:1],
                                    func=AF.Copy))[2]),
           tinc=("SACT", 1))
        inc("SACT", 1, ("asq2", t, c))

        def v_rs2(e):
            e.scalar_tensor_tensor(out=NMR[:, 0:1], in0=MEANB[:, 0:1],
                                   scalar=-1.0, in1=RSTD[:, 0:1],
                                   op0=OP.mult, op1=OP.mult)
            return e.tensor_tensor(out=JUNKV[:, 1:2], in0=RSTD[:, 0:1],
                                   in1=NMR[:, 0:1], op=OP.add)
        em("vector", [("SACT", at("SACT", ("asq2", t, c)))], v_rs2,
           tinc=("SVEC", 1))
        inc("SVEC", 1, ("vrs2", t, c))

        w = [("SVEC", at("SVEC", ("vrs2", t, c)))]
        if c == 0 and t > 0:
            w.append(("SXN2W", at("SXN2W", ("xn2w", t - 1))))
        em("scalar", w,
           (lambda e, c=c: (e.activation(out=XN2B[:, c * D:(c + 1) * D],
                                         in_=R1[:, c * D:(c + 1) * D],
                                         func=AF.Identity,
                                         scale=RSTD[:, 0:1],
                                         bias=NMR[:, 0:1]),
                            e.activation(out=JUNKA[:, 0:1],
                                         in_=XN2B[:, (c + 1) * D - 1:
                                                 (c + 1) * D],
                                         func=AF.Copy))[1]),
           tinc=("SACT", 1))
        inc("SACT", 1, ("aln2", t, c))

    def emit_xn2w(t):
        dma("scalar", [("SACT", at("SACT", ("aln2", t, NCH - 1)))],
            xn2_4[t].rearrange("c p d -> p c d"),
            XN2B[:, :].rearrange("p (c d) -> p c d", c=NCH),
            "SXN2W", ("xn2w", t))

    def emit_xn2t(t):
        bi = t % 2
        for dc in range(NDC):
            w = [("SXN2W", at("SXN2W", ("xn2w", t)))]
            if t >= 2:
                w.append(("SPE", at("SPE", ("ffn", t - 2, NCH - 1))))
            dma("sync", w, XN2T[bi][:, dc * TILE:(dc + 1) * TILE],
                XN2[t * TILE:(t + 1) * TILE, dc * 128:(dc + 1) * 128],
                "SXN2T", ("xn2t", t, dc), transpose=True)

    for t in range(nt):
        emit_zxnt(t)
        emit_actld(t, 0)
        emit_actld(t, 1)
        for pt in range(NPT):
            emit_zslot(t, pt)
            if t >= 1:
                if pt < NCH:
                    emit_dnslot(t - 1, pt)
                elif pt == 4:
                    emit_xn2w(t - 1)
                elif pt == 5:
                    emit_xn2t(t - 1)
                elif pt == 6:
                    emit_ffn_block(t - 1)
        emit_x2(t, 0)
        emit_x2(t, 1)

    for c in range(NCH):
        emit_dnslot(nt - 1, c)
    emit_xn2w(nt - 1)
    emit_xn2t(nt - 1)
    emit_ffn_block(nt - 1)

    if dbg:
        DXN = nc.dram_tensor("DXN", [t_tok, D], BF16, kind="ExternalOutput").ap()
        DAG = nc.dram_tensor("DAG", [NPT, 128, t_tok], BF16,
                             kind="ExternalOutput").ap()
        DXN2 = nc.dram_tensor("DXN2", [t_tok, D], BF16,
                              kind="ExternalOutput").ap()
        DCCB = nc.dram_tensor("DCCB", [128, 32], F32, kind="ExternalOutput").ap()
        DGATH = nc.dram_tensor("DGATH", [128, 256], F32,
                               kind="ExternalOutput").ap()
        DHL = nc.dram_tensor("DHL", [128, NPT], F32, kind="ExternalOutput").ap()
        DWRT = nc.dram_tensor("DWRT", [128, 4 * 512], BF16,
                              kind="ExternalOutput").ap()
        fo = cnt["SOUT"]
        dma("gpsimd", [("SOUT", fo)], DXN, XN, "SGP")
        dma("gpsimd", [], DAG, AG, "SGP")
        dma("gpsimd", [], DXN2, XN2, "SGP")
        dma("gpsimd", [], DCCB, CCB[:, :], "SGP")
        dma("gpsimd", [], DGATH, GATH[:, :], "SGP")
        dma("gpsimd", [], DHL, HL[:, :], "SGP")
        dma("gpsimd", [], DWRT, WRT[:, :], "SGP")
        em("gpsimd", [("SGP", cnt["SGP"])], lambda e: e.nop())

    final_out = cnt["SOUT"]
    em("scalar", [("SOUT", final_out)], lambda e: e.nop())

    for name in ("ph1", "ph2"):
        pl = plans[name]
        with nc.Block(name) as blk:
            @blk.sync
            def _(e, pl=pl):
                for eng, fn in pl:
                    if eng == "sync":
                        fn(e)

            @blk.gpsimd
            def _(e, pl=pl):
                for eng, fn in pl:
                    if eng == "gpsimd":
                        fn(e)

            @blk.tensor
            def _(e, pl=pl):
                for eng, fn in pl:
                    if eng == "tensor":
                        fn(e)

            @blk.scalar
            def _(e, pl=pl):
                for eng, fn in pl:
                    if eng == "scalar":
                        fn(e)

            @blk.vector
            def _(e, pl=pl):
                for eng, fn in pl:
                    if eng == "vector":
                        fn(e)

    st.close()
    return nc


def _host_inputs(inputs, core):
    b, h = core // 2, core % 2
    x = inputs["L0_attn_x"]
    w_up = inputs["L0_attn_w_up"]
    conv_w = inputs["L0_attn_conv_w"][:, 0, :]
    start = h * T
    xs = np.ascontiguousarray(x[b, start:start + T])
    if h == 0:
        xh = np.zeros((3, D), np.float32)
        halom = np.zeros((128, 1), np.float32)
    else:
        xh = np.ascontiguousarray(x[b, start - 3:start])
        halom = np.ones((128, 1), np.float32)
    pairm = np.zeros((128, 16), np.float32)
    pairm[:, 2 * b] = 1.0
    pairm[:, 8 + 2 * b + 1] = 1.0

    def pcol(v):
        return np.ascontiguousarray(v.reshape(NPT, 128).T)

    bd = {}
    for nm, w in (("BDF", inputs["L0_attn_w_f"]), ("BDI", inputs["L0_attn_w_i"]),
                  ("BDH", inputs["L0_attn_w_h"])):
        m = np.zeros((NPT, 128, 128), np.float32)
        for g in range(128):
            tt, j = g // 8, g % 8
            m[tt, 16 * j:16 * (j + 1), 16 * j:16 * (j + 1)] = w[g].T
        bd[nm] = np.ascontiguousarray(m.transpose(1, 0, 2).reshape(128, NPT * 128))

    g1r = inputs["L0_ffn_G1_R"][0]
    g2r = inputs["L0_ffn_G2_R"][:, :, 0, :]
    g1i = inputs["L0_ffn_G1_I"][0]
    g2i = inputs["L0_ffn_G2_I"][:, :, 0, :]

    def g1pack(g1):
        return np.ascontiguousarray(g1.transpose(1, 2, 0).reshape(1, 4096))

    def g2pack(g2):
        return np.ascontiguousarray(g2.transpose(0, 2, 1).reshape(1, 1024))

    cwm = np.ascontiguousarray(
        conv_w.reshape(NPT, 128, 4).transpose(1, 0, 2).reshape(128, 4 * NPT))

    return {
        "XS": xs, "XH": xh,
        "WUT": np.ascontiguousarray(w_up.T),
        "WDT": np.ascontiguousarray(inputs["L0_attn_w_down"].T),
        "BDF": bd["BDF"], "BDI": bd["BDI"], "BDH": bd["BDH"],
        "CWD": cwm,
        "BUPM": pcol(inputs["L0_attn_b_up"][:INNER]),
        "BUPZ": pcol(inputs["L0_attn_b_up"][INNER:]),
        "CBD": pcol(inputs["L0_attn_conv_b"]),
        "SKIPD": pcol(inputs["L0_attn_skip"]),
        "HALOM": halom, "PAIRM": pairm,
        "EYE": np.eye(128, dtype=np.float32),
        "BDOWND": np.ascontiguousarray(inputs["L0_attn_b_down"].reshape(1, D)),
        "BRD": np.ascontiguousarray(inputs["L0_ffn_bias_real"].reshape(1, 512)),
        "BID": np.ascontiguousarray(inputs["L0_ffn_bias_imag"].reshape(1, 512)),
        "G1R": g1pack(g1r), "G2R": g2pack(g2r),
        "G1I": g1pack(g1i), "G2I": g2pack(g2i),
        "CONS": np.tile(np.array([[LNEPS, 1.0, T * EPS, 2.0]], np.float32),
                        (128, 1)),
    }


def kernel(**inputs):
    inputs = {k: np.ascontiguousarray(np.asarray(v)) for k, v in inputs.items()}
    nc = build_nc()
    in_maps = [_host_inputs(inputs, k) for k in range(8)]
    res = run_bass_kernel_spmd(nc, in_maps, core_ids=list(range(8)))
    out = np.empty((B, S, D), np.float32)
    for k in range(8):
        b, h = k // 2, k % 2
        out[b, h * T:(h + 1) * T] = res.results[k]["OUT"]
    return out


